# version 17
# baseline (speedup 1.0000x reference)
import os
import numpy as np
import ml_dtypes
import concourse.bass as bass
import concourse.mybir as mybir
from concourse.bass_utils import run_bass_kernel_spmd
from contextlib import ExitStack

F32 = mybir.dt.float32
BF16 = mybir.dt.bfloat16
AF = mybir.ActivationFunctionType
ALU = mybir.AluOpType
AX = mybir.AxisListType

T = 2048
C = 4096
NCH = 32
SCALE = 1.0 / float(np.sqrt(np.float32(32)))
EXP_SHIFT = -12.0
EPS = 1e-5

DEBUG = int(os.environ.get("MK_DEBUG", "0"))
PHASES = os.environ.get("MK_PHASES", "ABCD")


class Em:

    ENGS = ("T", "V", "S", "P", "Y")

    def __init__(self, nc, stack):
        self.nc = nc
        self.ops = {e: [] for e in self.ENGS}
        self.sems = {}
        self.cnt = {}
        self.stack = stack

    def sem(self, name):
        if name not in self.sems:
            self.sems[name] = self.stack.enter_context(self.nc.semaphore(name))
            self.cnt[name] = 0
        return self.sems[name]

    def c(self, name):
        self.sem(name)
        return self.cnt[name]

    def emit(self, eng, fn, waits=(), inc=None):
        wl = [(self.sem(n), v) for (n, v) in waits]
        il = None
        if inc is not None:
            n, d = inc
            self.sem(n)
            self.cnt[n] += d
            il = (self.sems[n], d)
        self.ops[eng].append((wl, fn, il))

    def barrier(self, name):
        self.sem(name)
        tgt = self.cnt[name] + 5
        for e in self.ENGS:
            self.emit(e, lambda E: E.nop(), inc=(name, 1))
        for e in self.ENGS:
            self.ops[e].append(([(self.sems[name], tgt)], None, None))

    def play(self, eng, E):
        for wl, fn, il in self.ops[eng]:
            for (h, v) in wl:
                E.wait_ge(h, v)
            if fn is not None:
                ins = fn(E)
                if il is not None:
                    ins.then_inc(*il)


def build_program():
    nc = bass.Bass("TRN2", target_bir_lowering=False)

    xT = nc.dram_tensor("xT", [C, T], F32, kind="ExternalInput")
    xsel = nc.dram_tensor("xsel", [T, 2048], F32, kind="ExternalInput")
    wqT = nc.dram_tensor("wqT", [C, 2048], F32, kind="ExternalInput")
    wkT = nc.dram_tensor("wkT", [C, 1024], F32, kind="ExternalInput")
    wvT = nc.dram_tensor("wvT", [C, 1024], F32, kind="ExternalInput")
    bqT = nc.dram_tensor("bqT", [128, 16], F32, kind="ExternalInput")
    bkT = nc.dram_tensor("bkT", [128, 8], F32, kind="ExternalInput")
    bvr = nc.dram_tensor("bvr", [1, 1024], F32, kind="ExternalInput")
    bwT = nc.dram_tensor("bwT", [C, 2048], F32, kind="ExternalInput")
    lAi = nc.dram_tensor("lAi", [128, 4096], F32, kind="ExternalInput")
    lBi = nc.dram_tensor("lBi", [128, 2048], F32, kind="ExternalInput")
    rti = nc.dram_tensor("rti", [128, 256], F32, kind="ExternalInput")
    idbi = nc.dram_tensor("idbi", [128, 128], BF16, kind="ExternalInput")
    maski = nc.dram_tensor("maski", [128, 128], BF16, kind="ExternalInput")
    repli = nc.dram_tensor("repli", [8, 128], BF16, kind="ExternalInput")
    oneci = nc.dram_tensor("oneci", [128, 1], BF16, kind="ExternalInput")
    oneri = nc.dram_tensor("oneri", [1, 128], F32, kind="ExternalInput")
    idfi = nc.dram_tensor("idfi", [128, 128], F32, kind="ExternalInput")
    onerbi = nc.dram_tensor("onerbi", [1, 128], BF16, kind="ExternalInput")

    scratch_kind = "ExternalOutput" if DEBUG else "Internal"
    Qs = nc.dram_tensor("Qs", [16, 128, T], BF16, kind=scratch_kind)
    Ks = nc.dram_tensor("Ks", [8, 128, T], BF16, kind=scratch_kind)
    Vs = nc.dram_tensor("Vs", [T, 1024], BF16, kind=scratch_kind)
    gH = nc.dram_tensor("gH", [T, 2048], F32, kind=scratch_kind)
    out = nc.dram_tensor("out", [T, 2048], F32, kind="ExternalOutput")
    if DEBUG:
        dxn = nc.dram_tensor("dxn", [128, NCH * T], BF16, kind="ExternalOutput")
        dwd = nc.dram_tensor("dwd", [128, 128], BF16, kind="ExternalOutput")
        dmg = nc.dram_tensor("dmg", [128, 2048], BF16, kind="ExternalOutput")

    with ExitStack() as stack:
        en = stack.enter_context
        arena = en(nc.sbuf_tensor("arena", [128, 65536], BF16))
        wstr = en(nc.sbuf_tensor("wstr", [128, 16384], BF16))
        vsb = en(nc.sbuf_tensor("vsb", [128, 16512], BF16))
        bq_sb = en(nc.sbuf_tensor("bq_sb", [128, 16], F32))
        bk_sb = en(nc.sbuf_tensor("bk_sb", [128, 8], F32))
        bv_sb = en(nc.sbuf_tensor("bv_sb", [1, 1024], BF16))
        rt_sb = en(nc.sbuf_tensor("rt_sb", [128, 256], BF16))
        idb = en(nc.sbuf_tensor("idb", [128, 128], BF16))
        idf = en(nc.sbuf_tensor("idf", [128, 128], F32))
        mask_sb = en(nc.sbuf_tensor("mask_sb", [128, 128], BF16))
        repl_sb = en(nc.sbuf_tensor("repl_sb", [8, 128], BF16))
        onec = en(nc.sbuf_tensor("onec", [128, 1], BF16))
        oner = en(nc.sbuf_tensor("oner", [1, 128], F32))
        onerb = en(nc.sbuf_tensor("onerb", [1, 128], BF16))
        konst = en(nc.sbuf_tensor("konst", [128, 8], F32))
        nm1 = en(nc.sbuf_tensor("nm1", [128, 16], F32))
        nm2 = en(nc.sbuf_tensor("nm2", [128, 16], F32))
        trow = en(nc.sbuf_tensor("trow", [128, 16], F32))
        inv2 = en(nc.sbuf_tensor("inv2", [128, 16], F32))
        keep = en(nc.sbuf_tensor("keep", [128, 128], F32))
        lmsk = en(nc.sbuf_tensor("lmsk", [128, 128], F32))
        recip = en(nc.sbuf_tensor("recip", [128, 4], F32))

        arena32 = arena.bitcast(F32)
        wstr32 = wstr.bitcast(F32)
        vsb32 = vsb.bitcast(F32)

        xnT = arena
        s_slab = [arena32[:, 0:8192], arena32[:, 8192:16384]]
        g_slot = [arena32[:, 16384 + 2048 * i: 16384 + 2048 * (i + 1)]
                  for i in range(4)]
        wT = [arena[:, 49152:57344], arena[:, 57344:65536]]
        q_slot = [wstr[:, 4096 + 512 * i: 4096 + 512 * (i + 1)] for i in range(4)]
        k_slot = [wstr[:, 0:2048], wstr[:, 2048:4096]]

        mrow = wstr32[0:1, 4096:6144]
        qrow = wstr32[0:1, 6144:8192]
        wbuf = [wstr[:, 0:4096], wstr[:, 4096:8192]]
        vwbuf = [wstr[:, 0:8192], wstr[:, 8192:16384]]
        bwbuf = [wstr[:, 0:8192], wstr[:, 8192:16384]]

        rrow = vsb32[0:1, 0:2048]
        lgT_sb = vsb32[0:8, 0:2048]
        rb_sb = vsb[:, 4096:6144]
        mb_sb = vsb[:, 8192:10240]
        sqb = [vsb[:, 12288:14336], vsb[:, 14336:16384]]
        lA_sb = vsb[:, 0:4096]
        lB_sb = vsb[:, 4096:6144]
        wdbig = vsb[:, 6144:8192]
        midgT = vsb[:, 8192:10240]
        lgn = vsb32[:, 5120:5248]
        e_sb = vsb32[:, 5248:5376]
        wd_sb = vsb[:, 10752:10880]
        wdT_sb = vsb[0:8, 10880:12928]
        qkstage = [vsb[:, 12928:13440], vsb[:, 13440:13952]]
        vstage = [vsb[:, 13952:14208], vsb[:, 14208:14464]]
        gstage = [vsb32[:, 7232:7488], vsb32[:, 7488:7744]]
        vslots = vsb.rearrange("p (s n) -> p s n", n=129)

        rem = nc.sbuf_bytes_remaining
        assert rem >= 0, f"SBUF over budget by {-rem} bytes"

        psA = en(nc.psum_tensor("psA", [128, 1024], F32))
        psB = en(nc.psum_tensor("psB", [128, 1024], F32))
        pav = [en(nc.psum_tensor("pav0", [128, 512], F32)),
               en(nc.psum_tensor("pav1", [128, 512], F32))]
        pg = [en(nc.psum_tensor("pg0", [128, 512], F32)),
              en(nc.psum_tensor("pg1", [128, 512], F32))]
        pgb = [pg[0].bitcast(BF16), pg[1].bitcast(BF16)]

        em = Em(nc, stack)
        use = {}

        def acquire(name, scale=1):
            u = use.get(name, 0)
            use[name] = u + 1
            return ("f_" + name, scale * u)

        def fcnt(name):
            return ("f_" + name, em.c("f_" + name))

        em.emit("V", lambda E: nc.vector.memset(konst[:, 0:1], EXP_SHIFT), inc=("kon", 1))
        em.emit("V", lambda E: nc.vector.memset(konst[:, 1:2], EPS), inc=("kon", 1))
        for dst, src in [(bq_sb, bqT), (bk_sb, bkT), (idb, idbi), (idf, idfi),
                         (mask_sb, maski),
                         (repl_sb, repli), (onec, oneci), (oner, oneri), (onerb, onerbi)]:
            em.emit("Y", lambda E, d=dst, sr=src: E.dma_start(d[:, :], sr[:, :]),
                    inc=("kload", 16))
        em.emit("P", lambda E: E.dma_start(bv_sb[:, :], bvr[:, :]), inc=("kloadp", 16))
        em.emit("P", lambda E: E.dma_start(rt_sb[:, :], rti[:, :]), inc=("kloadp", 16))
        KL = em.c("kload")
        KLP = em.c("kloadp")

        if "A" in PHASES:
            sum_ps = [psA[0:1, 0:512], psA[0:1, 512:1024],
                      psB[0:1, 0:512], psB[0:1, 512:1024]]
            sq_ps = [pav[0][0:1, 0:512], pav[1][0:1, 0:512],
                     pg[0][0:1, 0:512], pg[1][0:1, 0:512]]
            aqs0 = [acquire(n) for n in
                    ("psAa", "psAb", "psBa", "psBb", "pav0", "pav1", "pg0", "pg1")]
            for c in range(NCH):
                b = c % 2
                sname = "axp%d" % (c % 8)
                w = [] if c < 8 else [(sname, 16 * (c // 8))]
                em.emit("P", lambda E, c=c: E.dma_start(
                    xnT[:, c * T:(c + 1) * T], xT[c * 128:(c + 1) * 128, :]),
                    waits=w, inc=(sname, 16))
                w = [(sname, 16 * (c // 8 + 1))]
                if c >= 2:
                    w.append(("astat", 8 * (c - 1)))
                em.emit("S", lambda E, b=b, c=c: nc.scalar.activation(
                    sqb[b][:, :], xnT[:, c * T:(c + 1) * T], AF.Square),
                    waits=w, inc=("asq", 1))
                w = [(sname, 16 * (c // 8 + 1)), ("asq", c + 1)]
                if c == 0:
                    w += [("kload", KL), ("kloadp", KLP)] + aqs0
                for ts in range(4):
                    em.emit("T", lambda E, c=c, ts=ts: nc.tensor.matmul(
                        sum_ps[ts], onec[:, :],
                        xnT[:, c * T + ts * 512: c * T + ts * 512 + 512],
                        start=(c == 0), stop=(c == NCH - 1)),
                        waits=w if ts == 0 else (), inc=("astat", 1))
                for ts in range(4):
                    em.emit("T", lambda E, b=b, ts=ts, c=c: nc.tensor.matmul(
                        sq_ps[ts], onec[:, :], sqb[b][:, ts * 512:(ts + 1) * 512],
                        start=(c == 0), stop=(c == NCH - 1)),
                        inc=("astat", 1))
            AST = em.c("astat")

            ev_rel = ["psAa", "psAb", "psBa", "psBb"]
            for ts in range(4):
                w = [("astat", AST)] if ts == 0 else ()
                em.emit("V", lambda E, ts=ts: nc.vector.tensor_copy(
                    mrow[:, ts * 512:(ts + 1) * 512], sum_ps[ts]),
                    waits=w, inc=("f_" + ev_rel[ts], 1))
            sq_rel = ["pav0", "pav1", "pg0", "pg1"]
            for ts in range(4):
                em.emit("V", lambda E, ts=ts: nc.vector.tensor_copy(
                    qrow[:, ts * 512:(ts + 1) * 512], sq_ps[ts]),
                    inc=("f_" + sq_rel[ts], 1))
            wrows = [fcnt("psAa"), fcnt("psAb"), fcnt("psBa"), fcnt("psBb"),
                     fcnt("pav0"), fcnt("pav1"), fcnt("pg0"), fcnt("pg1")]
            em.emit("V", lambda E: nc.vector.tensor_scalar(
                mrow[:, :], mrow[:, :], 1.0 / C, None, ALU.mult),
                waits=wrows, inc=("arow", 1))
            em.emit("V", lambda E: nc.vector.tensor_scalar(
                qrow[:, :], qrow[:, :], 1.0 / C, None, ALU.mult),
                waits=[("arow", em.c("arow"))], inc=("arow", 1))
            em.emit("V", lambda E: nc.vector.tensor_mul(rrow[:, :], mrow[:, :], mrow[:, :]),
                    waits=[("arow", em.c("arow"))], inc=("arow", 1))
            em.emit("V", lambda E: nc.vector.tensor_sub(qrow[:, :], qrow[:, :], rrow[:, :]),
                    waits=[("arow", em.c("arow"))], inc=("arow", 1))
            em.emit("S", lambda E: nc.scalar.activation(
                rrow[:, :], qrow[:, :], AF.Ln, bias=konst[0:1, 1:2]),
                waits=[("arow", em.c("arow")), ("kon", 2)], inc=("asd", 1))
            em.emit("S", lambda E: nc.scalar.activation(
                rrow[:, :], rrow[:, :], AF.Exp, scale=-0.5),
                waits=[("asd", 1)], inc=("asd", 1))
            em.emit("V", lambda E: E.nop(),
                    waits=[("asd", 2)], inc=("arow", 1))
            em.emit("V", lambda E: nc.vector.scalar_tensor_tensor(
                mrow[:, :], mrow[:, :], -1.0, rrow[:, :], ALU.mult, ALU.mult),
                waits=[("arow", em.c("arow"))], inc=("arow", 1))
            ARW = em.c("arow")

            for ts in range(4):
                a0 = acquire("pg0")
                a1 = acquire("pg1")
                em.emit("T", lambda E, ts=ts: nc.tensor.matmul(
                    pg[0][:, :], oner[:, :], rrow[:, ts * 512:(ts + 1) * 512],
                    start=True, stop=True),
                    waits=[("arow", ARW), a0, a1], inc=("abc", 1))
                em.emit("T", lambda E, ts=ts: nc.tensor.matmul(
                    pg[1][:, :], oner[:, :], mrow[:, ts * 512:(ts + 1) * 512],
                    start=True, stop=True), inc=("abc", 1))
                em.emit("V", lambda E, ts=ts: nc.vector.tensor_copy(
                    rb_sb[:, ts * 512:(ts + 1) * 512], pg[0][:, :]),
                    waits=[("abc", 2 * (ts + 1))], inc=("f_pg0", 1))
                em.emit("V", lambda E, ts=ts: nc.vector.tensor_copy(
                    mb_sb[:, ts * 512:(ts + 1) * 512], pg[1][:, :]),
                    inc=("f_pg1", 1))
            WBC = [fcnt("pg0"), fcnt("pg1")]

            for c in range(NCH):
                w = [("anrm", em.c("anrm"))]
                if c == 0:
                    w += WBC
                em.emit("V", lambda E, c=c: nc.vector.tensor_mul(
                    xnT[:, c * T:(c + 1) * T], xnT[:, c * T:(c + 1) * T], rb_sb[:, :]),
                    waits=w, inc=("anrm", 1))
                em.emit("V", lambda E, c=c: nc.vector.tensor_add(
                    xnT[:, c * T:(c + 1) * T], xnT[:, c * T:(c + 1) * T], mb_sb[:, :]),
                    waits=[("anrm", em.c("anrm"))], inc=("anrm", 1))
            if DEBUG:
                em.emit("Y", lambda E: E.dma_start(dxn[:, :], xnT[:, :]),
                        waits=[("anrm", em.c("anrm"))], inc=("dbg", 16))

        if "B" in PHASES:
            def wsrc(dram, blk, width):
                r = dram.rearrange("(c p) n -> p c n", p=128)
                return r[:, :, blk * width:(blk + 1) * width]

            for i in range(4):
                em.emit("Y", lambda E, i=i: E.dma_start(
                    out[i * 512:(i + 1) * 512, :], xsel[i * 512:(i + 1) * 512, :]),
                    inc=("dpre", 16))
            bcb = [(pg[0], "pg0"), (pg[1], "pg1"),
                   (psA[:, 0:512], "psAa"), (psA[:, 512:1024], "psAb"),
                   (psB[:, 0:512], "psBa"), (psB[:, 512:1024], "psBb")]
            ANRM = em.c("anrm")
            wld = {0: 0, 1: 0}
            for og in range(24):
                sl = og % 2
                isq = og < 16
                dram = wqT if isq else wkT
                ogl = og if isq else og - 16
                w = [] if og < 2 else [("bwu%d" % sl, og // 2)]
                wld[sl] += 1
                em.emit("P", lambda E, d=dram, ogl=ogl, sl=sl: E.dma_start(
                    wbuf[sl].rearrange("p (c n) -> p c n", n=128), wsrc(d, ogl, 128)),
                    waits=w, inc=("bw%d" % sl, 16))
                for ts in range(4):
                    gi = og * 4 + ts
                    pp, pname = bcb[gi % 6]
                    aq = acquire(pname)
                    w = [("bw%d" % sl, 16 * wld[sl]), aq]
                    for ch in range(NCH):
                        last = (ch == NCH - 1)
                        grp_inc = (("bwu%d" % sl, 1) if ts == 3 else ("bmm", 1)) if last else None
                        wch = w if ch == 0 else []
                        if og == 0 and ts == 0:
                            wch = list(wch) + [("anrm", 2 * (ch + 1))]
                        em.emit("T", lambda E, pp=pp, sl=sl, ch=ch, ts=ts: nc.tensor.matmul(
                            pp[:, :], wbuf[sl][:, ch * 128:(ch + 1) * 128],
                            xnT[:, ch * T + ts * 512: ch * T + ts * 512 + 512],
                            start=(ch == 0), stop=(ch == NCH - 1)),
                            waits=wch,
                            inc=grp_inc)
                    ssl = gi % 2
                    aqs = acquire("qkst%d" % ssl, 16)
                    bias = bq_sb[:, og:og + 1] if isq else bk_sb[:, ogl:ogl + 1]
                    gw = ("bwu%d" % sl, em.c("bwu%d" % sl)) if ts == 3 else ("bmm", em.c("bmm"))
                    em.emit("S", lambda E, ssl=ssl, pp=pp, bias=bias: nc.scalar.activation(
                        qkstage[ssl][:, :], pp[:, :], AF.Identity, bias=bias),
                        waits=[gw, aqs], inc=("f_" + pname, 1))
                    dst = (Qs[og, :, ts * 512:(ts + 1) * 512] if isq
                           else Ks[ogl, :, ts * 512:(ts + 1) * 512])
                    em.emit("Y", lambda E, dst=dst, ssl=ssl: E.dma_start(
                        dst, qkstage[ssl][:, :]),
                        waits=[fcnt(pname)], inc=("f_qkst%d" % ssl, 16))
            for ts in range(4):
                psl = ts % 2
                pname = "pav%d" % psl
                aq = acquire(pname)
                for ch in range(NCH):
                    em.emit("T", lambda E, psl=psl, ch=ch, ts=ts: nc.tensor.matmul(
                        pav[psl][0:8, :], rt_sb[:, ch * 8:(ch + 1) * 8],
                        xnT[:, ch * T + ts * 512: ch * T + ts * 512 + 512],
                        start=(ch == 0), stop=(ch == NCH - 1)),
                        waits=[aq] if ch == 0 else (),
                        inc=("clg", 1) if ch == NCH - 1 else None)
                em.emit("V", lambda E, psl=psl, ts=ts: nc.vector.tensor_copy(
                    lgT_sb[:, ts * 512:(ts + 1) * 512], pav[psl][0:8, :]),
                    waits=[("clg", em.c("clg"))], inc=("f_" + pname, 1))
            LGT = [("f_pav0", em.c("f_pav0")), ("f_pav1", em.c("f_pav1"))]
            BMM_QK = em.c("bmm")

            def piece_rtrans():
                aq_tr = acquire("pg0")
                for tc in range(16):
                    w = LGT + [aq_tr] if tc == 0 else ()
                    em.emit("T", lambda E, tc=tc: nc.tensor.transpose(
                        pg[0][:, tc * 8:(tc + 1) * 8],
                        lgT_sb[:, tc * 128:(tc + 1) * 128], idf[0:8, 0:8]),
                        waits=w, inc=("ctg", 1))
                em.emit("V", lambda E: nc.vector.tensor_copy(
                    lgn[:, :], pg[0][:, 0:128]),
                    waits=[("ctg", 16)], inc=("f_pg0", 1))
                def b8(ap2d, stride):
                    return bass.AP(ap2d.tensor, ap2d.offset,
                                   [[stride, 128], [1, 16], [0, 8]])
                lgn3 = lgn.rearrange("p (g n) -> p g n", n=8)
                e3 = e_sb.rearrange("p (g n) -> p g n", n=8)
                keep3 = keep.rearrange("p (g n) -> p g n", n=8)
                wd3 = wd_sb.rearrange("p (g n) -> p g n", n=8)
                lmsk3 = lmsk.rearrange("p (g n) -> p g n", n=8)
                nm1b = b8(nm1[:, :], 16)
                trowb = b8(trow[:, :], 16)
                inv2b = b8(inv2[:, :], 16)
                em.emit("V", lambda E: nc.vector.reduce_max(
                    nm1[:, :], lgn3, AX.X, negate=True),
                    waits=[("f_pg0", em.c("f_pg0"))], inc=("ct", 1))
                em.emit("V", lambda E: nc.vector.tensor_tensor(
                    lgn3, lgn3, nm1b, ALU.add),
                    waits=[("ct", em.c("ct"))], inc=("ct", 1))
                em.emit("V", lambda E: nc.vector.tensor_scalar(
                    keep[:, :], lgn[:, :], 0.0, None, ALU.is_ge),
                    waits=[("ct", em.c("ct"))], inc=("ct", 1))
                em.emit("V", lambda E: nc.vector.scalar_tensor_tensor(
                    lmsk[:, :], keep[:, :], -1e30, lgn[:, :], ALU.mult, ALU.add),
                    waits=[("ct", em.c("ct"))], inc=("ct", 1))
                em.emit("V", lambda E: nc.vector.reduce_max(
                    nm2[:, :], lmsk3, AX.X, negate=True),
                    waits=[("ct", em.c("ct"))], inc=("ct", 1))
                em.emit("S", lambda E: nc.scalar.activation(
                    e_sb[:, :], lgn[:, :], AF.Exp),
                    waits=[("ct", em.c("ct"))], inc=("ce", 1))
                em.emit("S", lambda E: nc.scalar.activation(
                    trow[:, :], nm2[:, :], AF.Exp, scale=-1.0),
                    inc=("ce", 1))
                em.emit("V", lambda E: nc.vector.tensor_scalar(
                    inv2[:, :], trow[:, :], 0.5, 0.5, ALU.mult, ALU.add),
                    waits=[("ce", em.c("ce"))], inc=("ct", 1))
                em.emit("V", lambda E: nc.vector.reciprocal(
                    inv2[:, :], inv2[:, :]),
                    waits=[("ct", em.c("ct"))], inc=("ct", 1))
                em.emit("V", lambda E: nc.vector.tensor_tensor(
                    keep3, e3, trowb, ALU.is_ge),
                    waits=[("ct", em.c("ct"))], inc=("ct", 1))
                em.emit("V", lambda E: nc.vector.tensor_mul(
                    keep[:, :], keep[:, :], e_sb[:, :]),
                    waits=[("ct", em.c("ct"))], inc=("ct", 1))
                em.emit("V", lambda E: nc.vector.tensor_tensor(
                    wd3, keep3, inv2b, ALU.mult),
                    waits=[("ct", em.c("ct"))], inc=("cw", 16))

            def piece_wdtrans():
                for half in range(2):
                    pname = "pg%d" % half
                    aq = acquire(pname)
                    for i in range(8):
                        tc = half * 8 + i
                        w = [("cw", 16), aq] if i == 0 else ()
                        em.emit("T", lambda E, half=half, i=i, tc=tc: nc.tensor.transpose(
                            pgb[half][0:8, i * 128:(i + 1) * 128],
                            wd_sb[:, tc * 8:(tc + 1) * 8], idb[:, :]),
                            waits=w, inc=("ctr", 1))
                    em.emit("V", lambda E, half=half: nc.vector.tensor_copy(
                        wdT_sb[:, half * 1024:(half + 1) * 1024], pgb[half][0:8, 0:1024]),
                        waits=[("ctr", em.c("ctr"))], inc=("f_" + pname, 1))

            def piece_repl():
                WDT = [fcnt("pg0"), fcnt("pg1")]
                for ts in range(4):
                    psl = ts % 2
                    pname = "pav%d" % psl
                    aq = acquire(pname)
                    em.emit("T", lambda E, ts=ts, psl=psl: nc.tensor.matmul(
                        pav[psl][:, :], repl_sb[:, :], wdT_sb[:, ts * 512:(ts + 1) * 512],
                        start=True, stop=True),
                        waits=WDT + [aq] if ts == 0 else [aq], inc=("cwb", 1))
                    em.emit("V", lambda E, ts=ts, psl=psl: nc.vector.tensor_copy(
                        wdbig[:, ts * 512:(ts + 1) * 512], pav[psl][:, :]),
                        waits=[("cwb", em.c("cwb"))], inc=("f_" + pname, 1))

            def emit_vog(vog):
                sl = vog % 2
                w = ([("bmm", BMM_QK), ("bwu0", em.c("bwu0")), ("bwu1", em.c("bwu1"))]
                     if vog < 2 else [("bvu%d" % sl, vog // 2)])
                em.emit("P", lambda E, vog=vog, sl=sl: E.dma_start(
                    vwbuf[sl].rearrange("p (c n) -> p c n", n=256), wsrc(wvT, vog, 256)),
                    waits=w, inc=("bv%d" % sl, 16))
                for tc in range(16):
                    gi = vog * 16 + tc
                    pp, pname = bcb[gi % 6]
                    aq = acquire(pname)
                    w2 = [("bv%d" % sl, 16 * (vog // 2 + 1)), aq]
                    for ch in range(NCH):
                        em.emit("T", lambda E, pp=pp, sl=sl, ch=ch, tc=tc: nc.tensor.matmul(
                            pp[:, 0:256], xnT[:, ch * T + tc * 128: ch * T + tc * 128 + 128],
                            vwbuf[sl][:, ch * 256:(ch + 1) * 256],
                            start=(ch == 0), stop=False),
                            waits=w2 if ch == 0 else ())
                    grp_inc = ("bvu%d" % sl, 1) if tc == 15 else ("bmm", 1)
                    em.emit("T", lambda E, pp=pp, vog=vog: nc.tensor.matmul(
                        pp[:, 0:256], onerb[:, :], bv_sb[:, vog * 256:(vog + 1) * 256],
                        start=False, stop=True), inc=grp_inc)
                    ssl = gi % 2
                    aqs = acquire("vst%d" % ssl, 16)
                    gw = ("bvu%d" % sl, em.c("bvu%d" % sl)) if tc == 15 else ("bmm", em.c("bmm"))
                    em.emit("S", lambda E, ssl=ssl, pp=pp: nc.scalar.activation(
                        vstage[ssl][:, :], pp[:, 0:256], AF.Identity),
                        waits=[gw, aqs], inc=("f_" + pname, 1))
                    em.emit("Y", lambda E, tc=tc, vog=vog, ssl=ssl: E.dma_start(
                        Vs[tc * 128:(tc + 1) * 128, vog * 256:(vog + 1) * 256],
                        vstage[ssl][:, :]),
                        waits=[fcnt(pname)], inc=("f_vst%d" % ssl, 16))

            emit_vog(0)
            piece_rtrans()
            emit_vog(1)
            piece_wdtrans()
            emit_vog(2)
            piece_repl()
            emit_vog(3)
            em.emit("P", lambda E: E.dma_start(lA_sb[:, :], lAi[:, :]),
                    waits=[("ctg", 16)], inc=("cld", 16))
            em.emit("P", lambda E: E.dma_start(lB_sb[:, :], lBi[:, :]),
                    inc=("cld", 16))
            WDB = [fcnt("pav0"), fcnt("pav1")]
        if "C" in PHASES:
            def wsrc(dram, blk, width):
                r = dram.rearrange("(c p) n -> p c n", p=128)
                return r[:, :, blk * width:(blk + 1) * width]

            CLD = em.c("cld")


            for ts in range(4):
                pp, pname = bcb[ts % 6]
                aq = acquire(pname)
                w = [("cld", CLD), aq]
                for ch in range(NCH):
                    em.emit("T", lambda E, pp=pp, ch=ch, ts=ts: nc.tensor.matmul(
                        pp[:, :], lA_sb[:, ch * 128:(ch + 1) * 128],
                        xnT[:, ch * T + ts * 512: ch * T + ts * 512 + 512],
                        start=(ch == 0), stop=(ch == NCH - 1)),
                        waits=w if ch == 0 else (),
                        inc=("cmid", 1) if ch == NCH - 1 else None)
                w = [("cmid", em.c("cmid"))]
                if ts == 0:
                    w += WDB
                em.emit("V", lambda E, pp=pp, ts=ts: nc.vector.tensor_mul(
                    midgT[:, ts * 512:(ts + 1) * 512], pp[:, :],
                    wdbig[:, ts * 512:(ts + 1) * 512]),
                    waits=w, inc=("f_" + pname, 1))
            MGT = [fcnt("pg0"), fcnt("pg1")]
            if DEBUG:
                em.emit("Y", lambda E: E.dma_start(dmg[:, :], midgT[:, :]),
                        waits=MGT, inc=("dbg", 16))

            bld = {0: 0, 1: 0}
            for ob in range(8):
                sl = ob % 2
                w = list(MGT) if ob < 2 else [("cbu%d" % sl, ob // 2)]
                bld[sl] += 1
                em.emit("P", lambda E, ob=ob, sl=sl: E.dma_start(
                    bwbuf[sl].rearrange("p (c n) -> p c n", n=256), wsrc(bwT, ob, 256)),
                    waits=w, inc=("cb%d" % sl, 16))
                for tc in range(16):
                    gi = ob * 16 + tc
                    pp, pname = bcb[gi % 6]
                    aq = acquire(pname)
                    w = [("cb%d" % sl, 16 * bld[sl]), aq]
                    if ob < 2 and tc == 0:
                        w += list(MGT)
                    for ch in range(NCH):
                        em.emit("T", lambda E, pp=pp, sl=sl, ch=ch, tc=tc: nc.tensor.matmul(
                            pp[:, 0:256], xnT[:, ch * T + tc * 128: ch * T + tc * 128 + 128],
                            bwbuf[sl][:, ch * 256:(ch + 1) * 256],
                            start=(ch == 0), stop=False),
                            waits=w if ch == 0 else ())
                    grp_inc = ("cbu%d" % sl, 1) if tc == 15 else ("cbm", 1)
                    em.emit("T", lambda E, pp=pp, ob=ob, tc=tc: nc.tensor.matmul(
                        pp[:, 0:256], midgT[:, tc * 128:(tc + 1) * 128],
                        lB_sb[:, ob * 256:(ob + 1) * 256], start=False, stop=True),
                        inc=grp_inc)
                    aqs = acquire("gst%d" % psl, 16)
                    gw = ("cbu%d" % sl, em.c("cbu%d" % sl)) if tc == 15 else ("cbm", em.c("cbm"))
                    em.emit("S", lambda E, psl=psl, pp=pp: nc.scalar.activation(
                        gstage[psl][:, :], pp[:, 0:256], AF.Silu),
                        waits=[gw, aqs], inc=("f_" + pname, 1))
                    em.emit("Y", lambda E, tc=tc, ob=ob, psl=psl: E.dma_start(
                        gH[tc * 128:(tc + 1) * 128, ob * 256:(ob + 1) * 256],
                        gstage[psl][:, :]),
                        waits=[fcnt(pname)], inc=("f_gst%d" % psl, 16))

        if "D" in PHASES:
            CBM_W = [("cbm", em.c("cbm")), ("cbu0", em.c("cbu0")),
                     ("cbu1", em.c("cbu1"))]
            B_ST = [("f_qkst0", 16 * use.get("qkst0", 0)),
                    ("f_qkst1", 16 * use.get("qkst1", 0)),
                    ("f_vst0", 16 * use.get("vst0", 0)),
                    ("f_vst1", 16 * use.get("vst1", 0))]
            G_ST = [("f_gst0", 16 * use.get("gst0", 0)),
                    ("f_gst1", 16 * use.get("gst1", 0))]
            em.emit("V", lambda E: nc.vector.memset(vslots[:, :, 128:129], 1.0),
                    waits=CBM_W + B_ST + G_ST, inc=("dms", 1))
            vsr = Vs.rearrange("(kc p) (h d) -> p kc h d", p=128, d=128)

            kld_cnt = {}
            qld_cnt = {}

            def emit_k(qsx, i):
                ksl = i % 2
                w = []
                if kuse[ksl] > 0:
                    w.append(("dkf%d" % ksl, kuse[ksl]))
                elif qsx == 0:
                    w += [("f_qkst0", 16 * use.get("qkst0", 0)),
                          ("f_qkst1", 16 * use.get("qkst1", 0))] + CBM_W
                kuse[ksl] += 1
                km = 4 * (qsx + 1) * 128
                em.emit("Y", lambda E, i=i, ksl=ksl, km=km: E.dma_start(
                    k_slot[ksl][:, 0:km], Ks[i, :, 0:km]),
                    waits=w, inc=("dk%d" % ksl, 16))
                kld_cnt[(qsx, i)] = (ksl, em.c("dk%d" % ksl))

            def emit_q(qsx, uu):
                i, j = uu // 2, uu % 2
                qidx = j * 8 + i
                qsl = (qsx * 16 + uu) % 4
                w = []
                if quse[qsl] > 0:
                    w.append(("dqf%d" % qsl, quse[qsl]))
                elif qsx == 0:
                    w += [("f_qkst0", 16 * use.get("qkst0", 0)),
                          ("f_qkst1", 16 * use.get("qkst1", 0))] + CBM_W
                quse[qsl] += 1
                em.emit("Y", lambda E, qidx=qidx, qsl=qsl, qsx=qsx: E.dma_start(
                    q_slot[qsl][:, :], Qs[qidx, :, qsx * 512:qsx * 512 + 512]),
                    waits=w, inc=("dq%d" % qsl, 16))
                qld_cnt[(qsx, uu)] = (qsl, em.c("dq%d" % qsl))

            def emit_kq_ahead(qsx, uu):
                if uu > 15:
                    qsx, uu = qsx + 1, uu - 16
                    if qsx > 3:
                        return
                if uu % 2 == 0:
                    emit_k(qsx, uu // 2)
                emit_q(qsx, uu)

            def emit_g(qs, qc):
                w = []
                if qs > 0:
                    w.append(("dfin", 4 * (qs - 1) + qc + 1))
                elif qs == 0 and qc < 2:
                    w += G_ST
                r0 = qs * 512 + qc * 128
                em.emit("S", lambda E, qc=qc, r0=r0: E.dma_start(
                    g_slot[qc][:, :], gH[r0:r0 + 128, :]),
                    waits=w, inc=("dg%d" % qc, 16))

            avbanks = [(pav[0], "pav0"), (pav[1], "pav1"), (pg[0], "pg0"), (pg[1], "pg1")]

            def emit_avqc(cx, qc, qs, u):
                gqc = 4 * qs + qc
                pt, pname = avbanks[qc]
                aq = acquire(pname)
                w = [aq]
                if qc == 0 or cx.get("needw"):
                    w += cx["WEXP"] + [("dmk", cx["DMK"])]
                    cx["needw"] = False
                if qc == 0 and u <= 1:
                    w += [("dv%d" % qq, 64) for qq in range(qs + 1)]
                for kc in range(gqc + 1):
                    mm_inc = ("dav", 1) if kc == gqc else None
                    if kc == 0 and kc != gqc:
                        if qc == 1:
                            mm_inc = ("dqf%d" % cx["qsl"], 1)
                        elif qc == 2 and cx["j"] == 1:
                            mm_inc = ("dkf%d" % cx["ksl"], 1)
                    em.emit("T", lambda E, pt=pt, wsl=cx["wsl"], kc=kc,
                            qc=qc, i=cx["i"], gqc=gqc:
                            nc.tensor.matmul(
                                pt[:, 0:129],
                                wT[wsl][:, kc * 512 + qc * 128:
                                         kc * 512 + qc * 128 + 128],
                                vslots[:, kc * 8 + i, :],
                                start=(kc == 0), stop=(kc == gqc)),
                            waits=w if kc == 0 else (),
                            inc=mm_inc)
                DAV = em.c("dav")
                em.emit("V", lambda E, pt=pt, qc=qc: nc.vector.reciprocal(
                    recip[:, qc:qc + 1], pt[:, 128:129]),
                    waits=[("dav", DAV), ("dvc", em.c("dvc")), aq],
                    inc=("dvc", 1))
                em.emit("V", lambda E, pt=pt, qc=qc, qidx=cx["qidx"], qs=qs:
                        nc.vector.tensor_scalar(
                            s_slab[qs % 2][:, qc * 2048 + qidx * 128:
                                           qc * 2048 + qidx * 128 + 128],
                            pt[:, 0:128], recip[:, qc:qc + 1],
                            None, ALU.mult),
                        waits=[("dvc", em.c("dvc")),
                               ("dofq%d" % qc, 16 * max(0, qs - 1))],
                        inc=("f_" + pname, 1))

            kuse = {0: 0, 1: 0}
            quse = {0: 0, 1: 0, 2: 0, 3: 0}
            batch_ctr = 0
            qh_ctr = 0
            pexp = {0: 0, 1: 0}
            wav = {0: 0, 1: 0}
            xg_use = {0: 0, 1: 0}
            out_ct = 0
            def emit_vload(qsx):
                for kc in range(4 * qsx, 4 * qsx + 4):
                    w = [("dms", 1)] + B_ST if (qsx == 0 and kc == 0) else ()
                    em.emit("Y", lambda E, kc=kc: E.dma_start(
                        vslots[:, kc * 8:(kc + 1) * 8, 0:128], vsr[:, kc, :, :]),
                        waits=w, inc=("dv%d" % qsx, 16))

            for qs in range(4):
                kmax = 4 * (qs + 1)
                if qs == 0:
                    emit_vload(0)
                if qs == 0:
                    emit_k(0, 0)
                    emit_q(0, 0)
                    emit_q(0, 1)
                pend = None
                for u in range(17):
                    if u < 16:
                        i = u // 2
                        j = u % 2
                        ksl = i % 2
                        emit_kq_ahead(qs, u + 2)
                        qidx = j * 8 + i
                        qsl, dq_w = qld_cnt[(qs, u)]
                        _, dk_w = kld_cnt[(qs, i)]
                        wsl = qh_ctr % 2
                        nb = kmax // 2
                        seq = []
                        pos = {}
                        if pend is not None:
                            for qj in range(4):
                                pos.setdefault((qj * nb) // 4, []).append(qj)
                        for bi in range(nb):
                            seq.append(("b", bi))
                            for qj in pos.get(bi, ()):
                                seq.append(("q", qj))
                        first = True
                        for kind, idx in seq:
                            if kind == "b":
                                kc0 = 2 * idx
                                psl = batch_ctr % 2
                                ps = psA if psl == 0 else psB
                                psn = "ps%d" % psl
                                w = [("f_" + psn, pexp[psl])]
                                if first:
                                    w += [("dk%d" % ksl, dk_w),
                                          ("dq%d" % qsl, dq_w),
                                          ("dav", wav[wsl])]
                                    if qs == 0 and u == 0 and bi < 2:
                                        w += [fcnt("psAa"), fcnt("psAb"),
                                              fcnt("psBa"), fcnt("psBb")]
                                    first = False
                                for uu in range(2):
                                    kc = kc0 + uu
                                    em.emit("T", lambda E, ps=ps, ksl=ksl, qsl=qsl, kc=kc, uu=uu:
                                            nc.tensor.matmul(
                                                ps[:, uu * 512:(uu + 1) * 512],
                                                k_slot[ksl][:, kc * 128:(kc + 1) * 128],
                                                q_slot[qsl][:, :], start=True, stop=True),
                                            waits=w if uu == 0 else (),
                                            inc=("dsc", 1) if uu == 1 else None)
                                em.emit("S", lambda E, ps=ps, wsl=wsl, kc0=kc0:
                                        nc.scalar.activation(
                                            wT[wsl][:, kc0 * 512: kc0 * 512 + 1024],
                                            ps[:, 0:1024], AF.Exp,
                                            bias=konst[:, 0:1], scale=SCALE),
                                        waits=[("dsc", em.c("dsc"))],
                                        inc=("f_" + psn, 1))
                                pexp[psl] = em.c("f_" + psn)
                                batch_ctr += 1
                            else:
                                emit_avqc(pend, idx, qs, u)
                        WEXP = [("f_ps0", em.c("f_ps0")), ("f_ps1", em.c("f_ps1"))]
                        for dkc in range(4 * qs, kmax):
                            off = dkc * 512 + (dkc - 4 * qs) * 128
                            em.emit("V", lambda E, wsl=wsl, off=off: nc.vector.tensor_mul(
                                wT[wsl][:, off:off + 128], wT[wsl][:, off:off + 128],
                                mask_sb[:, :]),
                                waits=WEXP if dkc == 4 * qs else (),
                                inc=("dmk", 1))
                        ctx = dict(wsl=wsl, WEXP=WEXP, DMK=em.c("dmk"),
                                   qidx=qidx, i=i, j=j, ksl=ksl, qsl=qsl)
                        if u == 8 and qs < 3:
                            emit_vload(qs + 1)
                        if u in (4, 7, 10, 13):
                            emit_g(qs, (u - 4) // 3)
                        qh_ctr += 1
                    if u == 16:
                        for qc in range(4):
                            emit_avqc(pend, qc, qs, u)
                    if u >= 1:
                        wav[pend["wsl"]] = em.c("dav")
                    if u == 16:
                        pend = None
                    if u < 16:
                        pend = ctx
                SEV = [("f_pav0", em.c("f_pav0")), ("f_pav1", em.c("f_pav1")),
                       ("f_pg0", em.c("f_pg0")), ("f_pg1", em.c("f_pg1"))]
                for qc in range(4):
                    r0 = qs * 512 + qc * 128
                    w = SEV + [("dg%d" % qc, 16 * (qs + 1)),
                               ("dofq%d" % qc, 16 * qs),
                               ("dfin", em.c("dfin"))]
                    em.emit("V", lambda E, qc=qc, qs=qs: nc.vector.tensor_mul(
                        s_slab[qs % 2][:, qc * 2048:(qc + 1) * 2048],
                        s_slab[qs % 2][:, qc * 2048:(qc + 1) * 2048], g_slot[qc][:, :]),
                        waits=w, inc=("dfin", 1))
                    em.emit("P", lambda E, qc=qc, r0=r0, qs=qs: E.dma_start(
                        out[r0:r0 + 128, :], s_slab[qs % 2][:, qc * 2048:(qc + 1) * 2048],
                        accum_op=ALU.add),
                        waits=[("dfin", em.c("dfin")), ("dpre", 64)],
                        inc=("dofq%d" % qc, 16))
                    out_ct += 1
            em.emit("Y", lambda E: E.nop(),
                    waits=[("dofq%d" % qc, 64) for qc in range(4)])

        with nc.Block() as block:
            @block.tensor
            def _(E):
                em.play("T", E)

            @block.vector
            def _(E):
                em.play("V", E)

            @block.scalar
            def _(E):
                em.play("S", E)

            @block.gpsimd
            def _(E):
                em.play("P", E)

            @block.sync
            def _(E):
                em.play("Y", E)

    return nc


def _host_prep(inputs):
    x = np.asarray(inputs["L0_attn_x"])
    Wq = np.asarray(inputs["L0_attn_Wq"]); bq = np.asarray(inputs["L0_attn_bq"])
    Wk = np.asarray(inputs["L0_attn_Wk"]); bk = np.asarray(inputs["L0_attn_bk"])
    Wv = np.asarray(inputs["L0_attn_Wv"]); bv = np.asarray(inputs["L0_attn_bv"])
    bW = np.asarray(inputs["L0_ffn_base_W"])
    rW = np.asarray(inputs["L0_ffn_router_W"])
    lA = np.asarray(inputs["L0_ffn_lora_A"])
    lB = np.asarray(inputs["L0_ffn_lora_B"])

    ident = np.eye(128, dtype=ml_dtypes.bfloat16)
    identf = np.eye(128, dtype=np.float32)
    masku = np.triu(np.ones((128, 128), np.float32)).astype(ml_dtypes.bfloat16)
    repl = np.zeros((8, 128), np.float32)
    for e in range(8):
        repl[e, e * 16:(e + 1) * 16] = 1.0
    repl = repl.astype(ml_dtypes.bfloat16)
    onec = np.ones((128, 1), dtype=ml_dtypes.bfloat16)
    oner = np.ones((1, 128), dtype=np.float32)
    onerb = np.ones((1, 128), dtype=ml_dtypes.bfloat16)
    lA2 = np.ascontiguousarray(
        lA.transpose(1, 0, 2).reshape(C, 128).reshape(32, 128, 128)
        .transpose(1, 0, 2).reshape(128, C)).astype(np.float32)
    rt2 = np.ascontiguousarray(
        rW.T.reshape(32, 128, 8).transpose(1, 0, 2).reshape(128, 256)).astype(np.float32)
    lBf = lB.reshape(128, C)

    in_maps = []
    qsels = []
    for c in range(8):
        b, h = c // 2, c % 2
        qlist = [8 * h + i for i in range(8)] + [16 + 8 * h + i for i in range(8)]
        qsel = np.concatenate([np.arange(qh * 128, (qh + 1) * 128) for qh in qlist])
        kvsel = np.arange(1024 * h, 1024 * h + 1024)
        qsels.append(qsel)
        m = dict(
            xT=np.ascontiguousarray(x[b].T).astype(np.float32),
            xsel=np.ascontiguousarray(x[b][:, qsel]).astype(np.float32),
            wqT=np.ascontiguousarray(Wq[qsel, :].T).astype(np.float32),
            wkT=np.ascontiguousarray(Wk[kvsel, :].T).astype(np.float32),
            wvT=np.ascontiguousarray(Wv[kvsel, :].T).astype(np.float32),
            bqT=np.ascontiguousarray(bq[qsel].reshape(16, 128).T).astype(np.float32),
            bkT=np.ascontiguousarray(bk[kvsel].reshape(8, 128).T).astype(np.float32),
            bvr=np.ascontiguousarray(bv[kvsel].reshape(1, 1024)).astype(np.float32),
            bwT=np.ascontiguousarray(bW[qsel, :].T).astype(np.float32),
            lAi=lA2, lBi=np.ascontiguousarray(lBf[:, qsel]).astype(np.float32),
            rti=rt2, idbi=ident, idfi=identf, maski=masku, repli=repl,
            oneci=onec, oneri=oner, onerbi=onerb,
        )
        in_maps.append(m)
    return in_maps, qsels


def kernel(**inputs):
    in_maps, qsels = _host_prep(inputs)
    nc = build_program()
    res = run_bass_kernel_spmd(nc, in_maps, core_ids=list(range(8)))
    outp = np.zeros((4, T, C), dtype=np.float32)
    for c in range(8):
        b = c // 2
        o = np.asarray(res.results[c]["out"], dtype=np.float32)
        outp[b][:, qsels[c]] = o
    return outp


# version 18
# speedup vs baseline: 1.0042x; 1.0042x over previous
import os
import numpy as np
import ml_dtypes
import concourse.bass as bass
import concourse.mybir as mybir
from concourse.bass_utils import run_bass_kernel_spmd
from contextlib import ExitStack

F32 = mybir.dt.float32
BF16 = mybir.dt.bfloat16
AF = mybir.ActivationFunctionType
ALU = mybir.AluOpType
AX = mybir.AxisListType

T = 2048
C = 4096
NCH = 32
SCALE = 1.0 / float(np.sqrt(np.float32(32)))
EXP_SHIFT = -12.0
EPS = 1e-5

DEBUG = int(os.environ.get("MK_DEBUG", "0"))
PHASES = os.environ.get("MK_PHASES", "ABCD")


class Em:

    ENGS = ("T", "V", "S", "P", "Y")

    def __init__(self, nc, stack):
        self.nc = nc
        self.ops = {e: [] for e in self.ENGS}
        self.sems = {}
        self.cnt = {}
        self.stack = stack

    def sem(self, name):
        if name not in self.sems:
            self.sems[name] = self.stack.enter_context(self.nc.semaphore(name))
            self.cnt[name] = 0
        return self.sems[name]

    def c(self, name):
        self.sem(name)
        return self.cnt[name]

    def emit(self, eng, fn, waits=(), inc=None):
        wl = [(self.sem(n), v) for (n, v) in waits]
        il = None
        if inc is not None:
            n, d = inc
            self.sem(n)
            self.cnt[n] += d
            il = (self.sems[n], d)
        self.ops[eng].append((wl, fn, il))

    def barrier(self, name):
        self.sem(name)
        tgt = self.cnt[name] + 5
        for e in self.ENGS:
            self.emit(e, lambda E: E.nop(), inc=(name, 1))
        for e in self.ENGS:
            self.ops[e].append(([(self.sems[name], tgt)], None, None))

    def play(self, eng, E):
        for wl, fn, il in self.ops[eng]:
            for (h, v) in wl:
                E.wait_ge(h, v)
            if fn is not None:
                ins = fn(E)
                if il is not None:
                    ins.then_inc(*il)


def build_program():
    nc = bass.Bass("TRN2", target_bir_lowering=False)

    xT = nc.dram_tensor("xT", [C, T], F32, kind="ExternalInput")
    xsel = nc.dram_tensor("xsel", [T, 2048], F32, kind="ExternalInput")
    wqT = nc.dram_tensor("wqT", [C, 2048], F32, kind="ExternalInput")
    wkT = nc.dram_tensor("wkT", [C, 1024], F32, kind="ExternalInput")
    wvT = nc.dram_tensor("wvT", [C, 1024], F32, kind="ExternalInput")
    bqT = nc.dram_tensor("bqT", [128, 16], F32, kind="ExternalInput")
    bkT = nc.dram_tensor("bkT", [128, 8], F32, kind="ExternalInput")
    bvr = nc.dram_tensor("bvr", [1, 1024], F32, kind="ExternalInput")
    bwT = nc.dram_tensor("bwT", [C, 2048], F32, kind="ExternalInput")
    lAi = nc.dram_tensor("lAi", [128, 4096], F32, kind="ExternalInput")
    lBi = nc.dram_tensor("lBi", [128, 2048], F32, kind="ExternalInput")
    rti = nc.dram_tensor("rti", [128, 256], F32, kind="ExternalInput")
    idbi = nc.dram_tensor("idbi", [128, 128], BF16, kind="ExternalInput")
    maski = nc.dram_tensor("maski", [128, 128], BF16, kind="ExternalInput")
    repli = nc.dram_tensor("repli", [8, 128], BF16, kind="ExternalInput")
    oneci = nc.dram_tensor("oneci", [128, 1], BF16, kind="ExternalInput")
    oneri = nc.dram_tensor("oneri", [1, 128], F32, kind="ExternalInput")
    idfi = nc.dram_tensor("idfi", [128, 128], F32, kind="ExternalInput")
    onerbi = nc.dram_tensor("onerbi", [1, 128], BF16, kind="ExternalInput")

    scratch_kind = "ExternalOutput" if DEBUG else "Internal"
    Qs = nc.dram_tensor("Qs", [16, 128, T], BF16, kind=scratch_kind)
    Ks = nc.dram_tensor("Ks", [8, 128, T], BF16, kind=scratch_kind)
    Vs = nc.dram_tensor("Vs", [T, 1024], BF16, kind=scratch_kind)
    gH = nc.dram_tensor("gH", [T, 2048], F32, kind=scratch_kind)
    out = nc.dram_tensor("out", [T, 2048], F32, kind="ExternalOutput")
    if DEBUG:
        dxn = nc.dram_tensor("dxn", [128, NCH * T], BF16, kind="ExternalOutput")
        dwd = nc.dram_tensor("dwd", [128, 128], BF16, kind="ExternalOutput")
        dmg = nc.dram_tensor("dmg", [128, 2048], BF16, kind="ExternalOutput")

    with ExitStack() as stack:
        en = stack.enter_context
        arena = en(nc.sbuf_tensor("arena", [128, 65536], BF16))
        wstr = en(nc.sbuf_tensor("wstr", [128, 16384], BF16))
        vsb = en(nc.sbuf_tensor("vsb", [128, 16512], BF16))
        bq_sb = en(nc.sbuf_tensor("bq_sb", [128, 16], F32))
        bk_sb = en(nc.sbuf_tensor("bk_sb", [128, 8], F32))
        bv_sb = en(nc.sbuf_tensor("bv_sb", [1, 1024], BF16))
        rt_sb = en(nc.sbuf_tensor("rt_sb", [128, 256], BF16))
        idb = en(nc.sbuf_tensor("idb", [128, 128], BF16))
        idf = en(nc.sbuf_tensor("idf", [128, 128], F32))
        mask_sb = en(nc.sbuf_tensor("mask_sb", [128, 128], BF16))
        repl_sb = en(nc.sbuf_tensor("repl_sb", [8, 128], BF16))
        onec = en(nc.sbuf_tensor("onec", [128, 1], BF16))
        oner = en(nc.sbuf_tensor("oner", [1, 128], F32))
        onerb = en(nc.sbuf_tensor("onerb", [1, 128], BF16))
        konst = en(nc.sbuf_tensor("konst", [128, 8], F32))
        nm1 = en(nc.sbuf_tensor("nm1", [128, 16], F32))
        nm2 = en(nc.sbuf_tensor("nm2", [128, 16], F32))
        trow = en(nc.sbuf_tensor("trow", [128, 16], F32))
        inv2 = en(nc.sbuf_tensor("inv2", [128, 16], F32))
        keep = en(nc.sbuf_tensor("keep", [128, 128], F32))
        lmsk = en(nc.sbuf_tensor("lmsk", [128, 128], F32))
        recip = en(nc.sbuf_tensor("recip", [128, 4], F32))

        arena32 = arena.bitcast(F32)
        wstr32 = wstr.bitcast(F32)
        vsb32 = vsb.bitcast(F32)

        xnT = arena
        s_slab = [arena32[:, 0:8192], arena32[:, 8192:16384]]
        g_slot = [arena32[:, 16384 + 2048 * i: 16384 + 2048 * (i + 1)]
                  for i in range(4)]
        wT = [arena[:, 49152:57344], arena[:, 57344:65536]]
        q_slot = [wstr[:, 4096 + 512 * i: 4096 + 512 * (i + 1)] for i in range(4)]
        k_slot = [wstr[:, 0:2048], wstr[:, 2048:4096]]

        mrow = wstr32[0:1, 4096:6144]
        qrow = wstr32[0:1, 6144:8192]
        wbuf = [wstr[:, 0:4096], wstr[:, 4096:8192]]
        vwbuf = [wstr[:, 0:8192], wstr[:, 8192:16384]]
        bwbuf = [wstr[:, 0:8192], wstr[:, 8192:16384]]

        rrow = vsb32[0:1, 0:2048]
        lgT_sb = vsb32[0:8, 0:2048]
        rb_sb = vsb[:, 4096:6144]
        mb_sb = vsb[:, 8192:10240]
        sqb = [vsb[:, 12288:14336], vsb[:, 14336:16384]]
        lA_sb = vsb[:, 0:4096]
        lB_sb = vsb[:, 4096:6144]
        wdbig = vsb[:, 6144:8192]
        midgT = vsb[:, 8192:10240]
        lgn = vsb32[:, 5120:5248]
        e_sb = vsb32[:, 5248:5376]
        wd_sb = vsb[:, 10752:10880]
        wdT_sb = vsb[0:8, 10880:12928]
        qkstage = [vsb[:, 12928:13440], vsb[:, 13440:13952]]
        vstage = [vsb[:, 13952:14208], vsb[:, 14208:14464]]
        gstage = [vsb32[:, 7232:7488], vsb32[:, 7488:7744]]
        vslots = vsb.rearrange("p (s n) -> p s n", n=129)

        rem = nc.sbuf_bytes_remaining
        assert rem >= 0, f"SBUF over budget by {-rem} bytes"

        psA = en(nc.psum_tensor("psA", [128, 1024], F32))
        psB = en(nc.psum_tensor("psB", [128, 1024], F32))
        pav = [en(nc.psum_tensor("pav0", [128, 512], F32)),
               en(nc.psum_tensor("pav1", [128, 512], F32))]
        pg = [en(nc.psum_tensor("pg0", [128, 512], F32)),
              en(nc.psum_tensor("pg1", [128, 512], F32))]
        pgb = [pg[0].bitcast(BF16), pg[1].bitcast(BF16)]

        em = Em(nc, stack)
        use = {}

        def acquire(name, scale=1):
            u = use.get(name, 0)
            use[name] = u + 1
            return ("f_" + name, scale * u)

        def fcnt(name):
            return ("f_" + name, em.c("f_" + name))

        em.emit("V", lambda E: nc.vector.memset(konst[:, 0:1], EXP_SHIFT), inc=("kon", 1))
        em.emit("V", lambda E: nc.vector.memset(konst[:, 1:2], EPS), inc=("kon", 1))
        for dst, src in [(bq_sb, bqT), (bk_sb, bkT), (idb, idbi), (idf, idfi),
                         (mask_sb, maski),
                         (repl_sb, repli), (onec, oneci), (oner, oneri), (onerb, onerbi)]:
            em.emit("Y", lambda E, d=dst, sr=src: E.dma_start(d[:, :], sr[:, :]),
                    inc=("kload", 16))
        em.emit("P", lambda E: E.dma_start(bv_sb[:, :], bvr[:, :]), inc=("kloadp", 16))
        em.emit("P", lambda E: E.dma_start(rt_sb[:, :], rti[:, :]), inc=("kloadp", 16))
        KL = em.c("kload")
        KLP = em.c("kloadp")

        if "A" in PHASES:
            sum_ps = [psA[0:1, 0:512], psA[0:1, 512:1024],
                      psB[0:1, 0:512], psB[0:1, 512:1024]]
            sq_ps = [pav[0][0:1, 0:512], pav[1][0:1, 0:512],
                     pg[0][0:1, 0:512], pg[1][0:1, 0:512]]
            aqs0 = [acquire(n) for n in
                    ("psAa", "psAb", "psBa", "psBb", "pav0", "pav1", "pg0", "pg1")]
            for c in range(NCH):
                b = c % 2
                sname = "axp%d" % (c % 8)
                w = [] if c < 8 else [(sname, 16 * (c // 8))]
                em.emit("P", lambda E, c=c: E.dma_start(
                    xnT[:, c * T:(c + 1) * T], xT[c * 128:(c + 1) * 128, :]),
                    waits=w, inc=(sname, 16))
                w = [(sname, 16 * (c // 8 + 1))]
                if c >= 2:
                    w.append(("astat", 8 * (c - 1)))
                em.emit("S", lambda E, b=b, c=c: nc.scalar.activation(
                    sqb[b][:, :], xnT[:, c * T:(c + 1) * T], AF.Square),
                    waits=w, inc=("asq", 1))
                w = [(sname, 16 * (c // 8 + 1)), ("asq", c + 1)]
                if c == 0:
                    w += [("kload", KL), ("kloadp", KLP)] + aqs0
                for ts in range(4):
                    em.emit("T", lambda E, c=c, ts=ts: nc.tensor.matmul(
                        sum_ps[ts], onec[:, :],
                        xnT[:, c * T + ts * 512: c * T + ts * 512 + 512],
                        start=(c == 0), stop=(c == NCH - 1)),
                        waits=w if ts == 0 else (), inc=("astat", 1))
                for ts in range(4):
                    em.emit("T", lambda E, b=b, ts=ts, c=c: nc.tensor.matmul(
                        sq_ps[ts], onec[:, :], sqb[b][:, ts * 512:(ts + 1) * 512],
                        start=(c == 0), stop=(c == NCH - 1)),
                        inc=("astat", 1))
            AST = em.c("astat")

            ev_rel = ["psAa", "psAb", "psBa", "psBb"]
            for ts in range(4):
                w = [("astat", AST)] if ts == 0 else ()
                em.emit("V", lambda E, ts=ts: nc.vector.tensor_copy(
                    mrow[:, ts * 512:(ts + 1) * 512], sum_ps[ts]),
                    waits=w, inc=("f_" + ev_rel[ts], 1))
            sq_rel = ["pav0", "pav1", "pg0", "pg1"]
            for ts in range(4):
                em.emit("V", lambda E, ts=ts: nc.vector.tensor_copy(
                    qrow[:, ts * 512:(ts + 1) * 512], sq_ps[ts]),
                    inc=("f_" + sq_rel[ts], 1))
            wrows = [fcnt("psAa"), fcnt("psAb"), fcnt("psBa"), fcnt("psBb"),
                     fcnt("pav0"), fcnt("pav1"), fcnt("pg0"), fcnt("pg1")]
            em.emit("V", lambda E: nc.vector.tensor_scalar(
                mrow[:, :], mrow[:, :], 1.0 / C, None, ALU.mult),
                waits=wrows, inc=("arow", 1))
            em.emit("V", lambda E: nc.vector.tensor_scalar(
                qrow[:, :], qrow[:, :], 1.0 / C, None, ALU.mult),
                waits=[("arow", em.c("arow"))], inc=("arow", 1))
            em.emit("V", lambda E: nc.vector.tensor_mul(rrow[:, :], mrow[:, :], mrow[:, :]),
                    waits=[("arow", em.c("arow"))], inc=("arow", 1))
            em.emit("V", lambda E: nc.vector.tensor_sub(qrow[:, :], qrow[:, :], rrow[:, :]),
                    waits=[("arow", em.c("arow"))], inc=("arow", 1))
            em.emit("S", lambda E: nc.scalar.activation(
                rrow[:, :], qrow[:, :], AF.Ln, bias=konst[0:1, 1:2]),
                waits=[("arow", em.c("arow")), ("kon", 2)], inc=("asd", 1))
            em.emit("S", lambda E: nc.scalar.activation(
                rrow[:, :], rrow[:, :], AF.Exp, scale=-0.5),
                waits=[("asd", 1)], inc=("asd", 1))
            em.emit("V", lambda E: E.nop(),
                    waits=[("asd", 2)], inc=("arow", 1))
            em.emit("V", lambda E: nc.vector.scalar_tensor_tensor(
                mrow[:, :], mrow[:, :], -1.0, rrow[:, :], ALU.mult, ALU.mult),
                waits=[("arow", em.c("arow"))], inc=("arow", 1))
            ARW = em.c("arow")

            for ts in range(4):
                a0 = acquire("pg0")
                a1 = acquire("pg1")
                em.emit("T", lambda E, ts=ts: nc.tensor.matmul(
                    pg[0][:, :], oner[:, :], rrow[:, ts * 512:(ts + 1) * 512],
                    start=True, stop=True),
                    waits=[("arow", ARW), a0, a1], inc=("abc", 1))
                em.emit("T", lambda E, ts=ts: nc.tensor.matmul(
                    pg[1][:, :], oner[:, :], mrow[:, ts * 512:(ts + 1) * 512],
                    start=True, stop=True), inc=("abc", 1))
                em.emit("V", lambda E, ts=ts: nc.vector.tensor_copy(
                    rb_sb[:, ts * 512:(ts + 1) * 512], pg[0][:, :]),
                    waits=[("abc", 2 * (ts + 1))], inc=("f_pg0", 1))
                em.emit("V", lambda E, ts=ts: nc.vector.tensor_copy(
                    mb_sb[:, ts * 512:(ts + 1) * 512], pg[1][:, :]),
                    inc=("f_pg1", 1))
            WBC = [fcnt("pg0"), fcnt("pg1")]

            for c in range(NCH):
                w = [("anrm", em.c("anrm"))]
                if c == 0:
                    w += WBC
                em.emit("V", lambda E, c=c: nc.vector.tensor_mul(
                    xnT[:, c * T:(c + 1) * T], xnT[:, c * T:(c + 1) * T], rb_sb[:, :]),
                    waits=w, inc=("anrm", 1))
                em.emit("V", lambda E, c=c: nc.vector.tensor_add(
                    xnT[:, c * T:(c + 1) * T], xnT[:, c * T:(c + 1) * T], mb_sb[:, :]),
                    waits=[("anrm", em.c("anrm"))], inc=("anrm", 1))
            if DEBUG:
                em.emit("Y", lambda E: E.dma_start(dxn[:, :], xnT[:, :]),
                        waits=[("anrm", em.c("anrm"))], inc=("dbg", 16))

        if "B" in PHASES:
            def wsrc(dram, blk, width):
                r = dram.rearrange("(c p) n -> p c n", p=128)
                return r[:, :, blk * width:(blk + 1) * width]

            for i in range(4):
                em.emit("Y", lambda E, i=i: E.dma_start(
                    out[i * 512:(i + 1) * 512, :], xsel[i * 512:(i + 1) * 512, :]),
                    inc=("dpre", 16))
            bcb = [(pg[0], "pg0"), (pg[1], "pg1"),
                   (psA[:, 0:512], "psAa"), (psA[:, 512:1024], "psAb"),
                   (psB[:, 0:512], "psBa"), (psB[:, 512:1024], "psBb")]
            ANRM = em.c("anrm")
            wld = {0: 0, 1: 0}
            for og in range(24):
                sl = og % 2
                isq = og < 16
                dram = wqT if isq else wkT
                ogl = og if isq else og - 16
                w = [] if og < 2 else [("bwu%d" % sl, og // 2)]
                wld[sl] += 1
                em.emit("P", lambda E, d=dram, ogl=ogl, sl=sl: E.dma_start(
                    wbuf[sl].rearrange("p (c n) -> p c n", n=128), wsrc(d, ogl, 128)),
                    waits=w, inc=("bw%d" % sl, 16))
                for ts in range(4):
                    gi = og * 4 + ts
                    pp, pname = bcb[gi % 6]
                    aq = acquire(pname)
                    w = [("bw%d" % sl, 16 * wld[sl]), aq]
                    for ch in range(NCH):
                        last = (ch == NCH - 1)
                        grp_inc = (("bwu%d" % sl, 1) if ts == 3 else ("bmm", 1)) if last else None
                        wch = w if ch == 0 else []
                        if og == 0 and ts == 0:
                            wch = list(wch) + [("anrm", 2 * (ch + 1))]
                        em.emit("T", lambda E, pp=pp, sl=sl, ch=ch, ts=ts: nc.tensor.matmul(
                            pp[:, :], wbuf[sl][:, ch * 128:(ch + 1) * 128],
                            xnT[:, ch * T + ts * 512: ch * T + ts * 512 + 512],
                            start=(ch == 0), stop=(ch == NCH - 1)),
                            waits=wch,
                            inc=grp_inc)
                    ssl = gi % 2
                    aqs = acquire("qkst%d" % ssl, 16)
                    bias = bq_sb[:, og:og + 1] if isq else bk_sb[:, ogl:ogl + 1]
                    gw = ("bwu%d" % sl, em.c("bwu%d" % sl)) if ts == 3 else ("bmm", em.c("bmm"))
                    em.emit("S", lambda E, ssl=ssl, pp=pp, bias=bias: nc.scalar.activation(
                        qkstage[ssl][:, :], pp[:, :], AF.Identity, bias=bias),
                        waits=[gw, aqs], inc=("f_" + pname, 1))
                    dst = (Qs[og, :, ts * 512:(ts + 1) * 512] if isq
                           else Ks[ogl, :, ts * 512:(ts + 1) * 512])
                    em.emit("Y", lambda E, dst=dst, ssl=ssl: E.dma_start(
                        dst, qkstage[ssl][:, :]),
                        waits=[fcnt(pname)], inc=("f_qkst%d" % ssl, 16))
            for ts in range(4):
                psl = ts % 2
                pname = "pav%d" % psl
                aq = acquire(pname)
                for ch in range(NCH):
                    em.emit("T", lambda E, psl=psl, ch=ch, ts=ts: nc.tensor.matmul(
                        pav[psl][0:8, :], rt_sb[:, ch * 8:(ch + 1) * 8],
                        xnT[:, ch * T + ts * 512: ch * T + ts * 512 + 512],
                        start=(ch == 0), stop=(ch == NCH - 1)),
                        waits=[aq] if ch == 0 else (),
                        inc=("clg", 1) if ch == NCH - 1 else None)
                em.emit("V", lambda E, psl=psl, ts=ts: nc.vector.tensor_copy(
                    lgT_sb[:, ts * 512:(ts + 1) * 512], pav[psl][0:8, :]),
                    waits=[("clg", em.c("clg"))], inc=("f_" + pname, 1))
            LGT = [("f_pav0", em.c("f_pav0")), ("f_pav1", em.c("f_pav1"))]
            BMM_QK = em.c("bmm")

            def piece_rtrans():
                aq_tr = acquire("pg0")
                for tc in range(16):
                    w = LGT + [aq_tr] if tc == 0 else ()
                    em.emit("T", lambda E, tc=tc: nc.tensor.transpose(
                        pg[0][:, tc * 8:(tc + 1) * 8],
                        lgT_sb[:, tc * 128:(tc + 1) * 128], idf[0:8, 0:8]),
                        waits=w, inc=("ctg", 1))
                em.emit("V", lambda E: nc.vector.tensor_copy(
                    lgn[:, :], pg[0][:, 0:128]),
                    waits=[("ctg", 16)], inc=("f_pg0", 1))
                def b8(ap2d, stride):
                    return bass.AP(ap2d.tensor, ap2d.offset,
                                   [[stride, 128], [1, 16], [0, 8]])
                lgn3 = lgn.rearrange("p (g n) -> p g n", n=8)
                e3 = e_sb.rearrange("p (g n) -> p g n", n=8)
                keep3 = keep.rearrange("p (g n) -> p g n", n=8)
                wd3 = wd_sb.rearrange("p (g n) -> p g n", n=8)
                lmsk3 = lmsk.rearrange("p (g n) -> p g n", n=8)
                nm1b = b8(nm1[:, :], 16)
                trowb = b8(trow[:, :], 16)
                inv2b = b8(inv2[:, :], 16)
                em.emit("V", lambda E: nc.vector.reduce_max(
                    nm1[:, :], lgn3, AX.X, negate=True),
                    waits=[("f_pg0", em.c("f_pg0"))], inc=("ct", 1))
                em.emit("V", lambda E: nc.vector.tensor_tensor(
                    lgn3, lgn3, nm1b, ALU.add),
                    waits=[("ct", em.c("ct"))], inc=("ct", 1))
                em.emit("V", lambda E: nc.vector.tensor_scalar(
                    keep[:, :], lgn[:, :], 0.0, None, ALU.is_ge),
                    waits=[("ct", em.c("ct"))], inc=("ct", 1))
                em.emit("V", lambda E: nc.vector.scalar_tensor_tensor(
                    lmsk[:, :], keep[:, :], -1e30, lgn[:, :], ALU.mult, ALU.add),
                    waits=[("ct", em.c("ct"))], inc=("ct", 1))
                em.emit("V", lambda E: nc.vector.reduce_max(
                    nm2[:, :], lmsk3, AX.X, negate=True),
                    waits=[("ct", em.c("ct"))], inc=("ct", 1))
                em.emit("S", lambda E: nc.scalar.activation(
                    e_sb[:, :], lgn[:, :], AF.Exp),
                    waits=[("ct", em.c("ct"))], inc=("ce", 1))
                em.emit("S", lambda E: nc.scalar.activation(
                    trow[:, :], nm2[:, :], AF.Exp, scale=-1.0),
                    inc=("ce", 1))
                em.emit("V", lambda E: nc.vector.tensor_scalar(
                    inv2[:, :], trow[:, :], 0.5, 0.5, ALU.mult, ALU.add),
                    waits=[("ce", em.c("ce"))], inc=("ct", 1))
                em.emit("V", lambda E: nc.vector.reciprocal(
                    inv2[:, :], inv2[:, :]),
                    waits=[("ct", em.c("ct"))], inc=("ct", 1))
                em.emit("V", lambda E: nc.vector.tensor_tensor(
                    keep3, e3, trowb, ALU.is_ge),
                    waits=[("ct", em.c("ct"))], inc=("ct", 1))
                em.emit("V", lambda E: nc.vector.tensor_mul(
                    keep[:, :], keep[:, :], e_sb[:, :]),
                    waits=[("ct", em.c("ct"))], inc=("ct", 1))
                em.emit("V", lambda E: nc.vector.tensor_tensor(
                    wd3, keep3, inv2b, ALU.mult),
                    waits=[("ct", em.c("ct"))], inc=("cw", 16))

            def piece_wdtrans():
                for half in range(2):
                    pname = "pg%d" % half
                    aq = acquire(pname)
                    for i in range(8):
                        tc = half * 8 + i
                        w = [("cw", 16), aq] if i == 0 else ()
                        em.emit("T", lambda E, half=half, i=i, tc=tc: nc.tensor.transpose(
                            pgb[half][0:8, i * 128:(i + 1) * 128],
                            wd_sb[:, tc * 8:(tc + 1) * 8], idb[:, :]),
                            waits=w, inc=("ctr", 1))
                    em.emit("V", lambda E, half=half: nc.vector.tensor_copy(
                        wdT_sb[:, half * 1024:(half + 1) * 1024], pgb[half][0:8, 0:1024]),
                        waits=[("ctr", em.c("ctr"))], inc=("f_" + pname, 1))

            def piece_repl():
                WDT = [fcnt("pg0"), fcnt("pg1")]
                for ts in range(4):
                    psl = ts % 2
                    pname = "pav%d" % psl
                    aq = acquire(pname)
                    em.emit("T", lambda E, ts=ts, psl=psl: nc.tensor.matmul(
                        pav[psl][:, :], repl_sb[:, :], wdT_sb[:, ts * 512:(ts + 1) * 512],
                        start=True, stop=True),
                        waits=WDT + [aq] if ts == 0 else [aq], inc=("cwb", 1))
                    em.emit("V", lambda E, ts=ts, psl=psl: nc.vector.tensor_copy(
                        wdbig[:, ts * 512:(ts + 1) * 512], pav[psl][:, :]),
                        waits=[("cwb", em.c("cwb"))], inc=("f_" + pname, 1))

            def emit_vog(vog):
                sl = vog % 2
                w = ([("bmm", BMM_QK), ("bwu0", em.c("bwu0")), ("bwu1", em.c("bwu1"))]
                     if vog < 2 else [("bvu%d" % sl, vog // 2)])
                em.emit("P", lambda E, vog=vog, sl=sl: E.dma_start(
                    vwbuf[sl].rearrange("p (c n) -> p c n", n=256), wsrc(wvT, vog, 256)),
                    waits=w, inc=("bv%d" % sl, 16))
                for tc in range(16):
                    gi = vog * 16 + tc
                    pp, pname = bcb[gi % 6]
                    aq = acquire(pname)
                    w2 = [("bv%d" % sl, 16 * (vog // 2 + 1)), aq]
                    for ch in range(NCH):
                        em.emit("T", lambda E, pp=pp, sl=sl, ch=ch, tc=tc: nc.tensor.matmul(
                            pp[:, 0:256], xnT[:, ch * T + tc * 128: ch * T + tc * 128 + 128],
                            vwbuf[sl][:, ch * 256:(ch + 1) * 256],
                            start=(ch == 0), stop=False),
                            waits=w2 if ch == 0 else ())
                    grp_inc = ("bvu%d" % sl, 1) if tc == 15 else ("bmm", 1)
                    em.emit("T", lambda E, pp=pp, vog=vog: nc.tensor.matmul(
                        pp[:, 0:256], onerb[:, :], bv_sb[:, vog * 256:(vog + 1) * 256],
                        start=False, stop=True), inc=grp_inc)
                    ssl = gi % 2
                    aqs = acquire("vst%d" % ssl, 16)
                    gw = ("bvu%d" % sl, em.c("bvu%d" % sl)) if tc == 15 else ("bmm", em.c("bmm"))
                    em.emit("S", lambda E, ssl=ssl, pp=pp: nc.scalar.activation(
                        vstage[ssl][:, :], pp[:, 0:256], AF.Identity),
                        waits=[gw, aqs], inc=("f_" + pname, 1))
                    em.emit("Y", lambda E, tc=tc, vog=vog, ssl=ssl: E.dma_start(
                        Vs[tc * 128:(tc + 1) * 128, vog * 256:(vog + 1) * 256],
                        vstage[ssl][:, :]),
                        waits=[fcnt(pname)], inc=("f_vst%d" % ssl, 16))

            emit_vog(0)
            piece_rtrans()
            emit_vog(1)
            piece_wdtrans()
            emit_vog(2)
            piece_repl()
            emit_vog(3)
            em.emit("P", lambda E: E.dma_start(lA_sb[:, :], lAi[:, :]),
                    waits=[("ctg", 16)], inc=("cld", 16))
            em.emit("P", lambda E: E.dma_start(lB_sb[:, :], lBi[:, :]),
                    inc=("cld", 16))
            WDB = [fcnt("pav0"), fcnt("pav1")]
        if "C" in PHASES:
            def wsrc(dram, blk, width):
                r = dram.rearrange("(c p) n -> p c n", p=128)
                return r[:, :, blk * width:(blk + 1) * width]

            CLD = em.c("cld")


            for ts in range(4):
                pp, pname = bcb[ts % 6]
                aq = acquire(pname)
                w = [("cld", CLD), aq]
                for ch in range(NCH):
                    em.emit("T", lambda E, pp=pp, ch=ch, ts=ts: nc.tensor.matmul(
                        pp[:, :], lA_sb[:, ch * 128:(ch + 1) * 128],
                        xnT[:, ch * T + ts * 512: ch * T + ts * 512 + 512],
                        start=(ch == 0), stop=(ch == NCH - 1)),
                        waits=w if ch == 0 else (),
                        inc=("cmid", 1) if ch == NCH - 1 else None)
                w = [("cmid", em.c("cmid"))]
                if ts == 0:
                    w += WDB
                em.emit("V", lambda E, pp=pp, ts=ts: nc.vector.tensor_mul(
                    midgT[:, ts * 512:(ts + 1) * 512], pp[:, :],
                    wdbig[:, ts * 512:(ts + 1) * 512]),
                    waits=w, inc=("f_" + pname, 1))
            MGT = [fcnt("pg0"), fcnt("pg1")]
            if DEBUG:
                em.emit("Y", lambda E: E.dma_start(dmg[:, :], midgT[:, :]),
                        waits=MGT, inc=("dbg", 16))

            bld = {0: 0, 1: 0}
            for ob in range(8):
                sl = ob % 2
                w = list(MGT) if ob < 2 else [("cbu%d" % sl, ob // 2)]
                bld[sl] += 1
                em.emit("P", lambda E, ob=ob, sl=sl: E.dma_start(
                    bwbuf[sl].rearrange("p (c n) -> p c n", n=256), wsrc(bwT, ob, 256)),
                    waits=w, inc=("cb%d" % sl, 16))
                for tc in range(16):
                    gi = ob * 16 + tc
                    pp, pname = bcb[gi % 6]
                    aq = acquire(pname)
                    w = [("cb%d" % sl, 16 * bld[sl]), aq]
                    if ob < 2 and tc == 0:
                        w += list(MGT)
                    for ch in range(NCH):
                        em.emit("T", lambda E, pp=pp, sl=sl, ch=ch, tc=tc: nc.tensor.matmul(
                            pp[:, 0:256], xnT[:, ch * T + tc * 128: ch * T + tc * 128 + 128],
                            bwbuf[sl][:, ch * 256:(ch + 1) * 256],
                            start=(ch == 0), stop=False),
                            waits=w if ch == 0 else ())
                    grp_inc = ("cbu%d" % sl, 1) if tc == 15 else ("cbm", 1)
                    em.emit("T", lambda E, pp=pp, ob=ob, tc=tc: nc.tensor.matmul(
                        pp[:, 0:256], midgT[:, tc * 128:(tc + 1) * 128],
                        lB_sb[:, ob * 256:(ob + 1) * 256], start=False, stop=True),
                        inc=grp_inc)
                    aqs = acquire("gst%d" % psl, 16)
                    gw = ("cbu%d" % sl, em.c("cbu%d" % sl)) if tc == 15 else ("cbm", em.c("cbm"))
                    em.emit("S", lambda E, psl=psl, pp=pp: nc.scalar.activation(
                        gstage[psl][:, :], pp[:, 0:256], AF.Silu),
                        waits=[gw, aqs], inc=("f_" + pname, 1))
                    em.emit("Y", lambda E, tc=tc, ob=ob, psl=psl: E.dma_start(
                        gH[tc * 128:(tc + 1) * 128, ob * 256:(ob + 1) * 256],
                        gstage[psl][:, :]),
                        waits=[fcnt(pname)], inc=("f_gst%d" % psl, 16))

        if "D" in PHASES:
            CBM_W = [("cbm", em.c("cbm")), ("cbu0", em.c("cbu0")),
                     ("cbu1", em.c("cbu1"))]
            B_ST = [("f_qkst0", 16 * use.get("qkst0", 0)),
                    ("f_qkst1", 16 * use.get("qkst1", 0)),
                    ("f_vst0", 16 * use.get("vst0", 0)),
                    ("f_vst1", 16 * use.get("vst1", 0))]
            G_ST = [("f_gst0", 16 * use.get("gst0", 0)),
                    ("f_gst1", 16 * use.get("gst1", 0))]
            em.emit("V", lambda E: nc.vector.memset(vslots[:, :, 128:129], 1.0),
                    waits=CBM_W + B_ST + G_ST, inc=("dms", 1))
            vsr = Vs.rearrange("(kc p) (h d) -> p kc h d", p=128, d=128)

            kld_cnt = {}
            qld_cnt = {}

            def emit_k(qsx, i):
                ksl = i % 2
                w = []
                if kuse[ksl] > 0:
                    w.append(("dkf%d" % ksl, kuse[ksl]))
                elif qsx == 0:
                    w += [("f_qkst0", 16 * use.get("qkst0", 0)),
                          ("f_qkst1", 16 * use.get("qkst1", 0))] + CBM_W
                kuse[ksl] += 1
                km = 4 * (qsx + 1) * 128
                em.emit("Y", lambda E, i=i, ksl=ksl, km=km: E.dma_start(
                    k_slot[ksl][:, 0:km], Ks[i, :, 0:km]),
                    waits=w, inc=("dk%d" % ksl, 16))
                kld_cnt[(qsx, i)] = (ksl, em.c("dk%d" % ksl))

            def emit_q(qsx, uu):
                i, j = uu // 2, uu % 2
                qidx = j * 8 + i
                qsl = (qsx * 16 + uu) % 4
                w = []
                if quse[qsl] > 0:
                    w.append(("dqf%d" % qsl, quse[qsl]))
                elif qsx == 0:
                    w += [("f_qkst0", 16 * use.get("qkst0", 0)),
                          ("f_qkst1", 16 * use.get("qkst1", 0))] + CBM_W
                quse[qsl] += 1
                em.emit("Y", lambda E, qidx=qidx, qsl=qsl, qsx=qsx: E.dma_start(
                    q_slot[qsl][:, :], Qs[qidx, :, qsx * 512:qsx * 512 + 512]),
                    waits=w, inc=("dq%d" % qsl, 16))
                qld_cnt[(qsx, uu)] = (qsl, em.c("dq%d" % qsl))

            def emit_kq_ahead(qsx, uu):
                if uu > 15:
                    qsx, uu = qsx + 1, uu - 16
                    if qsx > 3:
                        return
                if uu % 2 == 0:
                    emit_k(qsx, uu // 2)
                emit_q(qsx, uu)

            def emit_g(qs, qc):
                w = []
                if qs > 0:
                    w.append(("dfin", 4 * (qs - 1) + qc + 1))
                elif qs == 0 and qc < 2:
                    w += G_ST
                r0 = qs * 512 + qc * 128
                em.emit("S", lambda E, qc=qc, r0=r0: E.dma_start(
                    g_slot[qc][:, :], gH[r0:r0 + 128, :]),
                    waits=w, inc=("dg%d" % qc, 16))

            avbanks = [(pav[0], "pav0"), (pav[1], "pav1"), (pg[0], "pg0"), (pg[1], "pg1")]

            def emit_avqc(cx, qc, qs, u):
                gqc = 4 * qs + qc
                pt, pname = avbanks[qc]
                aq = acquire(pname)
                w = [aq]
                if qc == 0 or cx.get("needw"):
                    w += cx["WEXP"] + [("dmk", cx["DMK"])]
                    cx["needw"] = False
                if qc == 0 and u <= 1:
                    w += [("dv%d" % qq, 64) for qq in range(qs + 1)]
                for kc in range(gqc + 1):
                    mm_inc = ("dav", 1) if kc == gqc else None
                    if kc == 0 and kc != gqc:
                        if qc == 1:
                            mm_inc = ("dqf%d" % cx["qsl"], 1)
                        elif qc == 2 and cx["j"] == 1:
                            mm_inc = ("dkf%d" % cx["ksl"], 1)
                    em.emit("T", lambda E, pt=pt, wsl=cx["wsl"], kc=kc,
                            qc=qc, i=cx["i"], gqc=gqc:
                            nc.tensor.matmul(
                                pt[:, 0:129],
                                wT[wsl][:, kc * 512 + qc * 128:
                                         kc * 512 + qc * 128 + 128],
                                vslots[:, kc * 8 + i, :],
                                start=(kc == 0), stop=(kc == gqc)),
                            waits=w if kc == 0 else (),
                            inc=mm_inc)
                DAV = em.c("dav")
                em.emit("V", lambda E, pt=pt, qc=qc: nc.vector.reciprocal(
                    recip[:, qc:qc + 1], pt[:, 128:129]),
                    waits=[("dav", DAV), ("dvc", em.c("dvc")), aq],
                    inc=("dvc", 1))
                em.emit("V", lambda E, pt=pt, qc=qc, qidx=cx["qidx"], qs=qs:
                        nc.vector.tensor_scalar(
                            s_slab[qs % 2][:, qc * 2048 + qidx * 128:
                                           qc * 2048 + qidx * 128 + 128],
                            pt[:, 0:128], recip[:, qc:qc + 1],
                            None, ALU.mult),
                        waits=[("dvc", em.c("dvc")),
                               ("dofq%d" % qc, 16 * max(0, qs - 1))],
                        inc=("f_" + pname, 1))

            kuse = {0: 0, 1: 0}
            quse = {0: 0, 1: 0, 2: 0, 3: 0}
            batch_ctr = 0
            qh_ctr = 0
            pexp = {0: 0, 1: 0}
            wav = {0: 0, 1: 0}
            xg_use = {0: 0, 1: 0}
            out_ct = 0
            def emit_vload(qsx):
                for kc in range(4 * qsx, 4 * qsx + 4):
                    w = [("dms", 1)] + B_ST if (qsx == 0 and kc == 0) else ()
                    em.emit("Y", lambda E, kc=kc: E.dma_start(
                        vslots[:, kc * 8:(kc + 1) * 8, 0:128], vsr[:, kc, :, :]),
                        waits=w, inc=("dv%d" % qsx, 16))

            for qs in range(4):
                kmax = 4 * (qs + 1)
                if qs == 0:
                    emit_vload(0)
                if qs == 0:
                    emit_k(0, 0)
                    emit_q(0, 0)
                    emit_q(0, 1)
                pend = None
                for u in range(17):
                    if u < 16:
                        i = u // 2
                        j = u % 2
                        ksl = i % 2
                        emit_kq_ahead(qs, u + 2)
                        qidx = j * 8 + i
                        qsl, dq_w = qld_cnt[(qs, u)]
                        _, dk_w = kld_cnt[(qs, i)]
                        wsl = qh_ctr % 2
                        nb = kmax // 2
                        seq = []
                        pos = {}
                        if pend is not None:
                            for qj in range(4):
                                pos.setdefault(min(qj + 1, nb - 1), []).append(qj)
                        for bi in range(nb):
                            seq.append(("b", bi))
                            for qj in pos.get(bi, ()):
                                seq.append(("q", qj))
                        first = True
                        for kind, idx in seq:
                            if kind == "b":
                                kc0 = 2 * idx
                                psl = batch_ctr % 2
                                ps = psA if psl == 0 else psB
                                psn = "ps%d" % psl
                                w = [("f_" + psn, pexp[psl])]
                                if first:
                                    w += [("dk%d" % ksl, dk_w),
                                          ("dq%d" % qsl, dq_w),
                                          ("dav", wav[wsl])]
                                    if qs == 0 and u == 0 and bi < 2:
                                        w += [fcnt("psAa"), fcnt("psAb"),
                                              fcnt("psBa"), fcnt("psBb")]
                                    first = False
                                for uu in range(2):
                                    kc = kc0 + uu
                                    em.emit("T", lambda E, ps=ps, ksl=ksl, qsl=qsl, kc=kc, uu=uu:
                                            nc.tensor.matmul(
                                                ps[:, uu * 512:(uu + 1) * 512],
                                                k_slot[ksl][:, kc * 128:(kc + 1) * 128],
                                                q_slot[qsl][:, :], start=True, stop=True),
                                            waits=w if uu == 0 else (),
                                            inc=("dsc", 1) if uu == 1 else None)
                                em.emit("S", lambda E, ps=ps, wsl=wsl, kc0=kc0:
                                        nc.scalar.activation(
                                            wT[wsl][:, kc0 * 512: kc0 * 512 + 1024],
                                            ps[:, 0:1024], AF.Exp,
                                            bias=konst[:, 0:1], scale=SCALE),
                                        waits=[("dsc", em.c("dsc"))],
                                        inc=("f_" + psn, 1))
                                pexp[psl] = em.c("f_" + psn)
                                batch_ctr += 1
                            else:
                                emit_avqc(pend, idx, qs, u)
                        WEXP = [("f_ps0", em.c("f_ps0")), ("f_ps1", em.c("f_ps1"))]
                        for dkc in range(4 * qs, kmax):
                            off = dkc * 512 + (dkc - 4 * qs) * 128
                            em.emit("V", lambda E, wsl=wsl, off=off: nc.vector.tensor_mul(
                                wT[wsl][:, off:off + 128], wT[wsl][:, off:off + 128],
                                mask_sb[:, :]),
                                waits=WEXP if dkc == 4 * qs else (),
                                inc=("dmk", 1))
                        ctx = dict(wsl=wsl, WEXP=WEXP, DMK=em.c("dmk"),
                                   qidx=qidx, i=i, j=j, ksl=ksl, qsl=qsl)
                        if u == 8 and qs < 3:
                            emit_vload(qs + 1)
                        if u in (4, 7, 10, 13):
                            emit_g(qs, (u - 4) // 3)
                        qh_ctr += 1
                    if u == 16:
                        for qc in range(4):
                            emit_avqc(pend, qc, qs, u)
                    if u >= 1:
                        wav[pend["wsl"]] = em.c("dav")
                    if u == 16:
                        pend = None
                    if u < 16:
                        pend = ctx
                SEV = [("f_pav0", em.c("f_pav0")), ("f_pav1", em.c("f_pav1")),
                       ("f_pg0", em.c("f_pg0")), ("f_pg1", em.c("f_pg1"))]
                for qc in range(4):
                    r0 = qs * 512 + qc * 128
                    w = SEV + [("dg%d" % qc, 16 * (qs + 1)),
                               ("dofq%d" % qc, 16 * qs),
                               ("dfin", em.c("dfin"))]
                    em.emit("V", lambda E, qc=qc, qs=qs: nc.vector.tensor_mul(
                        s_slab[qs % 2][:, qc * 2048:(qc + 1) * 2048],
                        s_slab[qs % 2][:, qc * 2048:(qc + 1) * 2048], g_slot[qc][:, :]),
                        waits=w, inc=("dfin", 1))
                    em.emit("P", lambda E, qc=qc, r0=r0, qs=qs: E.dma_start(
                        out[r0:r0 + 128, :], s_slab[qs % 2][:, qc * 2048:(qc + 1) * 2048],
                        accum_op=ALU.add),
                        waits=[("dfin", em.c("dfin")), ("dpre", 64)],
                        inc=("dofq%d" % qc, 16))
                    out_ct += 1
            em.emit("Y", lambda E: E.nop(),
                    waits=[("dofq%d" % qc, 64) for qc in range(4)])

        with nc.Block() as block:
            @block.tensor
            def _(E):
                em.play("T", E)

            @block.vector
            def _(E):
                em.play("V", E)

            @block.scalar
            def _(E):
                em.play("S", E)

            @block.gpsimd
            def _(E):
                em.play("P", E)

            @block.sync
            def _(E):
                em.play("Y", E)

    return nc


def _host_prep(inputs):
    x = np.asarray(inputs["L0_attn_x"])
    Wq = np.asarray(inputs["L0_attn_Wq"]); bq = np.asarray(inputs["L0_attn_bq"])
    Wk = np.asarray(inputs["L0_attn_Wk"]); bk = np.asarray(inputs["L0_attn_bk"])
    Wv = np.asarray(inputs["L0_attn_Wv"]); bv = np.asarray(inputs["L0_attn_bv"])
    bW = np.asarray(inputs["L0_ffn_base_W"])
    rW = np.asarray(inputs["L0_ffn_router_W"])
    lA = np.asarray(inputs["L0_ffn_lora_A"])
    lB = np.asarray(inputs["L0_ffn_lora_B"])

    ident = np.eye(128, dtype=ml_dtypes.bfloat16)
    identf = np.eye(128, dtype=np.float32)
    masku = np.triu(np.ones((128, 128), np.float32)).astype(ml_dtypes.bfloat16)
    repl = np.zeros((8, 128), np.float32)
    for e in range(8):
        repl[e, e * 16:(e + 1) * 16] = 1.0
    repl = repl.astype(ml_dtypes.bfloat16)
    onec = np.ones((128, 1), dtype=ml_dtypes.bfloat16)
    oner = np.ones((1, 128), dtype=np.float32)
    onerb = np.ones((1, 128), dtype=ml_dtypes.bfloat16)
    lA2 = np.ascontiguousarray(
        lA.transpose(1, 0, 2).reshape(C, 128).reshape(32, 128, 128)
        .transpose(1, 0, 2).reshape(128, C)).astype(np.float32)
    rt2 = np.ascontiguousarray(
        rW.T.reshape(32, 128, 8).transpose(1, 0, 2).reshape(128, 256)).astype(np.float32)
    lBf = lB.reshape(128, C)

    in_maps = []
    qsels = []
    for c in range(8):
        b, h = c // 2, c % 2
        qlist = [8 * h + i for i in range(8)] + [16 + 8 * h + i for i in range(8)]
        qsel = np.concatenate([np.arange(qh * 128, (qh + 1) * 128) for qh in qlist])
        kvsel = np.arange(1024 * h, 1024 * h + 1024)
        qsels.append(qsel)
        m = dict(
            xT=np.ascontiguousarray(x[b].T).astype(np.float32),
            xsel=np.ascontiguousarray(x[b][:, qsel]).astype(np.float32),
            wqT=np.ascontiguousarray(Wq[qsel, :].T).astype(np.float32),
            wkT=np.ascontiguousarray(Wk[kvsel, :].T).astype(np.float32),
            wvT=np.ascontiguousarray(Wv[kvsel, :].T).astype(np.float32),
            bqT=np.ascontiguousarray(bq[qsel].reshape(16, 128).T).astype(np.float32),
            bkT=np.ascontiguousarray(bk[kvsel].reshape(8, 128).T).astype(np.float32),
            bvr=np.ascontiguousarray(bv[kvsel].reshape(1, 1024)).astype(np.float32),
            bwT=np.ascontiguousarray(bW[qsel, :].T).astype(np.float32),
            lAi=lA2, lBi=np.ascontiguousarray(lBf[:, qsel]).astype(np.float32),
            rti=rt2, idbi=ident, idfi=identf, maski=masku, repli=repl,
            oneci=onec, oneri=oner, onerbi=onerb,
        )
        in_maps.append(m)
    return in_maps, qsels


def kernel(**inputs):
    in_maps, qsels = _host_prep(inputs)
    nc = build_program()
    res = run_bass_kernel_spmd(nc, in_maps, core_ids=list(range(8)))
    outp = np.zeros((4, T, C), dtype=np.float32)
    for c in range(8):
        b = c // 2
        o = np.asarray(res.results[c]["out"], dtype=np.float32)
        outp[b][:, qsels[c]] = o
    return outp


# version 19
# speedup vs baseline: 1.0191x; 1.0149x over previous
import os
import numpy as np
import ml_dtypes
import concourse.bass as bass
import concourse.mybir as mybir
from concourse.bass_utils import run_bass_kernel_spmd
from contextlib import ExitStack

F32 = mybir.dt.float32
BF16 = mybir.dt.bfloat16
AF = mybir.ActivationFunctionType
ALU = mybir.AluOpType
AX = mybir.AxisListType

T = 2048
C = 4096
NCH = 32
SCALE = 1.0 / float(np.sqrt(np.float32(32)))
EXP_SHIFT = -12.0
EPS = 1e-5

DEBUG = int(os.environ.get("MK_DEBUG", "0"))
PHASES = os.environ.get("MK_PHASES", "ABCD")


class Em:

    ENGS = ("T", "V", "S", "P", "Y")

    def __init__(self, nc, stack):
        self.nc = nc
        self.ops = {e: [] for e in self.ENGS}
        self.sems = {}
        self.cnt = {}
        self.stack = stack

    def sem(self, name):
        if name not in self.sems:
            self.sems[name] = self.stack.enter_context(self.nc.semaphore(name))
            self.cnt[name] = 0
        return self.sems[name]

    def c(self, name):
        self.sem(name)
        return self.cnt[name]

    def emit(self, eng, fn, waits=(), inc=None):
        wl = [(self.sem(n), v) for (n, v) in waits]
        il = None
        if inc is not None:
            n, d = inc
            self.sem(n)
            self.cnt[n] += d
            il = (self.sems[n], d)
        self.ops[eng].append((wl, fn, il))

    def barrier(self, name):
        self.sem(name)
        tgt = self.cnt[name] + 5
        for e in self.ENGS:
            self.emit(e, lambda E: E.nop(), inc=(name, 1))
        for e in self.ENGS:
            self.ops[e].append(([(self.sems[name], tgt)], None, None))

    def play(self, eng, E):
        for wl, fn, il in self.ops[eng]:
            for (h, v) in wl:
                E.wait_ge(h, v)
            if fn is not None:
                ins = fn(E)
                if il is not None:
                    ins.then_inc(*il)


def build_program():
    nc = bass.Bass("TRN2", target_bir_lowering=False)

    xT = nc.dram_tensor("xT", [C, T], F32, kind="ExternalInput")
    xsel = nc.dram_tensor("xsel", [T, 2048], F32, kind="ExternalInput")
    wqT = nc.dram_tensor("wqT", [C, 2048], F32, kind="ExternalInput")
    wkT = nc.dram_tensor("wkT", [C, 1024], F32, kind="ExternalInput")
    wvT = nc.dram_tensor("wvT", [C, 1024], F32, kind="ExternalInput")
    bqT = nc.dram_tensor("bqT", [128, 16], F32, kind="ExternalInput")
    bkT = nc.dram_tensor("bkT", [128, 8], F32, kind="ExternalInput")
    bvr = nc.dram_tensor("bvr", [1, 1024], F32, kind="ExternalInput")
    bwT = nc.dram_tensor("bwT", [C, 2048], F32, kind="ExternalInput")
    lAi = nc.dram_tensor("lAi", [128, 4096], F32, kind="ExternalInput")
    lBi = nc.dram_tensor("lBi", [128, 2048], F32, kind="ExternalInput")
    rti = nc.dram_tensor("rti", [128, 256], F32, kind="ExternalInput")
    idbi = nc.dram_tensor("idbi", [128, 128], BF16, kind="ExternalInput")
    maski = nc.dram_tensor("maski", [128, 128], BF16, kind="ExternalInput")
    repli = nc.dram_tensor("repli", [8, 128], BF16, kind="ExternalInput")
    oneci = nc.dram_tensor("oneci", [128, 1], BF16, kind="ExternalInput")
    oneri = nc.dram_tensor("oneri", [1, 128], F32, kind="ExternalInput")
    idfi = nc.dram_tensor("idfi", [128, 128], F32, kind="ExternalInput")
    onerbi = nc.dram_tensor("onerbi", [1, 128], BF16, kind="ExternalInput")

    scratch_kind = "ExternalOutput" if DEBUG else "Internal"
    Qs = nc.dram_tensor("Qs", [16, 128, T], BF16, kind=scratch_kind)
    Ks = nc.dram_tensor("Ks", [8, 128, T], BF16, kind=scratch_kind)
    Vs = nc.dram_tensor("Vs", [T, 1024], BF16, kind=scratch_kind)
    gH = nc.dram_tensor("gH", [T, 2048], F32, kind=scratch_kind)
    out = nc.dram_tensor("out", [T, 2048], F32, kind="ExternalOutput")
    if DEBUG:
        dxn = nc.dram_tensor("dxn", [128, NCH * T], BF16, kind="ExternalOutput")
        dwd = nc.dram_tensor("dwd", [128, 128], BF16, kind="ExternalOutput")
        dmg = nc.dram_tensor("dmg", [128, 2048], BF16, kind="ExternalOutput")

    with ExitStack() as stack:
        en = stack.enter_context
        arena = en(nc.sbuf_tensor("arena", [128, 65536], BF16))
        wstr = en(nc.sbuf_tensor("wstr", [128, 16384], BF16))
        vsb = en(nc.sbuf_tensor("vsb", [128, 16512], BF16))
        bq_sb = en(nc.sbuf_tensor("bq_sb", [128, 16], F32))
        bk_sb = en(nc.sbuf_tensor("bk_sb", [128, 8], F32))
        bv_sb = en(nc.sbuf_tensor("bv_sb", [1, 1024], BF16))
        rt_sb = en(nc.sbuf_tensor("rt_sb", [128, 256], BF16))
        idb = en(nc.sbuf_tensor("idb", [128, 128], BF16))
        idf = en(nc.sbuf_tensor("idf", [128, 128], F32))
        mask_sb = en(nc.sbuf_tensor("mask_sb", [128, 128], BF16))
        repl_sb = en(nc.sbuf_tensor("repl_sb", [8, 128], BF16))
        onec = en(nc.sbuf_tensor("onec", [128, 1], BF16))
        oner = en(nc.sbuf_tensor("oner", [1, 128], F32))
        onerb = en(nc.sbuf_tensor("onerb", [1, 128], BF16))
        konst = en(nc.sbuf_tensor("konst", [128, 8], F32))
        nm1 = en(nc.sbuf_tensor("nm1", [128, 16], F32))
        nm2 = en(nc.sbuf_tensor("nm2", [128, 16], F32))
        trow = en(nc.sbuf_tensor("trow", [128, 16], F32))
        inv2 = en(nc.sbuf_tensor("inv2", [128, 16], F32))
        keep = en(nc.sbuf_tensor("keep", [128, 128], F32))
        lmsk = en(nc.sbuf_tensor("lmsk", [128, 128], F32))
        recip = en(nc.sbuf_tensor("recip", [128, 4], F32))

        arena32 = arena.bitcast(F32)
        wstr32 = wstr.bitcast(F32)
        vsb32 = vsb.bitcast(F32)

        xnT = arena
        s_slab = [arena32[:, 0:8192], arena32[:, 8192:16384]]
        g_slot = [arena32[:, 16384 + 2048 * i: 16384 + 2048 * (i + 1)]
                  for i in range(4)]
        wT = [arena[:, 49152:57344], arena[:, 57344:65536]]
        q_slot = ([wstr[:, 4096 + 512 * i: 4096 + 512 * (i + 1)] for i in range(4)]
                  + [wstr[:, 10240 + 512 * i: 10240 + 512 * (i + 1)] for i in range(4)])
        k_slot = [wstr[:, 0:2048], wstr[:, 2048:4096],
                  wstr[:, 6144:8192], wstr[:, 8192:10240]]

        mrow = wstr32[0:1, 4096:6144]
        qrow = wstr32[0:1, 6144:8192]
        wbuf = [wstr[:, 0:4096], wstr[:, 4096:8192]]
        vwbuf = [wstr[:, 0:8192], wstr[:, 8192:16384]]
        bwbuf = [wstr[:, 0:8192], wstr[:, 8192:16384]]

        rrow = vsb32[0:1, 0:2048]
        lgT_sb = vsb32[0:8, 0:2048]
        rb_sb = vsb[:, 4096:6144]
        mb_sb = vsb[:, 8192:10240]
        sqb = [vsb[:, 12288:14336], vsb[:, 14336:16384]]
        lA_sb = vsb[:, 0:4096]
        lB_sb = vsb[:, 4096:6144]
        wdbig = vsb[:, 6144:8192]
        midgT = vsb[:, 8192:10240]
        lgn = vsb32[:, 5120:5248]
        e_sb = vsb32[:, 5248:5376]
        wd_sb = vsb[:, 10752:10880]
        wdT_sb = vsb[0:8, 10880:12928]
        qkstage = [vsb[:, 12928:13440], vsb[:, 13440:13952]]
        vstage = [vsb[:, 13952:14208], vsb[:, 14208:14464]]
        gstage = [vsb32[:, 7232:7488], vsb32[:, 7488:7744]]
        vslots = vsb.rearrange("p (s n) -> p s n", n=129)

        rem = nc.sbuf_bytes_remaining
        assert rem >= 0, f"SBUF over budget by {-rem} bytes"

        psA = en(nc.psum_tensor("psA", [128, 1024], F32))
        psB = en(nc.psum_tensor("psB", [128, 1024], F32))
        pav = [en(nc.psum_tensor("pav0", [128, 512], F32)),
               en(nc.psum_tensor("pav1", [128, 512], F32))]
        pg = [en(nc.psum_tensor("pg0", [128, 512], F32)),
              en(nc.psum_tensor("pg1", [128, 512], F32))]
        pgb = [pg[0].bitcast(BF16), pg[1].bitcast(BF16)]

        em = Em(nc, stack)
        use = {}

        def acquire(name, scale=1):
            u = use.get(name, 0)
            use[name] = u + 1
            return ("f_" + name, scale * u)

        def fcnt(name):
            return ("f_" + name, em.c("f_" + name))

        em.emit("V", lambda E: nc.vector.memset(konst[:, 0:1], EXP_SHIFT), inc=("kon", 1))
        em.emit("V", lambda E: nc.vector.memset(konst[:, 1:2], EPS), inc=("kon", 1))
        for dst, src in [(bq_sb, bqT), (bk_sb, bkT), (idb, idbi), (idf, idfi),
                         (mask_sb, maski),
                         (repl_sb, repli), (onec, oneci), (oner, oneri), (onerb, onerbi)]:
            em.emit("Y", lambda E, d=dst, sr=src: E.dma_start(d[:, :], sr[:, :]),
                    inc=("kload", 16))
        em.emit("P", lambda E: E.dma_start(bv_sb[:, :], bvr[:, :]), inc=("kloadp", 16))
        em.emit("P", lambda E: E.dma_start(rt_sb[:, :], rti[:, :]), inc=("kloadp", 16))
        KL = em.c("kload")
        KLP = em.c("kloadp")

        if "A" in PHASES:
            sum_ps = [psA[0:1, 0:512], psA[0:1, 512:1024],
                      psB[0:1, 0:512], psB[0:1, 512:1024]]
            sq_ps = [pav[0][0:1, 0:512], pav[1][0:1, 0:512],
                     pg[0][0:1, 0:512], pg[1][0:1, 0:512]]
            aqs0 = [acquire(n) for n in
                    ("psAa", "psAb", "psBa", "psBb", "pav0", "pav1", "pg0", "pg1")]
            for c in range(NCH):
                b = c % 2
                sname = "axp%d" % (c % 8)
                w = [] if c < 8 else [(sname, 16 * (c // 8))]
                em.emit("P", lambda E, c=c: E.dma_start(
                    xnT[:, c * T:(c + 1) * T], xT[c * 128:(c + 1) * 128, :]),
                    waits=w, inc=(sname, 16))
                w = [(sname, 16 * (c // 8 + 1))]
                if c >= 2:
                    w.append(("astat", 8 * (c - 1)))
                em.emit("S", lambda E, b=b, c=c: nc.scalar.activation(
                    sqb[b][:, :], xnT[:, c * T:(c + 1) * T], AF.Square),
                    waits=w, inc=("asq", 1))
                w = [(sname, 16 * (c // 8 + 1)), ("asq", c + 1)]
                if c == 0:
                    w += [("kload", KL), ("kloadp", KLP)] + aqs0
                for ts in range(4):
                    em.emit("T", lambda E, c=c, ts=ts: nc.tensor.matmul(
                        sum_ps[ts], onec[:, :],
                        xnT[:, c * T + ts * 512: c * T + ts * 512 + 512],
                        start=(c == 0), stop=(c == NCH - 1)),
                        waits=w if ts == 0 else (), inc=("astat", 1))
                for ts in range(4):
                    em.emit("T", lambda E, b=b, ts=ts, c=c: nc.tensor.matmul(
                        sq_ps[ts], onec[:, :], sqb[b][:, ts * 512:(ts + 1) * 512],
                        start=(c == 0), stop=(c == NCH - 1)),
                        inc=("astat", 1))
            AST = em.c("astat")

            ev_rel = ["psAa", "psAb", "psBa", "psBb"]
            for ts in range(4):
                w = [("astat", AST)] if ts == 0 else ()
                em.emit("V", lambda E, ts=ts: nc.vector.tensor_copy(
                    mrow[:, ts * 512:(ts + 1) * 512], sum_ps[ts]),
                    waits=w, inc=("f_" + ev_rel[ts], 1))
            sq_rel = ["pav0", "pav1", "pg0", "pg1"]
            for ts in range(4):
                em.emit("V", lambda E, ts=ts: nc.vector.tensor_copy(
                    qrow[:, ts * 512:(ts + 1) * 512], sq_ps[ts]),
                    inc=("f_" + sq_rel[ts], 1))
            wrows = [fcnt("psAa"), fcnt("psAb"), fcnt("psBa"), fcnt("psBb"),
                     fcnt("pav0"), fcnt("pav1"), fcnt("pg0"), fcnt("pg1")]
            em.emit("V", lambda E: nc.vector.tensor_scalar(
                mrow[:, :], mrow[:, :], 1.0 / C, None, ALU.mult),
                waits=wrows, inc=("arow", 1))
            em.emit("V", lambda E: nc.vector.tensor_scalar(
                qrow[:, :], qrow[:, :], 1.0 / C, None, ALU.mult),
                waits=[("arow", em.c("arow"))], inc=("arow", 1))
            em.emit("V", lambda E: nc.vector.tensor_mul(rrow[:, :], mrow[:, :], mrow[:, :]),
                    waits=[("arow", em.c("arow"))], inc=("arow", 1))
            em.emit("V", lambda E: nc.vector.tensor_sub(qrow[:, :], qrow[:, :], rrow[:, :]),
                    waits=[("arow", em.c("arow"))], inc=("arow", 1))
            em.emit("S", lambda E: nc.scalar.activation(
                rrow[:, :], qrow[:, :], AF.Ln, bias=konst[0:1, 1:2]),
                waits=[("arow", em.c("arow")), ("kon", 2)], inc=("asd", 1))
            em.emit("S", lambda E: nc.scalar.activation(
                rrow[:, :], rrow[:, :], AF.Exp, scale=-0.5),
                waits=[("asd", 1)], inc=("asd", 1))
            em.emit("V", lambda E: E.nop(),
                    waits=[("asd", 2)], inc=("arow", 1))
            em.emit("V", lambda E: nc.vector.scalar_tensor_tensor(
                mrow[:, :], mrow[:, :], -1.0, rrow[:, :], ALU.mult, ALU.mult),
                waits=[("arow", em.c("arow"))], inc=("arow", 1))
            ARW = em.c("arow")

            for ts in range(4):
                a0 = acquire("pg0")
                a1 = acquire("pg1")
                em.emit("T", lambda E, ts=ts: nc.tensor.matmul(
                    pg[0][:, :], oner[:, :], rrow[:, ts * 512:(ts + 1) * 512],
                    start=True, stop=True),
                    waits=[("arow", ARW), a0, a1], inc=("abc", 1))
                em.emit("T", lambda E, ts=ts: nc.tensor.matmul(
                    pg[1][:, :], oner[:, :], mrow[:, ts * 512:(ts + 1) * 512],
                    start=True, stop=True), inc=("abc", 1))
                em.emit("V", lambda E, ts=ts: nc.vector.tensor_copy(
                    rb_sb[:, ts * 512:(ts + 1) * 512], pg[0][:, :]),
                    waits=[("abc", 2 * (ts + 1))], inc=("f_pg0", 1))
                em.emit("V", lambda E, ts=ts: nc.vector.tensor_copy(
                    mb_sb[:, ts * 512:(ts + 1) * 512], pg[1][:, :]),
                    inc=("f_pg1", 1))
            WBC = [fcnt("pg0"), fcnt("pg1")]

            for c in range(NCH):
                w = [("anrm", em.c("anrm"))]
                if c == 0:
                    w += WBC
                em.emit("V", lambda E, c=c: nc.vector.tensor_mul(
                    xnT[:, c * T:(c + 1) * T], xnT[:, c * T:(c + 1) * T], rb_sb[:, :]),
                    waits=w, inc=("anrm", 1))
                em.emit("V", lambda E, c=c: nc.vector.tensor_add(
                    xnT[:, c * T:(c + 1) * T], xnT[:, c * T:(c + 1) * T], mb_sb[:, :]),
                    waits=[("anrm", em.c("anrm"))], inc=("anrm", 1))
            if DEBUG:
                em.emit("Y", lambda E: E.dma_start(dxn[:, :], xnT[:, :]),
                        waits=[("anrm", em.c("anrm"))], inc=("dbg", 16))

        if "B" in PHASES:
            def wsrc(dram, blk, width):
                r = dram.rearrange("(c p) n -> p c n", p=128)
                return r[:, :, blk * width:(blk + 1) * width]

            for i in range(4):
                em.emit("Y", lambda E, i=i: E.dma_start(
                    out[i * 512:(i + 1) * 512, :], xsel[i * 512:(i + 1) * 512, :]),
                    inc=("dpre", 16))
            bcb = [(pg[0], "pg0"), (pg[1], "pg1"),
                   (psA[:, 0:512], "psAa"), (psA[:, 512:1024], "psAb"),
                   (psB[:, 0:512], "psBa"), (psB[:, 512:1024], "psBb")]
            ANRM = em.c("anrm")
            wld = {0: 0, 1: 0}
            for og in range(24):
                sl = og % 2
                isq = og < 16
                dram = wqT if isq else wkT
                ogl = og if isq else og - 16
                w = [] if og < 2 else [("bwu%d" % sl, og // 2)]
                wld[sl] += 1
                em.emit("P", lambda E, d=dram, ogl=ogl, sl=sl: E.dma_start(
                    wbuf[sl].rearrange("p (c n) -> p c n", n=128), wsrc(d, ogl, 128)),
                    waits=w, inc=("bw%d" % sl, 16))
                for ts in range(4):
                    gi = og * 4 + ts
                    pp, pname = bcb[gi % 6]
                    aq = acquire(pname)
                    w = [("bw%d" % sl, 16 * wld[sl]), aq]
                    for ch in range(NCH):
                        last = (ch == NCH - 1)
                        grp_inc = (("bwu%d" % sl, 1) if ts == 3 else ("bmm", 1)) if last else None
                        wch = w if ch == 0 else []
                        if og == 0 and ts == 0:
                            wch = list(wch) + [("anrm", 2 * (ch + 1))]
                        em.emit("T", lambda E, pp=pp, sl=sl, ch=ch, ts=ts: nc.tensor.matmul(
                            pp[:, :], wbuf[sl][:, ch * 128:(ch + 1) * 128],
                            xnT[:, ch * T + ts * 512: ch * T + ts * 512 + 512],
                            start=(ch == 0), stop=(ch == NCH - 1)),
                            waits=wch,
                            inc=grp_inc)
                    ssl = gi % 2
                    aqs = acquire("qkst%d" % ssl, 16)
                    bias = bq_sb[:, og:og + 1] if isq else bk_sb[:, ogl:ogl + 1]
                    gw = ("bwu%d" % sl, em.c("bwu%d" % sl)) if ts == 3 else ("bmm", em.c("bmm"))
                    em.emit("S", lambda E, ssl=ssl, pp=pp, bias=bias: nc.scalar.activation(
                        qkstage[ssl][:, :], pp[:, :], AF.Identity, bias=bias),
                        waits=[gw, aqs], inc=("f_" + pname, 1))
                    dst = (Qs[og, :, ts * 512:(ts + 1) * 512] if isq
                           else Ks[ogl, :, ts * 512:(ts + 1) * 512])
                    em.emit("Y", lambda E, dst=dst, ssl=ssl: E.dma_start(
                        dst, qkstage[ssl][:, :]),
                        waits=[fcnt(pname)], inc=("f_qkst%d" % ssl, 16))
            for ts in range(4):
                psl = ts % 2
                pname = "pav%d" % psl
                aq = acquire(pname)
                for ch in range(NCH):
                    em.emit("T", lambda E, psl=psl, ch=ch, ts=ts: nc.tensor.matmul(
                        pav[psl][0:8, :], rt_sb[:, ch * 8:(ch + 1) * 8],
                        xnT[:, ch * T + ts * 512: ch * T + ts * 512 + 512],
                        start=(ch == 0), stop=(ch == NCH - 1)),
                        waits=[aq] if ch == 0 else (),
                        inc=("clg", 1) if ch == NCH - 1 else None)
                em.emit("V", lambda E, psl=psl, ts=ts: nc.vector.tensor_copy(
                    lgT_sb[:, ts * 512:(ts + 1) * 512], pav[psl][0:8, :]),
                    waits=[("clg", em.c("clg"))], inc=("f_" + pname, 1))
            LGT = [("f_pav0", em.c("f_pav0")), ("f_pav1", em.c("f_pav1"))]
            BMM_QK = em.c("bmm")

            def piece_rtrans():
                aq_tr = acquire("pg0")
                for tc in range(16):
                    w = LGT + [aq_tr] if tc == 0 else ()
                    em.emit("T", lambda E, tc=tc: nc.tensor.transpose(
                        pg[0][:, tc * 8:(tc + 1) * 8],
                        lgT_sb[:, tc * 128:(tc + 1) * 128], idf[0:8, 0:8]),
                        waits=w, inc=("ctg", 1))
                em.emit("V", lambda E: nc.vector.tensor_copy(
                    lgn[:, :], pg[0][:, 0:128]),
                    waits=[("ctg", 16)], inc=("f_pg0", 1))
                def b8(ap2d, stride):
                    return bass.AP(ap2d.tensor, ap2d.offset,
                                   [[stride, 128], [1, 16], [0, 8]])
                lgn3 = lgn.rearrange("p (g n) -> p g n", n=8)
                e3 = e_sb.rearrange("p (g n) -> p g n", n=8)
                keep3 = keep.rearrange("p (g n) -> p g n", n=8)
                wd3 = wd_sb.rearrange("p (g n) -> p g n", n=8)
                lmsk3 = lmsk.rearrange("p (g n) -> p g n", n=8)
                nm1b = b8(nm1[:, :], 16)
                trowb = b8(trow[:, :], 16)
                inv2b = b8(inv2[:, :], 16)
                em.emit("V", lambda E: nc.vector.reduce_max(
                    nm1[:, :], lgn3, AX.X, negate=True),
                    waits=[("f_pg0", em.c("f_pg0"))], inc=("ct", 1))
                em.emit("V", lambda E: nc.vector.tensor_tensor(
                    lgn3, lgn3, nm1b, ALU.add),
                    waits=[("ct", em.c("ct"))], inc=("ct", 1))
                em.emit("V", lambda E: nc.vector.tensor_scalar(
                    keep[:, :], lgn[:, :], 0.0, None, ALU.is_ge),
                    waits=[("ct", em.c("ct"))], inc=("ct", 1))
                em.emit("V", lambda E: nc.vector.scalar_tensor_tensor(
                    lmsk[:, :], keep[:, :], -1e30, lgn[:, :], ALU.mult, ALU.add),
                    waits=[("ct", em.c("ct"))], inc=("ct", 1))
                em.emit("V", lambda E: nc.vector.reduce_max(
                    nm2[:, :], lmsk3, AX.X, negate=True),
                    waits=[("ct", em.c("ct"))], inc=("ct", 1))
                em.emit("S", lambda E: nc.scalar.activation(
                    e_sb[:, :], lgn[:, :], AF.Exp),
                    waits=[("ct", em.c("ct"))], inc=("ce", 1))
                em.emit("S", lambda E: nc.scalar.activation(
                    trow[:, :], nm2[:, :], AF.Exp, scale=-1.0),
                    inc=("ce", 1))
                em.emit("V", lambda E: nc.vector.tensor_scalar(
                    inv2[:, :], trow[:, :], 0.5, 0.5, ALU.mult, ALU.add),
                    waits=[("ce", em.c("ce"))], inc=("ct", 1))
                em.emit("V", lambda E: nc.vector.reciprocal(
                    inv2[:, :], inv2[:, :]),
                    waits=[("ct", em.c("ct"))], inc=("ct", 1))
                em.emit("V", lambda E: nc.vector.tensor_tensor(
                    keep3, e3, trowb, ALU.is_ge),
                    waits=[("ct", em.c("ct"))], inc=("ct", 1))
                em.emit("V", lambda E: nc.vector.tensor_mul(
                    keep[:, :], keep[:, :], e_sb[:, :]),
                    waits=[("ct", em.c("ct"))], inc=("ct", 1))
                em.emit("V", lambda E: nc.vector.tensor_tensor(
                    wd3, keep3, inv2b, ALU.mult),
                    waits=[("ct", em.c("ct"))], inc=("cw", 16))

            def piece_wdtrans():
                for half in range(2):
                    pname = "pg%d" % half
                    aq = acquire(pname)
                    for i in range(8):
                        tc = half * 8 + i
                        w = [("cw", 16), aq] if i == 0 else ()
                        em.emit("T", lambda E, half=half, i=i, tc=tc: nc.tensor.transpose(
                            pgb[half][0:8, i * 128:(i + 1) * 128],
                            wd_sb[:, tc * 8:(tc + 1) * 8], idb[:, :]),
                            waits=w, inc=("ctr", 1))
                    em.emit("V", lambda E, half=half: nc.vector.tensor_copy(
                        wdT_sb[:, half * 1024:(half + 1) * 1024], pgb[half][0:8, 0:1024]),
                        waits=[("ctr", em.c("ctr"))], inc=("f_" + pname, 1))

            def piece_repl():
                WDT = [fcnt("pg0"), fcnt("pg1")]
                for ts in range(4):
                    psl = ts % 2
                    pname = "pav%d" % psl
                    aq = acquire(pname)
                    em.emit("T", lambda E, ts=ts, psl=psl: nc.tensor.matmul(
                        pav[psl][:, :], repl_sb[:, :], wdT_sb[:, ts * 512:(ts + 1) * 512],
                        start=True, stop=True),
                        waits=WDT + [aq] if ts == 0 else [aq], inc=("cwb", 1))
                    em.emit("V", lambda E, ts=ts, psl=psl: nc.vector.tensor_copy(
                        wdbig[:, ts * 512:(ts + 1) * 512], pav[psl][:, :]),
                        waits=[("cwb", em.c("cwb"))], inc=("f_" + pname, 1))

            def emit_vog(vog):
                sl = vog % 2
                w = ([("bmm", BMM_QK), ("bwu0", em.c("bwu0")), ("bwu1", em.c("bwu1"))]
                     if vog < 2 else [("bvu%d" % sl, vog // 2)])
                em.emit("P", lambda E, vog=vog, sl=sl: E.dma_start(
                    vwbuf[sl].rearrange("p (c n) -> p c n", n=256), wsrc(wvT, vog, 256)),
                    waits=w, inc=("bv%d" % sl, 16))
                for tc in range(16):
                    gi = vog * 16 + tc
                    pp, pname = bcb[gi % 6]
                    aq = acquire(pname)
                    w2 = [("bv%d" % sl, 16 * (vog // 2 + 1)), aq]
                    for ch in range(NCH):
                        em.emit("T", lambda E, pp=pp, sl=sl, ch=ch, tc=tc: nc.tensor.matmul(
                            pp[:, 0:256], xnT[:, ch * T + tc * 128: ch * T + tc * 128 + 128],
                            vwbuf[sl][:, ch * 256:(ch + 1) * 256],
                            start=(ch == 0), stop=False),
                            waits=w2 if ch == 0 else ())
                    grp_inc = ("bvu%d" % sl, 1) if tc == 15 else ("bmm", 1)
                    em.emit("T", lambda E, pp=pp, vog=vog: nc.tensor.matmul(
                        pp[:, 0:256], onerb[:, :], bv_sb[:, vog * 256:(vog + 1) * 256],
                        start=False, stop=True), inc=grp_inc)
                    ssl = gi % 2
                    aqs = acquire("vst%d" % ssl, 16)
                    gw = ("bvu%d" % sl, em.c("bvu%d" % sl)) if tc == 15 else ("bmm", em.c("bmm"))
                    em.emit("S", lambda E, ssl=ssl, pp=pp: nc.scalar.activation(
                        vstage[ssl][:, :], pp[:, 0:256], AF.Identity),
                        waits=[gw, aqs], inc=("f_" + pname, 1))
                    em.emit("Y", lambda E, tc=tc, vog=vog, ssl=ssl: E.dma_start(
                        Vs[tc * 128:(tc + 1) * 128, vog * 256:(vog + 1) * 256],
                        vstage[ssl][:, :]),
                        waits=[fcnt(pname)], inc=("f_vst%d" % ssl, 16))

            emit_vog(0)
            piece_rtrans()
            emit_vog(1)
            piece_wdtrans()
            emit_vog(2)
            piece_repl()
            emit_vog(3)
            em.emit("P", lambda E: E.dma_start(lA_sb[:, :], lAi[:, :]),
                    waits=[("ctg", 16)], inc=("cld", 16))
            em.emit("P", lambda E: E.dma_start(lB_sb[:, :], lBi[:, :]),
                    inc=("cld", 16))
            WDB = [fcnt("pav0"), fcnt("pav1")]
        if "C" in PHASES:
            def wsrc(dram, blk, width):
                r = dram.rearrange("(c p) n -> p c n", p=128)
                return r[:, :, blk * width:(blk + 1) * width]

            CLD = em.c("cld")


            for ts in range(4):
                pp, pname = bcb[ts % 6]
                aq = acquire(pname)
                w = [("cld", CLD), aq]
                for ch in range(NCH):
                    em.emit("T", lambda E, pp=pp, ch=ch, ts=ts: nc.tensor.matmul(
                        pp[:, :], lA_sb[:, ch * 128:(ch + 1) * 128],
                        xnT[:, ch * T + ts * 512: ch * T + ts * 512 + 512],
                        start=(ch == 0), stop=(ch == NCH - 1)),
                        waits=w if ch == 0 else (),
                        inc=("cmid", 1) if ch == NCH - 1 else None)
                w = [("cmid", em.c("cmid"))]
                if ts == 0:
                    w += WDB
                em.emit("V", lambda E, pp=pp, ts=ts: nc.vector.tensor_mul(
                    midgT[:, ts * 512:(ts + 1) * 512], pp[:, :],
                    wdbig[:, ts * 512:(ts + 1) * 512]),
                    waits=w, inc=("f_" + pname, 1))
            MGT = [fcnt("pg0"), fcnt("pg1")]
            if DEBUG:
                em.emit("Y", lambda E: E.dma_start(dmg[:, :], midgT[:, :]),
                        waits=MGT, inc=("dbg", 16))

            bld = {0: 0, 1: 0}
            for ob in range(8):
                sl = ob % 2
                w = list(MGT) if ob < 2 else [("cbu%d" % sl, ob // 2)]
                bld[sl] += 1
                em.emit("P", lambda E, ob=ob, sl=sl: E.dma_start(
                    bwbuf[sl].rearrange("p (c n) -> p c n", n=256), wsrc(bwT, ob, 256)),
                    waits=w, inc=("cb%d" % sl, 16))
                for tc in range(16):
                    gi = ob * 16 + tc
                    pp, pname = bcb[gi % 6]
                    aq = acquire(pname)
                    w = [("cb%d" % sl, 16 * bld[sl]), aq]
                    if ob < 2 and tc == 0:
                        w += list(MGT)
                    for ch in range(NCH):
                        em.emit("T", lambda E, pp=pp, sl=sl, ch=ch, tc=tc: nc.tensor.matmul(
                            pp[:, 0:256], xnT[:, ch * T + tc * 128: ch * T + tc * 128 + 128],
                            bwbuf[sl][:, ch * 256:(ch + 1) * 256],
                            start=(ch == 0), stop=False),
                            waits=w if ch == 0 else ())
                    grp_inc = ("cbu%d" % sl, 1) if tc == 15 else ("cbm", 1)
                    em.emit("T", lambda E, pp=pp, ob=ob, tc=tc: nc.tensor.matmul(
                        pp[:, 0:256], midgT[:, tc * 128:(tc + 1) * 128],
                        lB_sb[:, ob * 256:(ob + 1) * 256], start=False, stop=True),
                        inc=grp_inc)
                    aqs = acquire("gst%d" % psl, 16)
                    gw = ("cbu%d" % sl, em.c("cbu%d" % sl)) if tc == 15 else ("cbm", em.c("cbm"))
                    em.emit("S", lambda E, psl=psl, pp=pp: nc.scalar.activation(
                        gstage[psl][:, :], pp[:, 0:256], AF.Silu),
                        waits=[gw, aqs], inc=("f_" + pname, 1))
                    em.emit("Y", lambda E, tc=tc, ob=ob, psl=psl: E.dma_start(
                        gH[tc * 128:(tc + 1) * 128, ob * 256:(ob + 1) * 256],
                        gstage[psl][:, :]),
                        waits=[fcnt(pname)], inc=("f_gst%d" % psl, 16))

        if "D" in PHASES:
            CBM_W = [("cbm", em.c("cbm")), ("cbu0", em.c("cbu0")),
                     ("cbu1", em.c("cbu1"))]
            B_ST = [("f_qkst0", 16 * use.get("qkst0", 0)),
                    ("f_qkst1", 16 * use.get("qkst1", 0)),
                    ("f_vst0", 16 * use.get("vst0", 0)),
                    ("f_vst1", 16 * use.get("vst1", 0))]
            G_ST = [("f_gst0", 16 * use.get("gst0", 0)),
                    ("f_gst1", 16 * use.get("gst1", 0))]
            em.emit("V", lambda E: nc.vector.memset(vslots[:, :, 128:129], 1.0),
                    waits=CBM_W + B_ST + G_ST, inc=("dms", 1))
            vsr = Vs.rearrange("(kc p) (h d) -> p kc h d", p=128, d=128)

            kld_cnt = {}
            qld_cnt = {}

            def emit_k(qsx, i):
                ksl = i % 4
                w = []
                if kuse[ksl] > 0:
                    w.append(("dkf%d" % ksl, kuse[ksl]))
                elif qsx == 0:
                    w += [("f_qkst0", 16 * use.get("qkst0", 0)),
                          ("f_qkst1", 16 * use.get("qkst1", 0))] + CBM_W
                kuse[ksl] += 1
                km = 4 * (qsx + 1) * 128
                em.emit("Y", lambda E, i=i, ksl=ksl, km=km: E.dma_start(
                    k_slot[ksl][:, 0:km], Ks[i, :, 0:km]),
                    waits=w, inc=("dk%d" % ksl, 16))
                kld_cnt[(qsx, i)] = (ksl, em.c("dk%d" % ksl))

            def emit_q(qsx, uu):
                i, j = uu // 2, uu % 2
                qidx = j * 8 + i
                qsl = (qsx * 16 + uu) % 8
                w = []
                if quse[qsl] > 0:
                    w.append(("dqf%d" % qsl, quse[qsl]))
                elif qsx == 0:
                    w += [("f_qkst0", 16 * use.get("qkst0", 0)),
                          ("f_qkst1", 16 * use.get("qkst1", 0))] + CBM_W
                quse[qsl] += 1
                em.emit("Y", lambda E, qidx=qidx, qsl=qsl, qsx=qsx: E.dma_start(
                    q_slot[qsl][:, :], Qs[qidx, :, qsx * 512:qsx * 512 + 512]),
                    waits=w, inc=("dq%d" % qsl, 16))
                qld_cnt[(qsx, uu)] = (qsl, em.c("dq%d" % qsl))

            def emit_kq_ahead(qsx, uu):
                if uu > 15:
                    qsx, uu = qsx + 1, uu - 16
                    if qsx > 3:
                        return
                if uu % 2 == 0:
                    emit_k(qsx, uu // 2)
                emit_q(qsx, uu)

            def emit_g(qs, qc):
                w = []
                if qs > 0:
                    w.append(("dfin", 4 * (qs - 1) + qc + 1))
                elif qs == 0 and qc < 2:
                    w += G_ST
                r0 = qs * 512 + qc * 128
                em.emit("S", lambda E, qc=qc, r0=r0: E.dma_start(
                    g_slot[qc][:, :], gH[r0:r0 + 128, :]),
                    waits=w, inc=("dg%d" % qc, 16))

            avbanks = [(pav[0], "pav0"), (pav[1], "pav1"), (pg[0], "pg0"), (pg[1], "pg1")]

            def emit_avqc(cx, qc, qs, u):
                gqc = 4 * qs + qc
                pt, pname = avbanks[qc]
                aq = acquire(pname)
                w = [aq]
                if qc == 0 or cx.get("needw"):
                    w += cx["WEXP"] + [("dmk", cx["DMK"])]
                    cx["needw"] = False
                if qc == 0 and u <= 1:
                    w += [("dv%d" % qq, 64) for qq in range(qs + 1)]
                for kc in range(gqc + 1):
                    mm_inc = ("dav", 1) if kc == gqc else None
                    if kc == 0 and kc != gqc:
                        if qc == 1:
                            mm_inc = ("dqf%d" % cx["qsl"], 1)
                        elif qc == 2 and cx["j"] == 1:
                            mm_inc = ("dkf%d" % cx["ksl"], 1)
                    em.emit("T", lambda E, pt=pt, wsl=cx["wsl"], kc=kc,
                            qc=qc, i=cx["i"], gqc=gqc:
                            nc.tensor.matmul(
                                pt[:, 0:129],
                                wT[wsl][:, kc * 512 + qc * 128:
                                         kc * 512 + qc * 128 + 128],
                                vslots[:, kc * 8 + i, :],
                                start=(kc == 0), stop=(kc == gqc)),
                            waits=w if kc == 0 else (),
                            inc=mm_inc)
                DAV = em.c("dav")
                em.emit("V", lambda E, pt=pt, qc=qc: nc.vector.reciprocal(
                    recip[:, qc:qc + 1], pt[:, 128:129]),
                    waits=[("dav", DAV), ("dvc", em.c("dvc")), aq],
                    inc=("dvc", 1))
                em.emit("V", lambda E, pt=pt, qc=qc, qidx=cx["qidx"], qs=qs:
                        nc.vector.tensor_scalar(
                            s_slab[qs % 2][:, qc * 2048 + qidx * 128:
                                           qc * 2048 + qidx * 128 + 128],
                            pt[:, 0:128], recip[:, qc:qc + 1],
                            None, ALU.mult),
                        waits=[("dvc", em.c("dvc")),
                               ("dofq%d" % qc, 16 * max(0, qs - 1))],
                        inc=("f_" + pname, 1))

            kuse = {i: 0 for i in range(4)}
            quse = {i: 0 for i in range(8)}
            batch_ctr = 0
            qh_ctr = 0
            pexp = {0: 0, 1: 0}
            wav = {0: 0, 1: 0}
            xg_use = {0: 0, 1: 0}
            out_ct = 0
            def emit_vload(qsx):
                for kc in range(4 * qsx, 4 * qsx + 4):
                    w = [("dms", 1)] + B_ST if (qsx == 0 and kc == 0) else ()
                    em.emit("Y", lambda E, kc=kc: E.dma_start(
                        vslots[:, kc * 8:(kc + 1) * 8, 0:128], vsr[:, kc, :, :]),
                        waits=w, inc=("dv%d" % qsx, 16))

            for qs in range(4):
                kmax = 4 * (qs + 1)
                if qs == 0:
                    emit_vload(0)
                if qs == 0:
                    for _w in range(4):
                        emit_kq_ahead(0, _w)
                pend = None
                for u in range(17):
                    if u < 16:
                        i = u // 2
                        j = u % 2
                        ksl = i % 4
                        emit_kq_ahead(qs, u + 4)
                        qidx = j * 8 + i
                        qsl, dq_w = qld_cnt[(qs, u)]
                        _, dk_w = kld_cnt[(qs, i)]
                        wsl = qh_ctr % 2
                        nb = kmax // 2
                        seq = []
                        pos = {}
                        if pend is not None:
                            for qj in range(4):
                                pos.setdefault(min(qj + 1, nb - 1), []).append(qj)
                        for bi in range(nb):
                            seq.append(("b", bi))
                            for qj in pos.get(bi, ()):
                                seq.append(("q", qj))
                        first = True
                        for kind, idx in seq:
                            if kind == "b":
                                kc0 = 2 * idx
                                psl = batch_ctr % 2
                                ps = psA if psl == 0 else psB
                                psn = "ps%d" % psl
                                w = [("f_" + psn, pexp[psl])]
                                if first:
                                    w += [("dk%d" % ksl, dk_w),
                                          ("dq%d" % qsl, dq_w),
                                          ("dav", wav[wsl])]
                                    if qs == 0 and u == 0 and bi < 2:
                                        w += [fcnt("psAa"), fcnt("psAb"),
                                              fcnt("psBa"), fcnt("psBb")]
                                    first = False
                                for uu in range(2):
                                    kc = kc0 + uu
                                    em.emit("T", lambda E, ps=ps, ksl=ksl, qsl=qsl, kc=kc, uu=uu:
                                            nc.tensor.matmul(
                                                ps[:, uu * 512:(uu + 1) * 512],
                                                k_slot[ksl][:, kc * 128:(kc + 1) * 128],
                                                q_slot[qsl][:, :], start=True, stop=True),
                                            waits=w if uu == 0 else (),
                                            inc=("dsc", 1) if uu == 1 else None)
                                em.emit("S", lambda E, ps=ps, wsl=wsl, kc0=kc0:
                                        nc.scalar.activation(
                                            wT[wsl][:, kc0 * 512: kc0 * 512 + 1024],
                                            ps[:, 0:1024], AF.Exp,
                                            bias=konst[:, 0:1], scale=SCALE),
                                        waits=[("dsc", em.c("dsc"))],
                                        inc=("f_" + psn, 1))
                                pexp[psl] = em.c("f_" + psn)
                                batch_ctr += 1
                            else:
                                emit_avqc(pend, idx, qs, u)
                        WEXP = [("f_ps0", em.c("f_ps0")), ("f_ps1", em.c("f_ps1"))]
                        for dkc in range(4 * qs, kmax):
                            off = dkc * 512 + (dkc - 4 * qs) * 128
                            em.emit("V", lambda E, wsl=wsl, off=off: nc.vector.tensor_mul(
                                wT[wsl][:, off:off + 128], wT[wsl][:, off:off + 128],
                                mask_sb[:, :]),
                                waits=WEXP if dkc == 4 * qs else (),
                                inc=("dmk", 1))
                        ctx = dict(wsl=wsl, WEXP=WEXP, DMK=em.c("dmk"),
                                   qidx=qidx, i=i, j=j, ksl=ksl, qsl=qsl)
                        if u == 8 and qs < 3:
                            emit_vload(qs + 1)
                        if u in (4, 7, 10, 13):
                            emit_g(qs, (u - 4) // 3)
                        qh_ctr += 1
                    if u == 16:
                        for qc in range(4):
                            emit_avqc(pend, qc, qs, u)
                    if u >= 1:
                        wav[pend["wsl"]] = em.c("dav")
                    if u == 16:
                        pend = None
                    if u < 16:
                        pend = ctx
                SEV = [("f_pav0", em.c("f_pav0")), ("f_pav1", em.c("f_pav1")),
                       ("f_pg0", em.c("f_pg0")), ("f_pg1", em.c("f_pg1"))]
                for qc in range(4):
                    r0 = qs * 512 + qc * 128
                    w = SEV + [("dg%d" % qc, 16 * (qs + 1)),
                               ("dofq%d" % qc, 16 * qs),
                               ("dfin", em.c("dfin"))]
                    em.emit("V", lambda E, qc=qc, qs=qs: nc.vector.tensor_mul(
                        s_slab[qs % 2][:, qc * 2048:(qc + 1) * 2048],
                        s_slab[qs % 2][:, qc * 2048:(qc + 1) * 2048], g_slot[qc][:, :]),
                        waits=w, inc=("dfin", 1))
                    em.emit("P", lambda E, qc=qc, r0=r0, qs=qs: E.dma_start(
                        out[r0:r0 + 128, :], s_slab[qs % 2][:, qc * 2048:(qc + 1) * 2048],
                        accum_op=ALU.add),
                        waits=[("dfin", em.c("dfin")), ("dpre", 64)],
                        inc=("dofq%d" % qc, 16))
                    out_ct += 1
            em.emit("Y", lambda E: E.nop(),
                    waits=[("dofq%d" % qc, 64) for qc in range(4)])

        with nc.Block() as block:
            @block.tensor
            def _(E):
                em.play("T", E)

            @block.vector
            def _(E):
                em.play("V", E)

            @block.scalar
            def _(E):
                em.play("S", E)

            @block.gpsimd
            def _(E):
                em.play("P", E)

            @block.sync
            def _(E):
                em.play("Y", E)

    return nc


def _host_prep(inputs):
    x = np.asarray(inputs["L0_attn_x"])
    Wq = np.asarray(inputs["L0_attn_Wq"]); bq = np.asarray(inputs["L0_attn_bq"])
    Wk = np.asarray(inputs["L0_attn_Wk"]); bk = np.asarray(inputs["L0_attn_bk"])
    Wv = np.asarray(inputs["L0_attn_Wv"]); bv = np.asarray(inputs["L0_attn_bv"])
    bW = np.asarray(inputs["L0_ffn_base_W"])
    rW = np.asarray(inputs["L0_ffn_router_W"])
    lA = np.asarray(inputs["L0_ffn_lora_A"])
    lB = np.asarray(inputs["L0_ffn_lora_B"])

    ident = np.eye(128, dtype=ml_dtypes.bfloat16)
    identf = np.eye(128, dtype=np.float32)
    masku = np.triu(np.ones((128, 128), np.float32)).astype(ml_dtypes.bfloat16)
    repl = np.zeros((8, 128), np.float32)
    for e in range(8):
        repl[e, e * 16:(e + 1) * 16] = 1.0
    repl = repl.astype(ml_dtypes.bfloat16)
    onec = np.ones((128, 1), dtype=ml_dtypes.bfloat16)
    oner = np.ones((1, 128), dtype=np.float32)
    onerb = np.ones((1, 128), dtype=ml_dtypes.bfloat16)
    lA2 = np.ascontiguousarray(
        lA.transpose(1, 0, 2).reshape(C, 128).reshape(32, 128, 128)
        .transpose(1, 0, 2).reshape(128, C)).astype(np.float32)
    rt2 = np.ascontiguousarray(
        rW.T.reshape(32, 128, 8).transpose(1, 0, 2).reshape(128, 256)).astype(np.float32)
    lBf = lB.reshape(128, C)

    in_maps = []
    qsels = []
    for c in range(8):
        b, h = c // 2, c % 2
        qlist = [8 * h + i for i in range(8)] + [16 + 8 * h + i for i in range(8)]
        qsel = np.concatenate([np.arange(qh * 128, (qh + 1) * 128) for qh in qlist])
        kvsel = np.arange(1024 * h, 1024 * h + 1024)
        qsels.append(qsel)
        m = dict(
            xT=np.ascontiguousarray(x[b].T).astype(np.float32),
            xsel=np.ascontiguousarray(x[b][:, qsel]).astype(np.float32),
            wqT=np.ascontiguousarray(Wq[qsel, :].T).astype(np.float32),
            wkT=np.ascontiguousarray(Wk[kvsel, :].T).astype(np.float32),
            wvT=np.ascontiguousarray(Wv[kvsel, :].T).astype(np.float32),
            bqT=np.ascontiguousarray(bq[qsel].reshape(16, 128).T).astype(np.float32),
            bkT=np.ascontiguousarray(bk[kvsel].reshape(8, 128).T).astype(np.float32),
            bvr=np.ascontiguousarray(bv[kvsel].reshape(1, 1024)).astype(np.float32),
            bwT=np.ascontiguousarray(bW[qsel, :].T).astype(np.float32),
            lAi=lA2, lBi=np.ascontiguousarray(lBf[:, qsel]).astype(np.float32),
            rti=rt2, idbi=ident, idfi=identf, maski=masku, repli=repl,
            oneci=onec, oneri=oner, onerbi=onerb,
        )
        in_maps.append(m)
    return in_maps, qsels


def kernel(**inputs):
    in_maps, qsels = _host_prep(inputs)
    nc = build_program()
    res = run_bass_kernel_spmd(nc, in_maps, core_ids=list(range(8)))
    outp = np.zeros((4, T, C), dtype=np.float32)
    for c in range(8):
        b = c // 2
        o = np.asarray(res.results[c]["out"], dtype=np.float32)
        outp[b][:, qsels[c]] = o
    return outp


# version 20
# speedup vs baseline: 1.0344x; 1.0150x over previous
import os
import numpy as np
import ml_dtypes
import concourse.bass as bass
import concourse.mybir as mybir
from concourse.bass_utils import run_bass_kernel_spmd
from contextlib import ExitStack

F32 = mybir.dt.float32
BF16 = mybir.dt.bfloat16
AF = mybir.ActivationFunctionType
ALU = mybir.AluOpType
AX = mybir.AxisListType

T = 2048
C = 4096
NCH = 32
SCALE = 1.0 / float(np.sqrt(np.float32(32)))
EXP_SHIFT = -12.0
EPS = 1e-5

DEBUG = int(os.environ.get("MK_DEBUG", "0"))
PHASES = os.environ.get("MK_PHASES", "ABCD")


class Em:

    ENGS = ("T", "V", "S", "P", "Y")

    def __init__(self, nc, stack):
        self.nc = nc
        self.ops = {e: [] for e in self.ENGS}
        self.sems = {}
        self.cnt = {}
        self.stack = stack

    def sem(self, name):
        if name not in self.sems:
            self.sems[name] = self.stack.enter_context(self.nc.semaphore(name))
            self.cnt[name] = 0
        return self.sems[name]

    def c(self, name):
        self.sem(name)
        return self.cnt[name]

    def emit(self, eng, fn, waits=(), inc=None):
        wl = [(self.sem(n), v) for (n, v) in waits]
        il = None
        if inc is not None:
            n, d = inc
            self.sem(n)
            self.cnt[n] += d
            il = (self.sems[n], d)
        self.ops[eng].append((wl, fn, il))

    def barrier(self, name):
        self.sem(name)
        tgt = self.cnt[name] + 5
        for e in self.ENGS:
            self.emit(e, lambda E: E.nop(), inc=(name, 1))
        for e in self.ENGS:
            self.ops[e].append(([(self.sems[name], tgt)], None, None))

    def play(self, eng, E):
        for wl, fn, il in self.ops[eng]:
            for (h, v) in wl:
                E.wait_ge(h, v)
            if fn is not None:
                ins = fn(E)
                if il is not None:
                    ins.then_inc(*il)


def build_program():
    nc = bass.Bass("TRN2", target_bir_lowering=False)

    xT = nc.dram_tensor("xT", [C, T], F32, kind="ExternalInput")
    xsel = nc.dram_tensor("xsel", [T, 2048], F32, kind="ExternalInput")
    wqT = nc.dram_tensor("wqT", [C, 2048], F32, kind="ExternalInput")
    wkT = nc.dram_tensor("wkT", [C, 1024], F32, kind="ExternalInput")
    wvT = nc.dram_tensor("wvT", [C, 1024], F32, kind="ExternalInput")
    bqT = nc.dram_tensor("bqT", [128, 16], F32, kind="ExternalInput")
    bkT = nc.dram_tensor("bkT", [128, 8], F32, kind="ExternalInput")
    bvr = nc.dram_tensor("bvr", [1, 1024], F32, kind="ExternalInput")
    bwT = nc.dram_tensor("bwT", [C, 2048], F32, kind="ExternalInput")
    lAi = nc.dram_tensor("lAi", [128, 4096], F32, kind="ExternalInput")
    lBi = nc.dram_tensor("lBi", [128, 2048], F32, kind="ExternalInput")
    rti = nc.dram_tensor("rti", [128, 256], F32, kind="ExternalInput")
    idbi = nc.dram_tensor("idbi", [128, 128], BF16, kind="ExternalInput")
    maski = nc.dram_tensor("maski", [128, 128], BF16, kind="ExternalInput")
    repli = nc.dram_tensor("repli", [8, 128], BF16, kind="ExternalInput")
    oneci = nc.dram_tensor("oneci", [128, 1], BF16, kind="ExternalInput")
    oneri = nc.dram_tensor("oneri", [1, 128], F32, kind="ExternalInput")
    idfi = nc.dram_tensor("idfi", [128, 128], F32, kind="ExternalInput")
    onerbi = nc.dram_tensor("onerbi", [1, 128], BF16, kind="ExternalInput")

    scratch_kind = "ExternalOutput" if DEBUG else "Internal"
    Qs = nc.dram_tensor("Qs", [16, 128, T], BF16, kind=scratch_kind)
    Ks = nc.dram_tensor("Ks", [8, 128, T], BF16, kind=scratch_kind)
    Vs = nc.dram_tensor("Vs", [T, 1024], BF16, kind=scratch_kind)
    gH = nc.dram_tensor("gH", [T, 2048], F32, kind=scratch_kind)
    out = nc.dram_tensor("out", [T, 2048], F32, kind="ExternalOutput")
    if DEBUG:
        dxn = nc.dram_tensor("dxn", [128, NCH * T], BF16, kind="ExternalOutput")
        dwd = nc.dram_tensor("dwd", [128, 128], BF16, kind="ExternalOutput")
        dmg = nc.dram_tensor("dmg", [128, 2048], BF16, kind="ExternalOutput")

    with ExitStack() as stack:
        en = stack.enter_context
        arena = en(nc.sbuf_tensor("arena", [128, 65536], BF16))
        wstr = en(nc.sbuf_tensor("wstr", [128, 16384], BF16))
        vsb = en(nc.sbuf_tensor("vsb", [128, 16512], BF16))
        bq_sb = en(nc.sbuf_tensor("bq_sb", [128, 16], F32))
        bk_sb = en(nc.sbuf_tensor("bk_sb", [128, 8], F32))
        bv_sb = en(nc.sbuf_tensor("bv_sb", [1, 1024], BF16))
        rt_sb = en(nc.sbuf_tensor("rt_sb", [128, 256], BF16))
        idb = en(nc.sbuf_tensor("idb", [128, 128], BF16))
        idf = en(nc.sbuf_tensor("idf", [128, 128], F32))
        mask_sb = en(nc.sbuf_tensor("mask_sb", [128, 128], BF16))
        repl_sb = en(nc.sbuf_tensor("repl_sb", [8, 128], BF16))
        onec = en(nc.sbuf_tensor("onec", [128, 1], BF16))
        oner = en(nc.sbuf_tensor("oner", [1, 128], F32))
        onerb = en(nc.sbuf_tensor("onerb", [1, 128], BF16))
        konst = en(nc.sbuf_tensor("konst", [128, 8], F32))
        nm1 = en(nc.sbuf_tensor("nm1", [128, 16], F32))
        nm2 = en(nc.sbuf_tensor("nm2", [128, 16], F32))
        trow = en(nc.sbuf_tensor("trow", [128, 16], F32))
        inv2 = en(nc.sbuf_tensor("inv2", [128, 16], F32))
        keep = en(nc.sbuf_tensor("keep", [128, 128], F32))
        lmsk = en(nc.sbuf_tensor("lmsk", [128, 128], F32))
        recip = en(nc.sbuf_tensor("recip", [128, 4], F32))

        arena32 = arena.bitcast(F32)
        wstr32 = wstr.bitcast(F32)
        vsb32 = vsb.bitcast(F32)

        xnT = arena
        s_slab = [arena32[:, 0:8192], arena32[:, 8192:16384]]
        g_slot = [arena32[:, 16384 + 2048 * i: 16384 + 2048 * (i + 1)]
                  for i in range(4)]
        wT = [arena[:, 49152:57344], arena[:, 57344:65536]]
        q_slot = ([wstr[:, 4096 + 512 * i: 4096 + 512 * (i + 1)] for i in range(4)]
                  + [wstr[:, 10240 + 512 * i: 10240 + 512 * (i + 1)] for i in range(4)])
        k_slot = [wstr[:, 0:2048], wstr[:, 2048:4096],
                  wstr[:, 6144:8192], wstr[:, 8192:10240]]

        mrow = wstr32[0:1, 4096:6144]
        qrow = wstr32[0:1, 6144:8192]
        wbuf = [wstr[:, 0:4096], wstr[:, 4096:8192]]
        vwbuf = [wstr[:, 0:8192], wstr[:, 8192:16384]]
        bwbuf = [wstr[:, 0:8192], wstr[:, 8192:16384]]

        rrow = vsb32[0:1, 0:2048]
        lgT_sb = vsb32[0:8, 0:2048]
        rb_sb = vsb[:, 4096:6144]
        mb_sb = vsb[:, 8192:10240]
        sqb = [vsb[:, 12288:14336], vsb[:, 14336:16384]]
        lA_sb = vsb[:, 0:4096]
        lB_sb = vsb[:, 4096:6144]
        wdbig = vsb[:, 6144:8192]
        midgT = vsb[:, 8192:10240]
        lgn = vsb32[:, 5120:5248]
        e_sb = vsb32[:, 5248:5376]
        wd_sb = vsb[:, 10752:10880]
        wdT_sb = vsb[0:8, 10880:12928]
        qkstage = [vsb[:, 12928:13440], vsb[:, 13440:13952]]
        vstage = [vsb[:, 13952:14208], vsb[:, 14208:14464]]
        gstage = [vsb32[:, 7232:7488], vsb32[:, 7488:7744]]
        vslots = vsb.rearrange("p (s n) -> p s n", n=129)

        rem = nc.sbuf_bytes_remaining
        assert rem >= 0, f"SBUF over budget by {-rem} bytes"

        psA = en(nc.psum_tensor("psA", [128, 1024], F32))
        psB = en(nc.psum_tensor("psB", [128, 1024], F32))
        pav = [en(nc.psum_tensor("pav0", [128, 512], F32)),
               en(nc.psum_tensor("pav1", [128, 512], F32))]
        pg = [en(nc.psum_tensor("pg0", [128, 512], F32)),
              en(nc.psum_tensor("pg1", [128, 512], F32))]
        pgb = [pg[0].bitcast(BF16), pg[1].bitcast(BF16)]

        em = Em(nc, stack)
        use = {}

        def acquire(name, scale=1):
            u = use.get(name, 0)
            use[name] = u + 1
            return ("f_" + name, scale * u)

        def fcnt(name):
            return ("f_" + name, em.c("f_" + name))

        em.emit("V", lambda E: nc.vector.memset(konst[:, 0:1], EXP_SHIFT), inc=("kon", 1))
        em.emit("V", lambda E: nc.vector.memset(konst[:, 1:2], EPS), inc=("kon", 1))
        for dst, src in [(bq_sb, bqT), (bk_sb, bkT), (idb, idbi), (idf, idfi),
                         (mask_sb, maski),
                         (repl_sb, repli), (onec, oneci), (oner, oneri), (onerb, onerbi)]:
            em.emit("Y", lambda E, d=dst, sr=src: E.dma_start(d[:, :], sr[:, :]),
                    inc=("kload", 16))
        em.emit("P", lambda E: E.dma_start(bv_sb[:, :], bvr[:, :]), inc=("kloadp", 16))
        em.emit("P", lambda E: E.dma_start(rt_sb[:, :], rti[:, :]), inc=("kloadp", 16))
        KL = em.c("kload")
        KLP = em.c("kloadp")

        if "A" in PHASES:
            sum_ps = [psA[0:1, 0:512], psA[0:1, 512:1024],
                      psB[0:1, 0:512], psB[0:1, 512:1024]]
            sq_ps = [pav[0][0:1, 0:512], pav[1][0:1, 0:512],
                     pg[0][0:1, 0:512], pg[1][0:1, 0:512]]
            aqs0 = [acquire(n) for n in
                    ("psAa", "psAb", "psBa", "psBb", "pav0", "pav1", "pg0", "pg1")]
            for c in range(NCH):
                b = c % 2
                sname = "axp%d" % (c % 8)
                w = [] if c < 8 else [(sname, 16 * (c // 8))]
                em.emit("P", lambda E, c=c: E.dma_start(
                    xnT[:, c * T:(c + 1) * T], xT[c * 128:(c + 1) * 128, :]),
                    waits=w, inc=(sname, 16))
                w = [(sname, 16 * (c // 8 + 1))]
                if c >= 2:
                    w.append(("astat", 8 * (c - 1)))
                em.emit("S", lambda E, b=b, c=c: nc.scalar.activation(
                    sqb[b][:, :], xnT[:, c * T:(c + 1) * T], AF.Square),
                    waits=w, inc=("asq", 1))
                w = [(sname, 16 * (c // 8 + 1)), ("asq", c + 1)]
                if c == 0:
                    w += [("kload", KL), ("kloadp", KLP)] + aqs0
                for ts in range(4):
                    em.emit("T", lambda E, c=c, ts=ts: nc.tensor.matmul(
                        sum_ps[ts], onec[:, :],
                        xnT[:, c * T + ts * 512: c * T + ts * 512 + 512],
                        start=(c == 0), stop=(c == NCH - 1)),
                        waits=w if ts == 0 else (), inc=("astat", 1))
                for ts in range(4):
                    em.emit("T", lambda E, b=b, ts=ts, c=c: nc.tensor.matmul(
                        sq_ps[ts], onec[:, :], sqb[b][:, ts * 512:(ts + 1) * 512],
                        start=(c == 0), stop=(c == NCH - 1)),
                        inc=("astat", 1))
            AST = em.c("astat")

            ev_rel = ["psAa", "psAb", "psBa", "psBb"]
            for ts in range(4):
                w = [("astat", AST)] if ts == 0 else ()
                em.emit("V", lambda E, ts=ts: nc.vector.tensor_copy(
                    mrow[:, ts * 512:(ts + 1) * 512], sum_ps[ts]),
                    waits=w, inc=("f_" + ev_rel[ts], 1))
            sq_rel = ["pav0", "pav1", "pg0", "pg1"]
            for ts in range(4):
                em.emit("V", lambda E, ts=ts: nc.vector.tensor_copy(
                    qrow[:, ts * 512:(ts + 1) * 512], sq_ps[ts]),
                    inc=("f_" + sq_rel[ts], 1))
            wrows = [fcnt("psAa"), fcnt("psAb"), fcnt("psBa"), fcnt("psBb"),
                     fcnt("pav0"), fcnt("pav1"), fcnt("pg0"), fcnt("pg1")]
            em.emit("V", lambda E: nc.vector.tensor_scalar(
                mrow[:, :], mrow[:, :], 1.0 / C, None, ALU.mult),
                waits=wrows, inc=("arow", 1))
            em.emit("V", lambda E: nc.vector.tensor_scalar(
                qrow[:, :], qrow[:, :], 1.0 / C, None, ALU.mult),
                waits=[("arow", em.c("arow"))], inc=("arow", 1))
            em.emit("V", lambda E: nc.vector.tensor_mul(rrow[:, :], mrow[:, :], mrow[:, :]),
                    waits=[("arow", em.c("arow"))], inc=("arow", 1))
            em.emit("V", lambda E: nc.vector.tensor_sub(qrow[:, :], qrow[:, :], rrow[:, :]),
                    waits=[("arow", em.c("arow"))], inc=("arow", 1))
            em.emit("S", lambda E: nc.scalar.activation(
                rrow[:, :], qrow[:, :], AF.Ln, bias=konst[0:1, 1:2]),
                waits=[("arow", em.c("arow")), ("kon", 2)], inc=("asd", 1))
            em.emit("S", lambda E: nc.scalar.activation(
                rrow[:, :], rrow[:, :], AF.Exp, scale=-0.5),
                waits=[("asd", 1)], inc=("asd", 1))
            em.emit("V", lambda E: E.nop(),
                    waits=[("asd", 2)], inc=("arow", 1))
            em.emit("V", lambda E: nc.vector.scalar_tensor_tensor(
                mrow[:, :], mrow[:, :], -1.0, rrow[:, :], ALU.mult, ALU.mult),
                waits=[("arow", em.c("arow"))], inc=("arow", 1))
            ARW = em.c("arow")

            for ts in range(4):
                a0 = acquire("pg0")
                a1 = acquire("pg1")
                em.emit("T", lambda E, ts=ts: nc.tensor.matmul(
                    pg[0][:, :], oner[:, :], rrow[:, ts * 512:(ts + 1) * 512],
                    start=True, stop=True),
                    waits=[("arow", ARW), a0, a1], inc=("abc", 1))
                em.emit("T", lambda E, ts=ts: nc.tensor.matmul(
                    pg[1][:, :], oner[:, :], mrow[:, ts * 512:(ts + 1) * 512],
                    start=True, stop=True), inc=("abc", 1))
                em.emit("V", lambda E, ts=ts: nc.vector.tensor_copy(
                    rb_sb[:, ts * 512:(ts + 1) * 512], pg[0][:, :]),
                    waits=[("abc", 2 * (ts + 1))], inc=("f_pg0", 1))
                em.emit("V", lambda E, ts=ts: nc.vector.tensor_copy(
                    mb_sb[:, ts * 512:(ts + 1) * 512], pg[1][:, :]),
                    inc=("f_pg1", 1))
            WBC = [fcnt("pg0"), fcnt("pg1")]

            for c in range(NCH):
                w = [("anrm", em.c("anrm"))]
                if c == 0:
                    w += WBC
                em.emit("V", lambda E, c=c: nc.vector.tensor_mul(
                    xnT[:, c * T:(c + 1) * T], xnT[:, c * T:(c + 1) * T], rb_sb[:, :]),
                    waits=w, inc=("anrm", 1))
                em.emit("V", lambda E, c=c: nc.vector.tensor_add(
                    xnT[:, c * T:(c + 1) * T], xnT[:, c * T:(c + 1) * T], mb_sb[:, :]),
                    waits=[("anrm", em.c("anrm"))], inc=("anrm", 1))
            if DEBUG:
                em.emit("Y", lambda E: E.dma_start(dxn[:, :], xnT[:, :]),
                        waits=[("anrm", em.c("anrm"))], inc=("dbg", 16))

        if "B" in PHASES:
            def wsrc(dram, blk, width):
                r = dram.rearrange("(c p) n -> p c n", p=128)
                return r[:, :, blk * width:(blk + 1) * width]

            for i in range(4):
                em.emit("Y", lambda E, i=i: E.dma_start(
                    out[i * 512:(i + 1) * 512, :], xsel[i * 512:(i + 1) * 512, :]),
                    inc=("dpre", 16))
            bcb = [(pg[0], "pg0"), (pg[1], "pg1"),
                   (psA[:, 0:512], "psAa"), (psA[:, 512:1024], "psAb"),
                   (psB[:, 0:512], "psBa"), (psB[:, 512:1024], "psBb")]
            ANRM = em.c("anrm")
            wld = {0: 0, 1: 0}
            for og in range(24):
                sl = og % 2
                isq = og < 16
                dram = wqT if isq else wkT
                ogl = og if isq else og - 16
                w = [] if og < 2 else [("bwu%d" % sl, og // 2)]
                wld[sl] += 1
                em.emit("P", lambda E, d=dram, ogl=ogl, sl=sl: E.dma_start(
                    wbuf[sl].rearrange("p (c n) -> p c n", n=128), wsrc(d, ogl, 128)),
                    waits=w, inc=("bw%d" % sl, 16))
                for ts in range(4):
                    gi = og * 4 + ts
                    pp, pname = bcb[gi % 6]
                    aq = acquire(pname)
                    w = [("bw%d" % sl, 16 * wld[sl]), aq]
                    for ch in range(NCH):
                        last = (ch == NCH - 1)
                        grp_inc = (("bwu%d" % sl, 1) if ts == 3 else ("bmm", 1)) if last else None
                        wch = w if ch == 0 else []
                        if og == 0 and ts == 0:
                            wch = list(wch) + [("anrm", 2 * (ch + 1))]
                        em.emit("T", lambda E, pp=pp, sl=sl, ch=ch, ts=ts: nc.tensor.matmul(
                            pp[:, :], wbuf[sl][:, ch * 128:(ch + 1) * 128],
                            xnT[:, ch * T + ts * 512: ch * T + ts * 512 + 512],
                            start=(ch == 0), stop=(ch == NCH - 1)),
                            waits=wch,
                            inc=grp_inc)
                    ssl = gi % 2
                    aqs = acquire("qkst%d" % ssl, 16)
                    bias = bq_sb[:, og:og + 1] if isq else bk_sb[:, ogl:ogl + 1]
                    gw = ("bwu%d" % sl, em.c("bwu%d" % sl)) if ts == 3 else ("bmm", em.c("bmm"))
                    em.emit("S", lambda E, ssl=ssl, pp=pp, bias=bias: nc.scalar.activation(
                        qkstage[ssl][:, :], pp[:, :], AF.Identity, bias=bias),
                        waits=[gw, aqs], inc=("f_" + pname, 1))
                    dst = (Qs[og, :, ts * 512:(ts + 1) * 512] if isq
                           else Ks[ogl, :, ts * 512:(ts + 1) * 512])
                    em.emit("Y", lambda E, dst=dst, ssl=ssl: E.dma_start(
                        dst, qkstage[ssl][:, :]),
                        waits=[fcnt(pname)], inc=("f_qkst%d" % ssl, 16))
            for ts in range(4):
                psl = ts % 2
                pname = "pav%d" % psl
                aq = acquire(pname)
                for ch in range(NCH):
                    em.emit("T", lambda E, psl=psl, ch=ch, ts=ts: nc.tensor.matmul(
                        pav[psl][0:8, :], rt_sb[:, ch * 8:(ch + 1) * 8],
                        xnT[:, ch * T + ts * 512: ch * T + ts * 512 + 512],
                        start=(ch == 0), stop=(ch == NCH - 1)),
                        waits=[aq] if ch == 0 else (),
                        inc=("clg", 1) if ch == NCH - 1 else None)
                em.emit("V", lambda E, psl=psl, ts=ts: nc.vector.tensor_copy(
                    lgT_sb[:, ts * 512:(ts + 1) * 512], pav[psl][0:8, :]),
                    waits=[("clg", em.c("clg"))], inc=("f_" + pname, 1))
            LGT = [("f_pav0", em.c("f_pav0")), ("f_pav1", em.c("f_pav1"))]
            BMM_QK = em.c("bmm")

            def piece_rtrans():
                aq_tr = acquire("pg0")
                for tc in range(16):
                    w = LGT + [aq_tr] if tc == 0 else ()
                    em.emit("T", lambda E, tc=tc: nc.tensor.transpose(
                        pg[0][:, tc * 8:(tc + 1) * 8],
                        lgT_sb[:, tc * 128:(tc + 1) * 128], idf[0:8, 0:8]),
                        waits=w, inc=("ctg", 1))
                em.emit("V", lambda E: nc.vector.tensor_copy(
                    lgn[:, :], pg[0][:, 0:128]),
                    waits=[("ctg", 16)], inc=("f_pg0", 1))
                def b8(ap2d, stride):
                    return bass.AP(ap2d.tensor, ap2d.offset,
                                   [[stride, 128], [1, 16], [0, 8]])
                lgn3 = lgn.rearrange("p (g n) -> p g n", n=8)
                e3 = e_sb.rearrange("p (g n) -> p g n", n=8)
                keep3 = keep.rearrange("p (g n) -> p g n", n=8)
                wd3 = wd_sb.rearrange("p (g n) -> p g n", n=8)
                lmsk3 = lmsk.rearrange("p (g n) -> p g n", n=8)
                nm1b = b8(nm1[:, :], 16)
                trowb = b8(trow[:, :], 16)
                inv2b = b8(inv2[:, :], 16)
                em.emit("V", lambda E: nc.vector.reduce_max(
                    nm1[:, :], lgn3, AX.X, negate=True),
                    waits=[("f_pg0", em.c("f_pg0"))], inc=("ct", 1))
                em.emit("V", lambda E: nc.vector.tensor_tensor(
                    lgn3, lgn3, nm1b, ALU.add),
                    waits=[("ct", em.c("ct"))], inc=("ct", 1))
                em.emit("V", lambda E: nc.vector.tensor_scalar(
                    keep[:, :], lgn[:, :], 0.0, None, ALU.is_ge),
                    waits=[("ct", em.c("ct"))], inc=("ct", 1))
                em.emit("V", lambda E: nc.vector.scalar_tensor_tensor(
                    lmsk[:, :], keep[:, :], -1e30, lgn[:, :], ALU.mult, ALU.add),
                    waits=[("ct", em.c("ct"))], inc=("ct", 1))
                em.emit("V", lambda E: nc.vector.reduce_max(
                    nm2[:, :], lmsk3, AX.X, negate=True),
                    waits=[("ct", em.c("ct"))], inc=("ct", 1))
                em.emit("S", lambda E: nc.scalar.activation(
                    e_sb[:, :], lgn[:, :], AF.Exp),
                    waits=[("ct", em.c("ct"))], inc=("ce", 1))
                em.emit("S", lambda E: nc.scalar.activation(
                    trow[:, :], nm2[:, :], AF.Exp, scale=-1.0),
                    inc=("ce", 1))
                em.emit("V", lambda E: nc.vector.tensor_scalar(
                    inv2[:, :], trow[:, :], 0.5, 0.5, ALU.mult, ALU.add),
                    waits=[("ce", em.c("ce"))], inc=("ct", 1))
                em.emit("V", lambda E: nc.vector.reciprocal(
                    inv2[:, :], inv2[:, :]),
                    waits=[("ct", em.c("ct"))], inc=("ct", 1))
                em.emit("V", lambda E: nc.vector.tensor_tensor(
                    keep3, e3, trowb, ALU.is_ge),
                    waits=[("ct", em.c("ct"))], inc=("ct", 1))
                em.emit("V", lambda E: nc.vector.tensor_mul(
                    keep[:, :], keep[:, :], e_sb[:, :]),
                    waits=[("ct", em.c("ct"))], inc=("ct", 1))
                em.emit("V", lambda E: nc.vector.tensor_tensor(
                    wd3, keep3, inv2b, ALU.mult),
                    waits=[("ct", em.c("ct"))], inc=("cw", 16))

            def piece_wdtrans():
                for half in range(2):
                    pname = "pg%d" % half
                    aq = acquire(pname)
                    for i in range(8):
                        tc = half * 8 + i
                        w = [("cw", 16), aq] if i == 0 else ()
                        em.emit("T", lambda E, half=half, i=i, tc=tc: nc.tensor.transpose(
                            pgb[half][0:8, i * 128:(i + 1) * 128],
                            wd_sb[:, tc * 8:(tc + 1) * 8], idb[:, :]),
                            waits=w, inc=("ctr", 1))
                    em.emit("V", lambda E, half=half: nc.vector.tensor_copy(
                        wdT_sb[:, half * 1024:(half + 1) * 1024], pgb[half][0:8, 0:1024]),
                        waits=[("ctr", em.c("ctr"))], inc=("f_" + pname, 1))

            def piece_repl():
                WDT = [fcnt("pg0"), fcnt("pg1")]
                for ts in range(4):
                    psl = ts % 2
                    pname = "pav%d" % psl
                    aq = acquire(pname)
                    em.emit("T", lambda E, ts=ts, psl=psl: nc.tensor.matmul(
                        pav[psl][:, :], repl_sb[:, :], wdT_sb[:, ts * 512:(ts + 1) * 512],
                        start=True, stop=True),
                        waits=WDT + [aq] if ts == 0 else [aq], inc=("cwb", 1))
                    em.emit("V", lambda E, ts=ts, psl=psl: nc.vector.tensor_copy(
                        wdbig[:, ts * 512:(ts + 1) * 512], pav[psl][:, :]),
                        waits=[("cwb", em.c("cwb"))], inc=("f_" + pname, 1))

            def emit_vog(vog):
                sl = vog % 2
                w = ([("bmm", BMM_QK), ("bwu0", em.c("bwu0")), ("bwu1", em.c("bwu1"))]
                     if vog < 2 else [("bvu%d" % sl, vog // 2)])
                em.emit("P", lambda E, vog=vog, sl=sl: E.dma_start(
                    vwbuf[sl].rearrange("p (c n) -> p c n", n=256), wsrc(wvT, vog, 256)),
                    waits=w, inc=("bv%d" % sl, 16))
                for tc in range(16):
                    gi = vog * 16 + tc
                    pp, pname = bcb[gi % 6]
                    aq = acquire(pname)
                    w2 = [("bv%d" % sl, 16 * (vog // 2 + 1)), aq]
                    for ch in range(NCH):
                        em.emit("T", lambda E, pp=pp, sl=sl, ch=ch, tc=tc: nc.tensor.matmul(
                            pp[:, 0:256], xnT[:, ch * T + tc * 128: ch * T + tc * 128 + 128],
                            vwbuf[sl][:, ch * 256:(ch + 1) * 256],
                            start=(ch == 0), stop=False),
                            waits=w2 if ch == 0 else ())
                    grp_inc = ("bvu%d" % sl, 1) if tc == 15 else ("bmm", 1)
                    em.emit("T", lambda E, pp=pp, vog=vog: nc.tensor.matmul(
                        pp[:, 0:256], onerb[:, :], bv_sb[:, vog * 256:(vog + 1) * 256],
                        start=False, stop=True), inc=grp_inc)
                    ssl = gi % 2
                    aqs = acquire("vst%d" % ssl, 16)
                    gw = ("bvu%d" % sl, em.c("bvu%d" % sl)) if tc == 15 else ("bmm", em.c("bmm"))
                    em.emit("S", lambda E, ssl=ssl, pp=pp: nc.scalar.activation(
                        vstage[ssl][:, :], pp[:, 0:256], AF.Identity),
                        waits=[gw, aqs], inc=("f_" + pname, 1))
                    em.emit("Y", lambda E, tc=tc, vog=vog, ssl=ssl: E.dma_start(
                        Vs[tc * 128:(tc + 1) * 128, vog * 256:(vog + 1) * 256],
                        vstage[ssl][:, :]),
                        waits=[fcnt(pname)], inc=("f_vst%d" % ssl, 16))

            emit_vog(0)
            piece_rtrans()
            emit_vog(1)
            piece_wdtrans()
            emit_vog(2)
            piece_repl()
            emit_vog(3)
            em.emit("P", lambda E: E.dma_start(lA_sb[:, :], lAi[:, :]),
                    waits=[("ctg", 16)], inc=("cld", 16))
            em.emit("P", lambda E: E.dma_start(lB_sb[:, :], lBi[:, :]),
                    inc=("cld", 16))
            WDB = [fcnt("pav0"), fcnt("pav1")]
        if "C" in PHASES:
            def wsrc(dram, blk, width):
                r = dram.rearrange("(c p) n -> p c n", p=128)
                return r[:, :, blk * width:(blk + 1) * width]

            CLD = em.c("cld")


            for ts in range(4):
                pp, pname = bcb[ts % 6]
                aq = acquire(pname)
                w = [("cld", CLD), aq]
                for ch in range(NCH):
                    em.emit("T", lambda E, pp=pp, ch=ch, ts=ts: nc.tensor.matmul(
                        pp[:, :], lA_sb[:, ch * 128:(ch + 1) * 128],
                        xnT[:, ch * T + ts * 512: ch * T + ts * 512 + 512],
                        start=(ch == 0), stop=(ch == NCH - 1)),
                        waits=w if ch == 0 else (),
                        inc=("cmid", 1) if ch == NCH - 1 else None)
                w = [("cmid", em.c("cmid"))]
                if ts == 0:
                    w += WDB
                em.emit("V", lambda E, pp=pp, ts=ts: nc.vector.tensor_mul(
                    midgT[:, ts * 512:(ts + 1) * 512], pp[:, :],
                    wdbig[:, ts * 512:(ts + 1) * 512]),
                    waits=w, inc=("f_" + pname, 1))
            MGT = [fcnt("pg0"), fcnt("pg1")]
            if DEBUG:
                em.emit("Y", lambda E: E.dma_start(dmg[:, :], midgT[:, :]),
                        waits=MGT, inc=("dbg", 16))

            bld = {0: 0, 1: 0}
            for ob in range(8):
                sl = ob % 2
                w = list(MGT) if ob < 2 else [("cbu%d" % sl, ob // 2)]
                bld[sl] += 1
                em.emit("P", lambda E, ob=ob, sl=sl: E.dma_start(
                    bwbuf[sl].rearrange("p (c n) -> p c n", n=256), wsrc(bwT, ob, 256)),
                    waits=w, inc=("cb%d" % sl, 16))
                for tc in range(16):
                    gi = ob * 16 + tc
                    pp, pname = bcb[gi % 6]
                    aq = acquire(pname)
                    w = [("cb%d" % sl, 16 * bld[sl]), aq]
                    if ob < 2 and tc == 0:
                        w += list(MGT)
                    for ch in range(NCH):
                        em.emit("T", lambda E, pp=pp, sl=sl, ch=ch, tc=tc: nc.tensor.matmul(
                            pp[:, 0:256], xnT[:, ch * T + tc * 128: ch * T + tc * 128 + 128],
                            bwbuf[sl][:, ch * 256:(ch + 1) * 256],
                            start=(ch == 0), stop=False),
                            waits=w if ch == 0 else ())
                    grp_inc = ("cbu%d" % sl, 1) if tc == 15 else ("cbm", 1)
                    em.emit("T", lambda E, pp=pp, ob=ob, tc=tc: nc.tensor.matmul(
                        pp[:, 0:256], midgT[:, tc * 128:(tc + 1) * 128],
                        lB_sb[:, ob * 256:(ob + 1) * 256], start=False, stop=True),
                        inc=grp_inc)
                    aqs = acquire("gst%d" % psl, 16)
                    gw = ("cbu%d" % sl, em.c("cbu%d" % sl)) if tc == 15 else ("cbm", em.c("cbm"))
                    em.emit("S", lambda E, psl=psl, pp=pp: nc.scalar.activation(
                        gstage[psl][:, :], pp[:, 0:256], AF.Silu),
                        waits=[gw, aqs], inc=("f_" + pname, 1))
                    em.emit("Y", lambda E, tc=tc, ob=ob, psl=psl: E.dma_start(
                        gH[tc * 128:(tc + 1) * 128, ob * 256:(ob + 1) * 256],
                        gstage[psl][:, :]),
                        waits=[fcnt(pname)], inc=("f_gst%d" % psl, 16))

        if "D" in PHASES:
            em.emit("S", lambda E: nc.scalar.activation(
                trow[:, 0:1], konst[:, 0:1], AF.Exp,
                bias=konst[:, 0:1], scale=1.0))
            CBM_W = [("cbm", em.c("cbm")), ("cbu0", em.c("cbu0")),
                     ("cbu1", em.c("cbu1"))]
            B_ST = [("f_qkst0", 16 * use.get("qkst0", 0)),
                    ("f_qkst1", 16 * use.get("qkst1", 0)),
                    ("f_vst0", 16 * use.get("vst0", 0)),
                    ("f_vst1", 16 * use.get("vst1", 0))]
            G_ST = [("f_gst0", 16 * use.get("gst0", 0)),
                    ("f_gst1", 16 * use.get("gst1", 0))]
            em.emit("V", lambda E: nc.vector.memset(vslots[:, :, 128:129], 1.0),
                    waits=CBM_W + B_ST + G_ST, inc=("dms", 1))
            vsr = Vs.rearrange("(kc p) (h d) -> p kc h d", p=128, d=128)

            kld_cnt = {}
            qld_cnt = {}

            def emit_k(qsx, i):
                ksl = i % 4
                w = []
                if kuse[ksl] > 0:
                    w.append(("dkf%d" % ksl, kuse[ksl]))
                elif qsx == 0:
                    w += [("f_qkst0", 16 * use.get("qkst0", 0)),
                          ("f_qkst1", 16 * use.get("qkst1", 0))] + CBM_W
                kuse[ksl] += 1
                km = 4 * (qsx + 1) * 128
                em.emit("Y", lambda E, i=i, ksl=ksl, km=km: E.dma_start(
                    k_slot[ksl][:, 0:km], Ks[i, :, 0:km]),
                    waits=w, inc=("dk%d" % ksl, 16))
                kld_cnt[(qsx, i)] = (ksl, em.c("dk%d" % ksl))

            def emit_q(qsx, uu):
                i, j = uu // 2, uu % 2
                qidx = j * 8 + i
                qsl = (qsx * 16 + uu) % 8
                w = []
                if quse[qsl] > 0:
                    w.append(("dqf%d" % qsl, quse[qsl]))
                elif qsx == 0:
                    w += [("f_qkst0", 16 * use.get("qkst0", 0)),
                          ("f_qkst1", 16 * use.get("qkst1", 0))] + CBM_W
                quse[qsl] += 1
                em.emit("Y", lambda E, qidx=qidx, qsl=qsl, qsx=qsx: E.dma_start(
                    q_slot[qsl][:, :], Qs[qidx, :, qsx * 512:qsx * 512 + 512]),
                    waits=w, inc=("dq%d" % qsl, 16))
                qld_cnt[(qsx, uu)] = (qsl, em.c("dq%d" % qsl))

            def emit_kq_ahead(qsx, uu):
                if uu > 15:
                    qsx, uu = qsx + 1, uu - 16
                    if qsx > 3:
                        return
                if uu % 2 == 0:
                    emit_k(qsx, uu // 2)
                emit_q(qsx, uu)

            def emit_g(qs, qc):
                w = []
                if qs > 0:
                    w.append(("dfin", 8 * (qs - 1) + 2 * (qc + 1)))
                elif qs == 0 and qc < 2:
                    w += G_ST
                r0 = qs * 512 + qc * 128
                em.emit("S", lambda E, qc=qc, r0=r0: E.dma_start(
                    g_slot[qc][:, :], gH[r0:r0 + 128, :]),
                    waits=w, inc=("dg%d" % qc, 16))

            avbanks = [(pav[0], "pav0"), (pav[1], "pav1"), (pg[0], "pg0"), (pg[1], "pg1")]

            def emit_avqc(cx, qc, qs, u):
                gqc = 4 * qs + qc
                pt, pname = avbanks[qc]
                aq = acquire(pname)
                w = [aq]
                if qc == 0 or cx.get("needw"):
                    w += cx["WEXP"] + [("dmk", cx["DMK"])]
                    cx["needw"] = False
                if qc == 0 and u <= 1:
                    w += [("dv%d" % qq, 64) for qq in range(qs + 1)]
                for kc in range(gqc + 1):
                    mm_inc = ("dav", 1) if kc == gqc else None
                    if kc == 0 and kc != gqc:
                        if qc == 1:
                            mm_inc = ("dqf%d" % cx["qsl"], 1)
                        elif qc == 2 and cx["j"] == 1:
                            mm_inc = ("dkf%d" % cx["ksl"], 1)
                    em.emit("T", lambda E, pt=pt, wsl=cx["wsl"], kc=kc,
                            qc=qc, i=cx["i"], gqc=gqc:
                            nc.tensor.matmul(
                                pt[:, 0:129],
                                wT[wsl][:, kc * 512 + qc * 128:
                                         kc * 512 + qc * 128 + 128],
                                vslots[:, kc * 8 + i, :],
                                start=(kc == 0), stop=(kc == gqc)),
                            waits=w if kc == 0 else (),
                            inc=mm_inc)
                DAV = em.c("dav")
                em.emit("V", lambda E, pt=pt, qc=qc: nc.vector.reciprocal(
                    recip[:, qc:qc + 1], pt[:, 128:129]),
                    waits=[("dav", DAV), ("dvc", em.c("dvc")), aq],
                    inc=("dvc", 1))
                em.emit("V", lambda E, pt=pt, qc=qc, qidx=cx["qidx"], qs=qs:
                        nc.vector.tensor_scalar(
                            s_slab[qs % 2][:, qc * 2048 + qidx * 128:
                                           qc * 2048 + qidx * 128 + 128],
                            pt[:, 0:128], recip[:, qc:qc + 1],
                            None, ALU.mult),
                        waits=[("dvc", em.c("dvc")),
                               ("dofq%d" % qc, 16 * max(0, qs - 1))],
                        inc=("f_" + pname, 1))

            def emit_fin(qsf, qc, half, SEV):
                r0 = qsf * 512 + qc * 128
                c0 = qc * 2048 + half * 1024
                w = SEV + [("dg%d" % qc, 16 * (qsf + 1)),
                           ("dofq%d" % qc, 16 * qsf),
                           ("dfin", em.c("dfin"))]
                em.emit("V", lambda E, qsf=qsf, qc=qc, c0=c0, half=half:
                        nc.vector.tensor_mul(
                            s_slab[qsf % 2][:, c0:c0 + 1024],
                            s_slab[qsf % 2][:, c0:c0 + 1024],
                            g_slot[qc][:, half * 1024:(half + 1) * 1024]),
                        waits=w, inc=("dfin", 1))
                if half == 1:
                    em.emit("P", lambda E, qc=qc, r0=r0, qsf=qsf: E.dma_start(
                        out[r0:r0 + 128, :],
                        s_slab[qsf % 2][:, qc * 2048:(qc + 1) * 2048],
                        accum_op=ALU.add),
                        waits=[("dfin", em.c("dfin")), ("dpre", 64)],
                        inc=("dofq%d" % qc, 16))

            fin_q = []
            kuse = {i: 0 for i in range(4)}
            quse = {i: 0 for i in range(8)}
            batch_ctr = 0
            qh_ctr = 0
            pexp = {0: 0, 1: 0}
            wav = {0: 0, 1: 0}
            xg_use = {0: 0, 1: 0}
            out_ct = 0
            def emit_vload(qsx):
                for kc in range(4 * qsx, 4 * qsx + 4):
                    w = [("dms", 1)] + B_ST if (qsx == 0 and kc == 0) else ()
                    em.emit("Y", lambda E, kc=kc: E.dma_start(
                        vslots[:, kc * 8:(kc + 1) * 8, 0:128], vsr[:, kc, :, :]),
                        waits=w, inc=("dv%d" % qsx, 16))

            for qs in range(4):
                kmax = 4 * (qs + 1)
                if qs == 0:
                    emit_vload(0)
                if qs == 0:
                    for _w in range(4):
                        emit_kq_ahead(0, _w)
                pend = None
                for u in range(17):
                    if u < 16:
                        i = u // 2
                        j = u % 2
                        ksl = i % 4
                        emit_kq_ahead(qs, u + 4)
                        qidx = j * 8 + i
                        qsl, dq_w = qld_cnt[(qs, u)]
                        _, dk_w = kld_cnt[(qs, i)]
                        wsl = qh_ctr % 2
                        nb = kmax // 2
                        seq = []
                        pos = {}
                        if pend is not None:
                            for qj in range(4):
                                pos.setdefault(min(qj + 1, nb - 1), []).append(qj)
                        for bi in range(nb):
                            seq.append(("b", bi))
                            for qj in pos.get(bi, ()):
                                seq.append(("q", qj))
                        first = True
                        for kind, idx in seq:
                            if kind == "b":
                                kc0 = 2 * idx
                                psl = batch_ctr % 2
                                ps = psA if psl == 0 else psB
                                psn = "ps%d" % psl
                                w = [("f_" + psn, pexp[psl])]
                                if first:
                                    w += [("dk%d" % ksl, dk_w),
                                          ("dq%d" % qsl, dq_w),
                                          ("dav", wav[wsl])]
                                    if qs == 0 and u == 0 and bi < 2:
                                        w += [fcnt("psAa"), fcnt("psAb"),
                                              fcnt("psBa"), fcnt("psBb")]
                                    first = False
                                for uu in range(2):
                                    kc = kc0 + uu
                                    em.emit("T", lambda E, ps=ps, ksl=ksl, qsl=qsl, kc=kc, uu=uu:
                                            nc.tensor.matmul(
                                                ps[:, uu * 512:(uu + 1) * 512],
                                                k_slot[ksl][:, kc * 128:(kc + 1) * 128],
                                                q_slot[qsl][:, :], start=True, stop=True),
                                            waits=w if uu == 0 else (),
                                            inc=("dsc", 1) if uu == 1 else None)
                                em.emit("S", lambda E, ps=ps, wsl=wsl, kc0=kc0:
                                        nc.scalar.activation(
                                            wT[wsl][:, kc0 * 512: kc0 * 512 + 1024],
                                            ps[:, 0:1024], AF.Exp,
                                            bias=konst[:, 0:1], scale=SCALE),
                                        waits=[("dsc", em.c("dsc"))],
                                        inc=("f_" + psn, 1))
                                pexp[psl] = em.c("f_" + psn)
                                batch_ctr += 1
                            else:
                                emit_avqc(pend, idx, qs, u)
                        WEXP = [("f_ps0", em.c("f_ps0")), ("f_ps1", em.c("f_ps1"))]
                        for dkc in range(4 * qs, kmax):
                            off = dkc * 512 + (dkc - 4 * qs) * 128
                            em.emit("V", lambda E, wsl=wsl, off=off: nc.vector.tensor_mul(
                                wT[wsl][:, off:off + 128], wT[wsl][:, off:off + 128],
                                mask_sb[:, :]),
                                waits=WEXP if dkc == 4 * qs else (),
                                inc=("dmk", 1))
                        ctx = dict(wsl=wsl, WEXP=WEXP, DMK=em.c("dmk"),
                                   qidx=qidx, i=i, j=j, ksl=ksl, qsl=qsl)
                        if u == 8 and qs < 3:
                            emit_vload(qs + 1)
                        if u in (4, 7, 10, 13):
                            emit_g(qs, (u - 4) // 3)
                        if fin_q and 2 <= u <= 9:
                            emit_fin(*fin_q.pop(0))
                        qh_ctr += 1
                    if u == 16:
                        for qc in range(4):
                            emit_avqc(pend, qc, qs, u)
                    if u >= 1:
                        wav[pend["wsl"]] = em.c("dav")
                    if u == 16:
                        pend = None
                    if u < 16:
                        pend = ctx
                SEV = [("f_pav0", em.c("f_pav0")), ("f_pav1", em.c("f_pav1")),
                       ("f_pg0", em.c("f_pg0")), ("f_pg1", em.c("f_pg1"))]
                if qs < 3:
                    fin_q = [(qs, qc, half, SEV)
                             for qc in range(4) for half in range(2)]
                else:
                    for qc in range(4):
                        for half in range(2):
                            emit_fin(qs, qc, half, SEV)
            em.emit("Y", lambda E: E.nop(),
                    waits=[("dofq%d" % qc, 64) for qc in range(4)])

        with nc.Block() as block:
            @block.tensor
            def _(E):
                em.play("T", E)

            @block.vector
            def _(E):
                em.play("V", E)

            @block.scalar
            def _(E):
                em.play("S", E)

            @block.gpsimd
            def _(E):
                em.play("P", E)

            @block.sync
            def _(E):
                em.play("Y", E)

    return nc


def _host_prep(inputs):
    x = np.asarray(inputs["L0_attn_x"])
    Wq = np.asarray(inputs["L0_attn_Wq"]); bq = np.asarray(inputs["L0_attn_bq"])
    Wk = np.asarray(inputs["L0_attn_Wk"]); bk = np.asarray(inputs["L0_attn_bk"])
    Wv = np.asarray(inputs["L0_attn_Wv"]); bv = np.asarray(inputs["L0_attn_bv"])
    bW = np.asarray(inputs["L0_ffn_base_W"])
    rW = np.asarray(inputs["L0_ffn_router_W"])
    lA = np.asarray(inputs["L0_ffn_lora_A"])
    lB = np.asarray(inputs["L0_ffn_lora_B"])

    ident = np.eye(128, dtype=ml_dtypes.bfloat16)
    identf = np.eye(128, dtype=np.float32)
    masku = np.triu(np.ones((128, 128), np.float32)).astype(ml_dtypes.bfloat16)
    repl = np.zeros((8, 128), np.float32)
    for e in range(8):
        repl[e, e * 16:(e + 1) * 16] = 1.0
    repl = repl.astype(ml_dtypes.bfloat16)
    onec = np.ones((128, 1), dtype=ml_dtypes.bfloat16)
    oner = np.ones((1, 128), dtype=np.float32)
    onerb = np.ones((1, 128), dtype=ml_dtypes.bfloat16)
    lA2 = np.ascontiguousarray(
        lA.transpose(1, 0, 2).reshape(C, 128).reshape(32, 128, 128)
        .transpose(1, 0, 2).reshape(128, C)).astype(np.float32)
    rt2 = np.ascontiguousarray(
        rW.T.reshape(32, 128, 8).transpose(1, 0, 2).reshape(128, 256)).astype(np.float32)
    lBf = lB.reshape(128, C)

    in_maps = []
    qsels = []
    for c in range(8):
        b, h = c // 2, c % 2
        qlist = [8 * h + i for i in range(8)] + [16 + 8 * h + i for i in range(8)]
        qsel = np.concatenate([np.arange(qh * 128, (qh + 1) * 128) for qh in qlist])
        kvsel = np.arange(1024 * h, 1024 * h + 1024)
        qsels.append(qsel)
        m = dict(
            xT=np.ascontiguousarray(x[b].T).astype(np.float32),
            xsel=np.ascontiguousarray(x[b][:, qsel]).astype(np.float32),
            wqT=np.ascontiguousarray(Wq[qsel, :].T).astype(np.float32),
            wkT=np.ascontiguousarray(Wk[kvsel, :].T).astype(np.float32),
            wvT=np.ascontiguousarray(Wv[kvsel, :].T).astype(np.float32),
            bqT=np.ascontiguousarray(bq[qsel].reshape(16, 128).T).astype(np.float32),
            bkT=np.ascontiguousarray(bk[kvsel].reshape(8, 128).T).astype(np.float32),
            bvr=np.ascontiguousarray(bv[kvsel].reshape(1, 1024)).astype(np.float32),
            bwT=np.ascontiguousarray(bW[qsel, :].T).astype(np.float32),
            lAi=lA2, lBi=np.ascontiguousarray(lBf[:, qsel]).astype(np.float32),
            rti=rt2, idbi=ident, idfi=identf, maski=masku, repli=repl,
            oneci=onec, oneri=oner, onerbi=onerb,
        )
        in_maps.append(m)
    return in_maps, qsels


def kernel(**inputs):
    in_maps, qsels = _host_prep(inputs)
    nc = build_program()
    res = run_bass_kernel_spmd(nc, in_maps, core_ids=list(range(8)))
    outp = np.zeros((4, T, C), dtype=np.float32)
    for c in range(8):
        b = c // 2
        o = np.asarray(res.results[c]["out"], dtype=np.float32)
        outp[b][:, qsels[c]] = o
    return outp


# version 21
# speedup vs baseline: 1.0362x; 1.0018x over previous
import os
import numpy as np
import ml_dtypes
import concourse.bass as bass
import concourse.mybir as mybir
from concourse.bass_utils import run_bass_kernel_spmd
from contextlib import ExitStack

F32 = mybir.dt.float32
BF16 = mybir.dt.bfloat16
AF = mybir.ActivationFunctionType
ALU = mybir.AluOpType
AX = mybir.AxisListType

T = 2048
C = 4096
NCH = 32
SCALE = 1.0 / float(np.sqrt(np.float32(32)))
EXP_SHIFT = -12.0
EPS = 1e-5

DEBUG = int(os.environ.get("MK_DEBUG", "0"))
PHASES = os.environ.get("MK_PHASES", "ABCD")


class Em:

    ENGS = ("T", "V", "S", "P", "Y")

    def __init__(self, nc, stack):
        self.nc = nc
        self.ops = {e: [] for e in self.ENGS}
        self.sems = {}
        self.cnt = {}
        self.stack = stack

    def sem(self, name):
        if name not in self.sems:
            self.sems[name] = self.stack.enter_context(self.nc.semaphore(name))
            self.cnt[name] = 0
        return self.sems[name]

    def c(self, name):
        self.sem(name)
        return self.cnt[name]

    def emit(self, eng, fn, waits=(), inc=None):
        wl = [(self.sem(n), v) for (n, v) in waits]
        il = None
        if inc is not None:
            n, d = inc
            self.sem(n)
            self.cnt[n] += d
            il = (self.sems[n], d)
        self.ops[eng].append((wl, fn, il))

    def barrier(self, name):
        self.sem(name)
        tgt = self.cnt[name] + 5
        for e in self.ENGS:
            self.emit(e, lambda E: E.nop(), inc=(name, 1))
        for e in self.ENGS:
            self.ops[e].append(([(self.sems[name], tgt)], None, None))

    def play(self, eng, E):
        for wl, fn, il in self.ops[eng]:
            for (h, v) in wl:
                E.wait_ge(h, v)
            if fn is not None:
                ins = fn(E)
                if il is not None:
                    ins.then_inc(*il)


def build_program():
    nc = bass.Bass("TRN2", target_bir_lowering=False)

    xT = nc.dram_tensor("xT", [C, T], F32, kind="ExternalInput")
    xsel = nc.dram_tensor("xsel", [T, 2048], F32, kind="ExternalInput")
    wqT = nc.dram_tensor("wqT", [C, 2048], F32, kind="ExternalInput")
    wkT = nc.dram_tensor("wkT", [C, 1024], F32, kind="ExternalInput")
    wvT = nc.dram_tensor("wvT", [C, 1024], F32, kind="ExternalInput")
    bqT = nc.dram_tensor("bqT", [128, 16], F32, kind="ExternalInput")
    bkT = nc.dram_tensor("bkT", [128, 8], F32, kind="ExternalInput")
    bvr = nc.dram_tensor("bvr", [1, 1024], F32, kind="ExternalInput")
    bwT = nc.dram_tensor("bwT", [C, 2048], F32, kind="ExternalInput")
    lAi = nc.dram_tensor("lAi", [128, 4096], F32, kind="ExternalInput")
    lBi = nc.dram_tensor("lBi", [128, 2048], F32, kind="ExternalInput")
    rti = nc.dram_tensor("rti", [128, 256], F32, kind="ExternalInput")
    idbi = nc.dram_tensor("idbi", [128, 128], BF16, kind="ExternalInput")
    maski = nc.dram_tensor("maski", [128, 128], BF16, kind="ExternalInput")
    repli = nc.dram_tensor("repli", [8, 128], BF16, kind="ExternalInput")
    oneci = nc.dram_tensor("oneci", [128, 1], BF16, kind="ExternalInput")
    oneri = nc.dram_tensor("oneri", [1, 128], F32, kind="ExternalInput")
    idfi = nc.dram_tensor("idfi", [128, 128], F32, kind="ExternalInput")
    onerbi = nc.dram_tensor("onerbi", [1, 128], BF16, kind="ExternalInput")

    scratch_kind = "ExternalOutput" if DEBUG else "Internal"
    Qs = nc.dram_tensor("Qs", [16, 128, T], BF16, kind=scratch_kind)
    Ks = nc.dram_tensor("Ks", [8, 128, T], BF16, kind=scratch_kind)
    Vs = nc.dram_tensor("Vs", [T, 1024], BF16, kind=scratch_kind)
    gH = nc.dram_tensor("gH", [T, 2048], F32, kind=scratch_kind)
    out = nc.dram_tensor("out", [T, 2048], F32, kind="ExternalOutput")
    if DEBUG:
        dxn = nc.dram_tensor("dxn", [128, NCH * T], BF16, kind="ExternalOutput")
        dwd = nc.dram_tensor("dwd", [128, 128], BF16, kind="ExternalOutput")
        dmg = nc.dram_tensor("dmg", [128, 2048], BF16, kind="ExternalOutput")

    with ExitStack() as stack:
        en = stack.enter_context
        arena = en(nc.sbuf_tensor("arena", [128, 65536], BF16))
        wstr = en(nc.sbuf_tensor("wstr", [128, 16384], BF16))
        vsb = en(nc.sbuf_tensor("vsb", [128, 16512], BF16))
        bq_sb = en(nc.sbuf_tensor("bq_sb", [128, 16], F32))
        bk_sb = en(nc.sbuf_tensor("bk_sb", [128, 8], F32))
        bv_sb = en(nc.sbuf_tensor("bv_sb", [1, 1024], BF16))
        rt_sb = en(nc.sbuf_tensor("rt_sb", [128, 256], BF16))
        idb = en(nc.sbuf_tensor("idb", [128, 128], BF16))
        idf = en(nc.sbuf_tensor("idf", [128, 128], F32))
        mask_sb = en(nc.sbuf_tensor("mask_sb", [128, 128], BF16))
        repl_sb = en(nc.sbuf_tensor("repl_sb", [8, 128], BF16))
        onec = en(nc.sbuf_tensor("onec", [128, 1], BF16))
        oner = en(nc.sbuf_tensor("oner", [1, 128], F32))
        onerb = en(nc.sbuf_tensor("onerb", [1, 128], BF16))
        konst = en(nc.sbuf_tensor("konst", [128, 8], F32))
        nm1 = en(nc.sbuf_tensor("nm1", [128, 16], F32))
        nm2 = en(nc.sbuf_tensor("nm2", [128, 16], F32))
        trow = en(nc.sbuf_tensor("trow", [128, 16], F32))
        inv2 = en(nc.sbuf_tensor("inv2", [128, 16], F32))
        keep = en(nc.sbuf_tensor("keep", [128, 128], F32))
        lmsk = en(nc.sbuf_tensor("lmsk", [128, 128], F32))
        recip = en(nc.sbuf_tensor("recip", [128, 4], F32))

        arena32 = arena.bitcast(F32)
        wstr32 = wstr.bitcast(F32)
        vsb32 = vsb.bitcast(F32)

        xnT = arena
        s_slab = [arena32[:, 0:8192], arena32[:, 8192:16384]]
        g_slot = [arena32[:, 16384 + 2048 * i: 16384 + 2048 * (i + 1)]
                  for i in range(4)]
        wT = [arena[:, 49152:57344], arena[:, 57344:65536]]
        q_slot = ([wstr[:, 4096 + 512 * i: 4096 + 512 * (i + 1)] for i in range(4)]
                  + [wstr[:, 10240 + 512 * i: 10240 + 512 * (i + 1)] for i in range(4)])
        k_slot = [wstr[:, 0:2048], wstr[:, 2048:4096],
                  wstr[:, 6144:8192], wstr[:, 8192:10240]]

        mrow = wstr32[0:1, 4096:6144]
        qrow = wstr32[0:1, 6144:8192]
        wbuf = [wstr[:, 0:4096], wstr[:, 4096:8192]]
        vwbuf = [wstr[:, 0:8192], wstr[:, 8192:16384]]
        bwbuf = [wstr[:, 0:8192], wstr[:, 8192:16384]]

        rrow = vsb32[0:1, 0:2048]
        lgT_sb = vsb32[0:8, 0:2048]
        rb_sb = vsb[:, 4096:6144]
        mb_sb = vsb[:, 8192:10240]
        sqb = [vsb[:, 12288:14336], vsb[:, 14336:16384]]
        lA_sb = vsb[:, 0:4096]
        lB_sb = vsb[:, 4096:6144]
        wdbig = vsb[:, 6144:8192]
        midgT = vsb[:, 8192:10240]
        lgn = vsb32[:, 5120:5248]
        e_sb = vsb32[:, 5248:5376]
        wd_sb = vsb[:, 10752:10880]
        wdT_sb = vsb[0:8, 10880:12928]
        qkstage = [vsb[:, 12928:13440], vsb[:, 13440:13952]]
        vstage = [vsb[:, 13952:14208], vsb[:, 14208:14464]]
        gstage = [vsb32[:, 7232:7488], vsb32[:, 7488:7744]]
        vslots = vsb.rearrange("p (s n) -> p s n", n=129)

        rem = nc.sbuf_bytes_remaining
        assert rem >= 0, f"SBUF over budget by {-rem} bytes"

        psA = en(nc.psum_tensor("psA", [128, 1024], F32))
        psB = en(nc.psum_tensor("psB", [128, 1024], F32))
        pav = [en(nc.psum_tensor("pav0", [128, 512], F32)),
               en(nc.psum_tensor("pav1", [128, 512], F32))]
        pg = [en(nc.psum_tensor("pg0", [128, 512], F32)),
              en(nc.psum_tensor("pg1", [128, 512], F32))]
        pgb = [pg[0].bitcast(BF16), pg[1].bitcast(BF16)]

        em = Em(nc, stack)
        use = {}

        def acquire(name, scale=1):
            u = use.get(name, 0)
            use[name] = u + 1
            return ("f_" + name, scale * u)

        def fcnt(name):
            return ("f_" + name, em.c("f_" + name))

        em.emit("V", lambda E: nc.vector.memset(konst[:, 0:1], EXP_SHIFT), inc=("kon", 1))
        em.emit("V", lambda E: nc.vector.memset(konst[:, 1:2], EPS), inc=("kon", 1))
        for dst, src in [(bq_sb, bqT), (bk_sb, bkT), (idb, idbi), (idf, idfi),
                         (mask_sb, maski),
                         (repl_sb, repli), (onec, oneci), (oner, oneri), (onerb, onerbi)]:
            em.emit("Y", lambda E, d=dst, sr=src: E.dma_start(d[:, :], sr[:, :]),
                    inc=("kload", 16))
        em.emit("P", lambda E: E.dma_start(bv_sb[:, :], bvr[:, :]), inc=("kloadp", 16))
        em.emit("P", lambda E: E.dma_start(rt_sb[:, :], rti[:, :]), inc=("kloadp", 16))
        KL = em.c("kload")
        KLP = em.c("kloadp")

        if "A" in PHASES:
            sum_ps = [psA[0:1, 0:512], psA[0:1, 512:1024],
                      psB[0:1, 0:512], psB[0:1, 512:1024]]
            sq_ps = [pav[0][0:1, 0:512], pav[1][0:1, 0:512],
                     pg[0][0:1, 0:512], pg[1][0:1, 0:512]]
            aqs0 = [acquire(n) for n in
                    ("psAa", "psAb", "psBa", "psBb", "pav0", "pav1", "pg0", "pg1")]
            for c in range(NCH):
                b = c % 2
                sname = "axp%d" % (c % 8)
                w = [] if c < 8 else [(sname, 16 * (c // 8))]
                em.emit("P", lambda E, c=c: E.dma_start(
                    xnT[:, c * T:(c + 1) * T], xT[c * 128:(c + 1) * 128, :]),
                    waits=w, inc=(sname, 16))
                w = [(sname, 16 * (c // 8 + 1))]
                if c >= 2:
                    w.append(("astat", 8 * (c - 1)))
                em.emit("S", lambda E, b=b, c=c: nc.scalar.activation(
                    sqb[b][:, :], xnT[:, c * T:(c + 1) * T], AF.Square),
                    waits=w, inc=("asq", 1))
                w = [(sname, 16 * (c // 8 + 1)), ("asq", c + 1)]
                if c == 0:
                    w += [("kload", KL), ("kloadp", KLP)] + aqs0
                for ts in range(4):
                    em.emit("T", lambda E, c=c, ts=ts: nc.tensor.matmul(
                        sum_ps[ts], onec[:, :],
                        xnT[:, c * T + ts * 512: c * T + ts * 512 + 512],
                        start=(c == 0), stop=(c == NCH - 1)),
                        waits=w if ts == 0 else (), inc=("astat", 1))
                for ts in range(4):
                    em.emit("T", lambda E, b=b, ts=ts, c=c: nc.tensor.matmul(
                        sq_ps[ts], onec[:, :], sqb[b][:, ts * 512:(ts + 1) * 512],
                        start=(c == 0), stop=(c == NCH - 1)),
                        inc=("astat", 1))
            AST = em.c("astat")

            ev_rel = ["psAa", "psAb", "psBa", "psBb"]
            for ts in range(4):
                w = [("astat", AST)] if ts == 0 else ()
                em.emit("V", lambda E, ts=ts: nc.vector.tensor_copy(
                    mrow[:, ts * 512:(ts + 1) * 512], sum_ps[ts]),
                    waits=w, inc=("f_" + ev_rel[ts], 1))
            sq_rel = ["pav0", "pav1", "pg0", "pg1"]
            for ts in range(4):
                em.emit("V", lambda E, ts=ts: nc.vector.tensor_copy(
                    qrow[:, ts * 512:(ts + 1) * 512], sq_ps[ts]),
                    inc=("f_" + sq_rel[ts], 1))
            wrows = [fcnt("psAa"), fcnt("psAb"), fcnt("psBa"), fcnt("psBb"),
                     fcnt("pav0"), fcnt("pav1"), fcnt("pg0"), fcnt("pg1")]
            em.emit("V", lambda E: nc.vector.tensor_scalar(
                mrow[:, :], mrow[:, :], 1.0 / C, None, ALU.mult),
                waits=wrows, inc=("arow", 1))
            em.emit("V", lambda E: nc.vector.tensor_scalar(
                qrow[:, :], qrow[:, :], 1.0 / C, None, ALU.mult),
                waits=[("arow", em.c("arow"))], inc=("arow", 1))
            em.emit("V", lambda E: nc.vector.tensor_mul(rrow[:, :], mrow[:, :], mrow[:, :]),
                    waits=[("arow", em.c("arow"))], inc=("arow", 1))
            em.emit("V", lambda E: nc.vector.tensor_sub(qrow[:, :], qrow[:, :], rrow[:, :]),
                    waits=[("arow", em.c("arow"))], inc=("arow", 1))
            em.emit("S", lambda E: nc.scalar.activation(
                rrow[:, :], qrow[:, :], AF.Ln, bias=konst[0:1, 1:2]),
                waits=[("arow", em.c("arow")), ("kon", 2)], inc=("asd", 1))
            em.emit("S", lambda E: nc.scalar.activation(
                rrow[:, :], rrow[:, :], AF.Exp, scale=-0.5),
                waits=[("asd", 1)], inc=("asd", 1))
            em.emit("V", lambda E: E.nop(),
                    waits=[("asd", 2)], inc=("arow", 1))
            em.emit("V", lambda E: nc.vector.scalar_tensor_tensor(
                mrow[:, :], mrow[:, :], -1.0, rrow[:, :], ALU.mult, ALU.mult),
                waits=[("arow", em.c("arow"))], inc=("arow", 1))
            ARW = em.c("arow")

            for ts in range(4):
                a0 = acquire("pg0")
                a1 = acquire("pg1")
                em.emit("T", lambda E, ts=ts: nc.tensor.matmul(
                    pg[0][:, :], oner[:, :], rrow[:, ts * 512:(ts + 1) * 512],
                    start=True, stop=True),
                    waits=[("arow", ARW), a0, a1], inc=("abc", 1))
                em.emit("T", lambda E, ts=ts: nc.tensor.matmul(
                    pg[1][:, :], oner[:, :], mrow[:, ts * 512:(ts + 1) * 512],
                    start=True, stop=True), inc=("abc", 1))
                em.emit("V", lambda E, ts=ts: nc.vector.tensor_copy(
                    rb_sb[:, ts * 512:(ts + 1) * 512], pg[0][:, :]),
                    waits=[("abc", 2 * (ts + 1))], inc=("f_pg0", 1))
                em.emit("V", lambda E, ts=ts: nc.vector.tensor_copy(
                    mb_sb[:, ts * 512:(ts + 1) * 512], pg[1][:, :]),
                    inc=("f_pg1", 1))
            WBC = [fcnt("pg0"), fcnt("pg1")]

            for c in range(NCH):
                w = [("anrm", em.c("anrm"))]
                if c == 0:
                    w += WBC
                em.emit("V", lambda E, c=c: nc.vector.tensor_mul(
                    xnT[:, c * T:(c + 1) * T], xnT[:, c * T:(c + 1) * T], rb_sb[:, :]),
                    waits=w, inc=("anrm", 1))
                em.emit("V", lambda E, c=c: nc.vector.tensor_add(
                    xnT[:, c * T:(c + 1) * T], xnT[:, c * T:(c + 1) * T], mb_sb[:, :]),
                    waits=[("anrm", em.c("anrm"))], inc=("anrm", 1))
            if DEBUG:
                em.emit("Y", lambda E: E.dma_start(dxn[:, :], xnT[:, :]),
                        waits=[("anrm", em.c("anrm"))], inc=("dbg", 16))

        if "B" in PHASES:
            def wsrc(dram, blk, width):
                r = dram.rearrange("(c p) n -> p c n", p=128)
                return r[:, :, blk * width:(blk + 1) * width]

            for i in range(4):
                em.emit("Y", lambda E, i=i: E.dma_start(
                    out[i * 512:(i + 1) * 512, :], xsel[i * 512:(i + 1) * 512, :]),
                    inc=("dpre", 16))
            bcb = [(pg[0], "pg0"), (pg[1], "pg1"),
                   (psA[:, 0:512], "psAa"), (psA[:, 512:1024], "psAb"),
                   (psB[:, 0:512], "psBa"), (psB[:, 512:1024], "psBb")]
            ANRM = em.c("anrm")
            wld = {0: 0, 1: 0}
            for og in range(24):
                sl = og % 2
                isq = og < 16
                dram = wqT if isq else wkT
                ogl = og if isq else og - 16
                w = [] if og < 2 else [("bwu%d" % sl, og // 2)]
                wld[sl] += 1
                em.emit("P", lambda E, d=dram, ogl=ogl, sl=sl: E.dma_start(
                    wbuf[sl].rearrange("p (c n) -> p c n", n=128), wsrc(d, ogl, 128)),
                    waits=w, inc=("bw%d" % sl, 16))
                for ts in range(4):
                    gi = og * 4 + ts
                    pp, pname = bcb[gi % 6]
                    aq = acquire(pname)
                    w = [("bw%d" % sl, 16 * wld[sl]), aq]
                    for ch in range(NCH):
                        last = (ch == NCH - 1)
                        grp_inc = (("bwu%d" % sl, 1) if ts == 3 else ("bmm", 1)) if last else None
                        wch = w if ch == 0 else []
                        if og == 0 and ts == 0:
                            wch = list(wch) + [("anrm", 2 * (ch + 1))]
                        em.emit("T", lambda E, pp=pp, sl=sl, ch=ch, ts=ts: nc.tensor.matmul(
                            pp[:, :], wbuf[sl][:, ch * 128:(ch + 1) * 128],
                            xnT[:, ch * T + ts * 512: ch * T + ts * 512 + 512],
                            start=(ch == 0), stop=(ch == NCH - 1)),
                            waits=wch,
                            inc=grp_inc)
                    ssl = gi % 2
                    aqs = acquire("qkst%d" % ssl, 16)
                    bias = bq_sb[:, og:og + 1] if isq else bk_sb[:, ogl:ogl + 1]
                    gw = ("bwu%d" % sl, em.c("bwu%d" % sl)) if ts == 3 else ("bmm", em.c("bmm"))
                    em.emit("S", lambda E, ssl=ssl, pp=pp, bias=bias: nc.scalar.activation(
                        qkstage[ssl][:, :], pp[:, :], AF.Identity, bias=bias),
                        waits=[gw, aqs], inc=("f_" + pname, 1))
                    dst = (Qs[og, :, ts * 512:(ts + 1) * 512] if isq
                           else Ks[ogl, :, ts * 512:(ts + 1) * 512])
                    em.emit("Y", lambda E, dst=dst, ssl=ssl: E.dma_start(
                        dst, qkstage[ssl][:, :]),
                        waits=[fcnt(pname)], inc=("f_qkst%d" % ssl, 16))
            for ts in range(4):
                psl = ts % 2
                pname = "pav%d" % psl
                aq = acquire(pname)
                for ch in range(NCH):
                    em.emit("T", lambda E, psl=psl, ch=ch, ts=ts: nc.tensor.matmul(
                        pav[psl][0:8, :], rt_sb[:, ch * 8:(ch + 1) * 8],
                        xnT[:, ch * T + ts * 512: ch * T + ts * 512 + 512],
                        start=(ch == 0), stop=(ch == NCH - 1)),
                        waits=[aq] if ch == 0 else (),
                        inc=("clg", 1) if ch == NCH - 1 else None)
                em.emit("V", lambda E, psl=psl, ts=ts: nc.vector.tensor_copy(
                    lgT_sb[:, ts * 512:(ts + 1) * 512], pav[psl][0:8, :]),
                    waits=[("clg", em.c("clg"))], inc=("f_" + pname, 1))
            LGT = [("f_pav0", em.c("f_pav0")), ("f_pav1", em.c("f_pav1"))]
            BMM_QK = em.c("bmm")

            def piece_rtrans():
                aq_tr = acquire("pg0")
                for tc in range(16):
                    w = LGT + [aq_tr] if tc == 0 else ()
                    em.emit("T", lambda E, tc=tc: nc.tensor.transpose(
                        pg[0][:, tc * 8:(tc + 1) * 8],
                        lgT_sb[:, tc * 128:(tc + 1) * 128], idf[0:8, 0:8]),
                        waits=w, inc=("ctg", 1))
                em.emit("V", lambda E: nc.vector.tensor_copy(
                    lgn[:, :], pg[0][:, 0:128]),
                    waits=[("ctg", 16)], inc=("f_pg0", 1))
                def b8(ap2d, stride):
                    return bass.AP(ap2d.tensor, ap2d.offset,
                                   [[stride, 128], [1, 16], [0, 8]])
                lgn3 = lgn.rearrange("p (g n) -> p g n", n=8)
                e3 = e_sb.rearrange("p (g n) -> p g n", n=8)
                keep3 = keep.rearrange("p (g n) -> p g n", n=8)
                wd3 = wd_sb.rearrange("p (g n) -> p g n", n=8)
                lmsk3 = lmsk.rearrange("p (g n) -> p g n", n=8)
                nm1b = b8(nm1[:, :], 16)
                trowb = b8(trow[:, :], 16)
                inv2b = b8(inv2[:, :], 16)
                em.emit("V", lambda E: nc.vector.reduce_max(
                    nm1[:, :], lgn3, AX.X, negate=True),
                    waits=[("f_pg0", em.c("f_pg0"))], inc=("ct", 1))
                em.emit("V", lambda E: nc.vector.tensor_tensor(
                    lgn3, lgn3, nm1b, ALU.add),
                    waits=[("ct", em.c("ct"))], inc=("ct", 1))
                em.emit("V", lambda E: nc.vector.tensor_scalar(
                    keep[:, :], lgn[:, :], 0.0, None, ALU.is_ge),
                    waits=[("ct", em.c("ct"))], inc=("ct", 1))
                em.emit("V", lambda E: nc.vector.scalar_tensor_tensor(
                    lmsk[:, :], keep[:, :], -1e30, lgn[:, :], ALU.mult, ALU.add),
                    waits=[("ct", em.c("ct"))], inc=("ct", 1))
                em.emit("V", lambda E: nc.vector.reduce_max(
                    nm2[:, :], lmsk3, AX.X, negate=True),
                    waits=[("ct", em.c("ct"))], inc=("ct", 1))
                em.emit("S", lambda E: nc.scalar.activation(
                    e_sb[:, :], lgn[:, :], AF.Exp),
                    waits=[("ct", em.c("ct"))], inc=("ce", 1))
                em.emit("S", lambda E: nc.scalar.activation(
                    trow[:, :], nm2[:, :], AF.Exp, scale=-1.0),
                    inc=("ce", 1))
                em.emit("V", lambda E: nc.vector.tensor_scalar(
                    inv2[:, :], trow[:, :], 0.5, 0.5, ALU.mult, ALU.add),
                    waits=[("ce", em.c("ce"))], inc=("ct", 1))
                em.emit("V", lambda E: nc.vector.reciprocal(
                    inv2[:, :], inv2[:, :]),
                    waits=[("ct", em.c("ct"))], inc=("ct", 1))
                em.emit("V", lambda E: nc.vector.tensor_tensor(
                    keep3, e3, trowb, ALU.is_ge),
                    waits=[("ct", em.c("ct"))], inc=("ct", 1))
                em.emit("V", lambda E: nc.vector.tensor_mul(
                    keep[:, :], keep[:, :], e_sb[:, :]),
                    waits=[("ct", em.c("ct"))], inc=("ct", 1))
                em.emit("V", lambda E: nc.vector.tensor_tensor(
                    wd3, keep3, inv2b, ALU.mult),
                    waits=[("ct", em.c("ct"))], inc=("cw", 16))

            def piece_wdtrans():
                for half in range(2):
                    pname = "pg%d" % half
                    aq = acquire(pname)
                    for i in range(8):
                        tc = half * 8 + i
                        w = [("cw", 16), aq] if i == 0 else ()
                        em.emit("T", lambda E, half=half, i=i, tc=tc: nc.tensor.transpose(
                            pgb[half][0:8, i * 128:(i + 1) * 128],
                            wd_sb[:, tc * 8:(tc + 1) * 8], idb[:, :]),
                            waits=w, inc=("ctr", 1))
                    em.emit("V", lambda E, half=half: nc.vector.tensor_copy(
                        wdT_sb[:, half * 1024:(half + 1) * 1024], pgb[half][0:8, 0:1024]),
                        waits=[("ctr", em.c("ctr"))], inc=("f_" + pname, 1))

            def piece_repl():
                WDT = [fcnt("pg0"), fcnt("pg1")]
                for ts in range(4):
                    psl = ts % 2
                    pname = "pav%d" % psl
                    aq = acquire(pname)
                    em.emit("T", lambda E, ts=ts, psl=psl: nc.tensor.matmul(
                        pav[psl][:, :], repl_sb[:, :], wdT_sb[:, ts * 512:(ts + 1) * 512],
                        start=True, stop=True),
                        waits=WDT + [aq] if ts == 0 else [aq], inc=("cwb", 1))
                    em.emit("V", lambda E, ts=ts, psl=psl: nc.vector.tensor_copy(
                        wdbig[:, ts * 512:(ts + 1) * 512], pav[psl][:, :]),
                        waits=[("cwb", em.c("cwb"))], inc=("f_" + pname, 1))

            def emit_vog(vog):
                sl = vog % 2
                w = ([("bmm", BMM_QK), ("bwu0", em.c("bwu0")), ("bwu1", em.c("bwu1"))]
                     if vog < 2 else [("bvu%d" % sl, vog // 2)])
                em.emit("P", lambda E, vog=vog, sl=sl: E.dma_start(
                    vwbuf[sl].rearrange("p (c n) -> p c n", n=256), wsrc(wvT, vog, 256)),
                    waits=w, inc=("bv%d" % sl, 16))
                for tc in range(16):
                    gi = vog * 16 + tc
                    pp, pname = bcb[gi % 6]
                    aq = acquire(pname)
                    w2 = [("bv%d" % sl, 16 * (vog // 2 + 1)), aq]
                    for ch in range(NCH):
                        em.emit("T", lambda E, pp=pp, sl=sl, ch=ch, tc=tc: nc.tensor.matmul(
                            pp[:, 0:256], xnT[:, ch * T + tc * 128: ch * T + tc * 128 + 128],
                            vwbuf[sl][:, ch * 256:(ch + 1) * 256],
                            start=(ch == 0), stop=False),
                            waits=w2 if ch == 0 else ())
                    grp_inc = ("bvu%d" % sl, 1) if tc == 15 else ("bmm", 1)
                    em.emit("T", lambda E, pp=pp, vog=vog: nc.tensor.matmul(
                        pp[:, 0:256], onerb[:, :], bv_sb[:, vog * 256:(vog + 1) * 256],
                        start=False, stop=True), inc=grp_inc)
                    ssl = gi % 2
                    aqs = acquire("vst%d" % ssl, 16)
                    gw = ("bvu%d" % sl, em.c("bvu%d" % sl)) if tc == 15 else ("bmm", em.c("bmm"))
                    em.emit("S", lambda E, ssl=ssl, pp=pp: nc.scalar.activation(
                        vstage[ssl][:, :], pp[:, 0:256], AF.Identity),
                        waits=[gw, aqs], inc=("f_" + pname, 1))
                    em.emit("Y", lambda E, tc=tc, vog=vog, ssl=ssl: E.dma_start(
                        Vs[tc * 128:(tc + 1) * 128, vog * 256:(vog + 1) * 256],
                        vstage[ssl][:, :]),
                        waits=[fcnt(pname)], inc=("f_vst%d" % ssl, 16))

            emit_vog(0)
            piece_rtrans()
            emit_vog(1)
            piece_wdtrans()
            emit_vog(2)
            piece_repl()
            emit_vog(3)
            em.emit("P", lambda E: E.dma_start(lA_sb[:, :], lAi[:, :]),
                    waits=[("ctg", 16)], inc=("cld", 16))
            em.emit("P", lambda E: E.dma_start(lB_sb[:, :], lBi[:, :]),
                    inc=("cld", 16))
            WDB = [fcnt("pav0"), fcnt("pav1")]
        if "C" in PHASES:
            def wsrc(dram, blk, width):
                r = dram.rearrange("(c p) n -> p c n", p=128)
                return r[:, :, blk * width:(blk + 1) * width]

            CLD = em.c("cld")


            for ts in range(4):
                pp, pname = bcb[ts % 6]
                aq = acquire(pname)
                w = [("cld", CLD), aq]
                for ch in range(NCH):
                    em.emit("T", lambda E, pp=pp, ch=ch, ts=ts: nc.tensor.matmul(
                        pp[:, :], lA_sb[:, ch * 128:(ch + 1) * 128],
                        xnT[:, ch * T + ts * 512: ch * T + ts * 512 + 512],
                        start=(ch == 0), stop=(ch == NCH - 1)),
                        waits=w if ch == 0 else (),
                        inc=("cmid", 1) if ch == NCH - 1 else None)
                w = [("cmid", em.c("cmid"))]
                if ts == 0:
                    w += WDB
                em.emit("V", lambda E, pp=pp, ts=ts: nc.vector.tensor_mul(
                    midgT[:, ts * 512:(ts + 1) * 512], pp[:, :],
                    wdbig[:, ts * 512:(ts + 1) * 512]),
                    waits=w, inc=("f_" + pname, 1))
            MGT = [fcnt("pg0"), fcnt("pg1")]
            if DEBUG:
                em.emit("Y", lambda E: E.dma_start(dmg[:, :], midgT[:, :]),
                        waits=MGT, inc=("dbg", 16))

            bld = {0: 0, 1: 0}
            for ob in range(8):
                sl = ob % 2
                w = list(MGT) if ob < 2 else [("cbu%d" % sl, ob // 2)]
                bld[sl] += 1
                em.emit("P", lambda E, ob=ob, sl=sl: E.dma_start(
                    bwbuf[sl].rearrange("p (c n) -> p c n", n=256), wsrc(bwT, ob, 256)),
                    waits=w, inc=("cb%d" % sl, 16))
                for tc in range(16):
                    gi = ob * 16 + tc
                    pp, pname = bcb[gi % 6]
                    aq = acquire(pname)
                    w = [("cb%d" % sl, 16 * bld[sl]), aq]
                    if ob < 2 and tc == 0:
                        w += list(MGT)
                    for ch in range(NCH):
                        em.emit("T", lambda E, pp=pp, sl=sl, ch=ch, tc=tc: nc.tensor.matmul(
                            pp[:, 0:256], xnT[:, ch * T + tc * 128: ch * T + tc * 128 + 128],
                            bwbuf[sl][:, ch * 256:(ch + 1) * 256],
                            start=(ch == 0), stop=False),
                            waits=w if ch == 0 else ())
                    grp_inc = ("cbu%d" % sl, 1) if tc == 15 else ("cbm", 1)
                    em.emit("T", lambda E, pp=pp, ob=ob, tc=tc: nc.tensor.matmul(
                        pp[:, 0:256], midgT[:, tc * 128:(tc + 1) * 128],
                        lB_sb[:, ob * 256:(ob + 1) * 256], start=False, stop=True),
                        inc=grp_inc)
                    aqs = acquire("gst%d" % psl, 16)
                    gw = ("cbu%d" % sl, em.c("cbu%d" % sl)) if tc == 15 else ("cbm", em.c("cbm"))
                    em.emit("S", lambda E, psl=psl, pp=pp: nc.scalar.activation(
                        gstage[psl][:, :], pp[:, 0:256], AF.Silu),
                        waits=[gw, aqs], inc=("f_" + pname, 1))
                    em.emit("Y", lambda E, tc=tc, ob=ob, psl=psl: E.dma_start(
                        gH[tc * 128:(tc + 1) * 128, ob * 256:(ob + 1) * 256],
                        gstage[psl][:, :]),
                        waits=[fcnt(pname)], inc=("f_gst%d" % psl, 16))

        if "D" in PHASES:
            em.emit("S", lambda E: nc.scalar.activation(
                trow[:, 0:1], konst[:, 0:1], AF.Exp,
                bias=konst[:, 0:1], scale=1.0))
            CBM_W = [("cbm", em.c("cbm")), ("cbu0", em.c("cbu0")),
                     ("cbu1", em.c("cbu1"))]
            CBU0_W = [("cbu0", em.c("cbu0"))]
            B_ST = [("f_qkst0", 16 * use.get("qkst0", 0)),
                    ("f_qkst1", 16 * use.get("qkst1", 0)),
                    ("f_vst0", 16 * use.get("vst0", 0)),
                    ("f_vst1", 16 * use.get("vst1", 0))]
            G_ST = [("f_gst0", 16 * use.get("gst0", 0)),
                    ("f_gst1", 16 * use.get("gst1", 0))]
            em.emit("V", lambda E: nc.vector.memset(vslots[:, :, 128:129], 1.0),
                    waits=CBM_W + B_ST + G_ST, inc=("dms", 1))
            vsr = Vs.rearrange("(kc p) (h d) -> p kc h d", p=128, d=128)

            kld_cnt = {}
            qld_cnt = {}

            def emit_k(qsx, i):
                ksl = i % 4
                w = []
                if kuse[ksl] > 0:
                    w.append(("dkf%d" % ksl, kuse[ksl]))
                elif qsx == 0:
                    w += [("f_qkst0", 16 * use.get("qkst0", 0)),
                          ("f_qkst1", 16 * use.get("qkst1", 0))]
                    w += CBU0_W if ksl < 3 else CBM_W
                kuse[ksl] += 1
                km = 4 * (qsx + 1) * 128
                em.emit("Y", lambda E, i=i, ksl=ksl, km=km: E.dma_start(
                    k_slot[ksl][:, 0:km], Ks[i, :, 0:km]),
                    waits=w, inc=("dk%d" % ksl, 16))
                kld_cnt[(qsx, i)] = (ksl, em.c("dk%d" % ksl))

            def emit_q(qsx, uu):
                i, j = uu // 2, uu % 2
                qidx = j * 8 + i
                qsl = (qsx * 16 + uu) % 8
                w = []
                if quse[qsl] > 0:
                    w.append(("dqf%d" % qsl, quse[qsl]))
                elif qsx == 0:
                    w += [("f_qkst0", 16 * use.get("qkst0", 0)),
                          ("f_qkst1", 16 * use.get("qkst1", 0))]
                    w += CBU0_W if qsl < 4 else CBM_W
                quse[qsl] += 1
                em.emit("Y", lambda E, qidx=qidx, qsl=qsl, qsx=qsx: E.dma_start(
                    q_slot[qsl][:, :], Qs[qidx, :, qsx * 512:qsx * 512 + 512]),
                    waits=w, inc=("dq%d" % qsl, 16))
                qld_cnt[(qsx, uu)] = (qsl, em.c("dq%d" % qsl))

            def emit_kq_ahead(qsx, uu):
                if uu > 15:
                    qsx, uu = qsx + 1, uu - 16
                    if qsx > 3:
                        return
                if uu % 2 == 0:
                    emit_k(qsx, uu // 2)
                emit_q(qsx, uu)

            def emit_g(qs, qc):
                w = []
                if qs > 0:
                    w.append(("dfin", 8 * (qs - 1) + 2 * (qc + 1)))
                elif qs == 0 and qc < 2:
                    w += G_ST
                r0 = qs * 512 + qc * 128
                em.emit("S", lambda E, qc=qc, r0=r0: E.dma_start(
                    g_slot[qc][:, :], gH[r0:r0 + 128, :]),
                    waits=w, inc=("dg%d" % qc, 16))

            avbanks = [(pav[0], "pav0"), (pav[1], "pav1"), (pg[0], "pg0"), (pg[1], "pg1")]

            def emit_avqc(cx, qc, qs, u):
                gqc = 4 * qs + qc
                pt, pname = avbanks[qc]
                aq = acquire(pname)
                w = [aq]
                if qc == 0 or cx.get("needw"):
                    w += cx["WEXP"] + [("dmk", cx["DMK"])]
                    cx["needw"] = False
                if qc == 0 and u <= 1:
                    w += [("dv%d" % qq, 64) for qq in range(qs + 1)]
                for kc in range(gqc + 1):
                    mm_inc = ("dav", 1) if kc == gqc else None
                    if kc == 0 and kc != gqc:
                        if qc == 1:
                            mm_inc = ("dqf%d" % cx["qsl"], 1)
                        elif qc == 2 and cx["j"] == 1:
                            mm_inc = ("dkf%d" % cx["ksl"], 1)
                    em.emit("T", lambda E, pt=pt, wsl=cx["wsl"], kc=kc,
                            qc=qc, i=cx["i"], gqc=gqc:
                            nc.tensor.matmul(
                                pt[:, 0:129],
                                wT[wsl][:, kc * 512 + qc * 128:
                                         kc * 512 + qc * 128 + 128],
                                vslots[:, kc * 8 + i, :],
                                start=(kc == 0), stop=(kc == gqc)),
                            waits=w if kc == 0 else (),
                            inc=mm_inc)
                DAV = em.c("dav")
                em.emit("V", lambda E, pt=pt, qc=qc: nc.vector.reciprocal(
                    recip[:, qc:qc + 1], pt[:, 128:129]),
                    waits=[("dav", DAV), ("dvc", em.c("dvc")), aq],
                    inc=("dvc", 1))
                em.emit("V", lambda E, pt=pt, qc=qc, qidx=cx["qidx"], qs=qs:
                        nc.vector.tensor_scalar(
                            s_slab[qs % 2][:, qc * 2048 + qidx * 128:
                                           qc * 2048 + qidx * 128 + 128],
                            pt[:, 0:128], recip[:, qc:qc + 1],
                            None, ALU.mult),
                        waits=[("dvc", em.c("dvc")),
                               ("dofq%d" % qc, 16 * max(0, qs - 1))],
                        inc=("f_" + pname, 1))

            def emit_fin(qsf, qc, half, SEV):
                r0 = qsf * 512 + qc * 128
                c0 = qc * 2048 + half * 1024
                w = SEV + [("dg%d" % qc, 16 * (qsf + 1)),
                           ("dofq%d" % qc, 16 * qsf),
                           ("dfin", em.c("dfin"))]
                em.emit("V", lambda E, qsf=qsf, qc=qc, c0=c0, half=half:
                        nc.vector.tensor_mul(
                            s_slab[qsf % 2][:, c0:c0 + 1024],
                            s_slab[qsf % 2][:, c0:c0 + 1024],
                            g_slot[qc][:, half * 1024:(half + 1) * 1024]),
                        waits=w, inc=("dfin", 1))
                if half == 1:
                    em.emit("P", lambda E, qc=qc, r0=r0, qsf=qsf: E.dma_start(
                        out[r0:r0 + 128, :],
                        s_slab[qsf % 2][:, qc * 2048:(qc + 1) * 2048],
                        accum_op=ALU.add),
                        waits=[("dfin", em.c("dfin")), ("dpre", 64)],
                        inc=("dofq%d" % qc, 16))

            fin_q = []
            kuse = {i: 0 for i in range(4)}
            quse = {i: 0 for i in range(8)}
            batch_ctr = 0
            qh_ctr = 0
            pexp = {0: 0, 1: 0}
            wav = {0: 0, 1: 0}
            xg_use = {0: 0, 1: 0}
            out_ct = 0
            def emit_vload(qsx):
                for kc in range(4 * qsx, 4 * qsx + 4):
                    w = [("dms", 1)] + B_ST if (qsx == 0 and kc == 0) else ()
                    em.emit("Y", lambda E, kc=kc: E.dma_start(
                        vslots[:, kc * 8:(kc + 1) * 8, 0:128], vsr[:, kc, :, :]),
                        waits=w, inc=("dv%d" % qsx, 16))

            for qs in range(4):
                kmax = 4 * (qs + 1)
                if qs == 0:
                    emit_vload(0)
                if qs == 0:
                    for _w in range(4):
                        emit_kq_ahead(0, _w)
                pend = None
                for u in range(17):
                    if u < 16:
                        i = u // 2
                        j = u % 2
                        ksl = i % 4
                        emit_kq_ahead(qs, u + 4)
                        qidx = j * 8 + i
                        qsl, dq_w = qld_cnt[(qs, u)]
                        _, dk_w = kld_cnt[(qs, i)]
                        wsl = qh_ctr % 2
                        nb = kmax // 2
                        seq = []
                        pos = {}
                        if pend is not None:
                            for qj in range(4):
                                pos.setdefault(min(qj + 1, nb - 1), []).append(qj)
                        for bi in range(nb):
                            seq.append(("b", bi))
                            for qj in pos.get(bi, ()):
                                seq.append(("q", qj))
                        first = True
                        for kind, idx in seq:
                            if kind == "b":
                                kc0 = 2 * idx
                                psl = batch_ctr % 2
                                ps = psA if psl == 0 else psB
                                psn = "ps%d" % psl
                                w = [("f_" + psn, pexp[psl])]
                                if first:
                                    w += [("dk%d" % ksl, dk_w),
                                          ("dq%d" % qsl, dq_w),
                                          ("dav", wav[wsl])]
                                    if qs == 0 and u == 0 and bi < 2:
                                        w += [fcnt("psAa"), fcnt("psAb"),
                                              fcnt("psBa"), fcnt("psBb")]
                                    first = False
                                for uu in range(2):
                                    kc = kc0 + uu
                                    em.emit("T", lambda E, ps=ps, ksl=ksl, qsl=qsl, kc=kc, uu=uu:
                                            nc.tensor.matmul(
                                                ps[:, uu * 512:(uu + 1) * 512],
                                                k_slot[ksl][:, kc * 128:(kc + 1) * 128],
                                                q_slot[qsl][:, :], start=True, stop=True),
                                            waits=w if uu == 0 else (),
                                            inc=("dsc", 1) if uu == 1 else None)
                                em.emit("S", lambda E, ps=ps, wsl=wsl, kc0=kc0:
                                        nc.scalar.activation(
                                            wT[wsl][:, kc0 * 512: kc0 * 512 + 1024],
                                            ps[:, 0:1024], AF.Exp,
                                            bias=konst[:, 0:1], scale=SCALE),
                                        waits=[("dsc", em.c("dsc"))],
                                        inc=("f_" + psn, 1))
                                pexp[psl] = em.c("f_" + psn)
                                batch_ctr += 1
                            else:
                                emit_avqc(pend, idx, qs, u)
                        WEXP = [("f_ps0", em.c("f_ps0")), ("f_ps1", em.c("f_ps1"))]
                        for dkc in range(4 * qs, kmax):
                            off = dkc * 512 + (dkc - 4 * qs) * 128
                            em.emit("V", lambda E, wsl=wsl, off=off: nc.vector.tensor_mul(
                                wT[wsl][:, off:off + 128], wT[wsl][:, off:off + 128],
                                mask_sb[:, :]),
                                waits=WEXP if dkc == 4 * qs else (),
                                inc=("dmk", 1))
                        ctx = dict(wsl=wsl, WEXP=WEXP, DMK=em.c("dmk"),
                                   qidx=qidx, i=i, j=j, ksl=ksl, qsl=qsl)
                        if u == 8 and qs < 3:
                            emit_vload(qs + 1)
                        if u in (4, 7, 10, 13):
                            emit_g(qs, (u - 4) // 3)
                        if fin_q and 2 <= u <= 9:
                            emit_fin(*fin_q.pop(0))
                        qh_ctr += 1
                    if u == 16:
                        for qc in range(4):
                            emit_avqc(pend, qc, qs, u)
                    if u >= 1:
                        wav[pend["wsl"]] = em.c("dav")
                    if u == 16:
                        pend = None
                    if u < 16:
                        pend = ctx
                SEV = [("f_pav0", em.c("f_pav0")), ("f_pav1", em.c("f_pav1")),
                       ("f_pg0", em.c("f_pg0")), ("f_pg1", em.c("f_pg1"))]
                if qs < 3:
                    fin_q = [(qs, qc, half, SEV)
                             for qc in range(4) for half in range(2)]
                else:
                    for qc in range(4):
                        for half in range(2):
                            emit_fin(qs, qc, half, SEV)
            em.emit("Y", lambda E: E.nop(),
                    waits=[("dofq%d" % qc, 64) for qc in range(4)])

        with nc.Block() as block:
            @block.tensor
            def _(E):
                em.play("T", E)

            @block.vector
            def _(E):
                em.play("V", E)

            @block.scalar
            def _(E):
                em.play("S", E)

            @block.gpsimd
            def _(E):
                em.play("P", E)

            @block.sync
            def _(E):
                em.play("Y", E)

    return nc


def _host_prep(inputs):
    x = np.asarray(inputs["L0_attn_x"])
    Wq = np.asarray(inputs["L0_attn_Wq"]); bq = np.asarray(inputs["L0_attn_bq"])
    Wk = np.asarray(inputs["L0_attn_Wk"]); bk = np.asarray(inputs["L0_attn_bk"])
    Wv = np.asarray(inputs["L0_attn_Wv"]); bv = np.asarray(inputs["L0_attn_bv"])
    bW = np.asarray(inputs["L0_ffn_base_W"])
    rW = np.asarray(inputs["L0_ffn_router_W"])
    lA = np.asarray(inputs["L0_ffn_lora_A"])
    lB = np.asarray(inputs["L0_ffn_lora_B"])

    ident = np.eye(128, dtype=ml_dtypes.bfloat16)
    identf = np.eye(128, dtype=np.float32)
    masku = np.triu(np.ones((128, 128), np.float32)).astype(ml_dtypes.bfloat16)
    repl = np.zeros((8, 128), np.float32)
    for e in range(8):
        repl[e, e * 16:(e + 1) * 16] = 1.0
    repl = repl.astype(ml_dtypes.bfloat16)
    onec = np.ones((128, 1), dtype=ml_dtypes.bfloat16)
    oner = np.ones((1, 128), dtype=np.float32)
    onerb = np.ones((1, 128), dtype=ml_dtypes.bfloat16)
    lA2 = np.ascontiguousarray(
        lA.transpose(1, 0, 2).reshape(C, 128).reshape(32, 128, 128)
        .transpose(1, 0, 2).reshape(128, C)).astype(np.float32)
    rt2 = np.ascontiguousarray(
        rW.T.reshape(32, 128, 8).transpose(1, 0, 2).reshape(128, 256)).astype(np.float32)
    lBf = lB.reshape(128, C)

    in_maps = []
    qsels = []
    for c in range(8):
        b, h = c // 2, c % 2
        qlist = [8 * h + i for i in range(8)] + [16 + 8 * h + i for i in range(8)]
        qsel = np.concatenate([np.arange(qh * 128, (qh + 1) * 128) for qh in qlist])
        kvsel = np.arange(1024 * h, 1024 * h + 1024)
        qsels.append(qsel)
        m = dict(
            xT=np.ascontiguousarray(x[b].T).astype(np.float32),
            xsel=np.ascontiguousarray(x[b][:, qsel]).astype(np.float32),
            wqT=np.ascontiguousarray(Wq[qsel, :].T).astype(np.float32),
            wkT=np.ascontiguousarray(Wk[kvsel, :].T).astype(np.float32),
            wvT=np.ascontiguousarray(Wv[kvsel, :].T).astype(np.float32),
            bqT=np.ascontiguousarray(bq[qsel].reshape(16, 128).T).astype(np.float32),
            bkT=np.ascontiguousarray(bk[kvsel].reshape(8, 128).T).astype(np.float32),
            bvr=np.ascontiguousarray(bv[kvsel].reshape(1, 1024)).astype(np.float32),
            bwT=np.ascontiguousarray(bW[qsel, :].T).astype(np.float32),
            lAi=lA2, lBi=np.ascontiguousarray(lBf[:, qsel]).astype(np.float32),
            rti=rt2, idbi=ident, idfi=identf, maski=masku, repli=repl,
            oneci=onec, oneri=oner, onerbi=onerb,
        )
        in_maps.append(m)
    return in_maps, qsels


def kernel(**inputs):
    in_maps, qsels = _host_prep(inputs)
    nc = build_program()
    res = run_bass_kernel_spmd(nc, in_maps, core_ids=list(range(8)))
    outp = np.zeros((4, T, C), dtype=np.float32)
    for c in range(8):
        b = c // 2
        o = np.asarray(res.results[c]["out"], dtype=np.float32)
        outp[b][:, qsels[c]] = o
    return outp
